# version 01
# baseline (speedup 1.0000x reference)
import contextlib
import numpy as np
import ml_dtypes
import concourse.bass as bass
import concourse.mybir as mybir
from concourse.bass_utils import run_bass_kernel_spmd

F32 = mybir.dt.float32
BF16 = mybir.dt.bfloat16
I32 = mybir.dt.int32
AF = mybir.ActivationFunctionType
ALU = mybir.AluOpType
AX = mybir.AxisListType

B, L, D = 4, 4096, 2048
NH0, DH0, DR = 16, 128, 32
HID0 = 5632
NH1, HQK, HV = 32, 32, 64
HID1 = 8192
SOFT_C = 12.0
INVSQ = float(1.0 / np.sqrt(128.0))
BF = ml_dtypes.bfloat16
SHUF_ROT16 = [(j + 16) % 32 for j in range(32)]


class Sem:

    def __init__(self, h, step):
        self.h = h
        self.n = 0
        self.step = step

    def tag(self, inst):
        inst.then_inc(self.h, self.step)
        self.n += self.step
        return inst

    def guard_tag(self, eng, fn):
        if self.n:
            eng.wait_ge(self.h, self.n)
        inst = fn()
        self.tag(inst)
        return inst

    def use(self, eng):
        eng.wait_ge(self.h, self.n)

    def expect(self, k):
        assert self.n == k * self.step, (self.n, k, self.step)

    def at(self, k):
        return k * self.step


class DRing:

    def __init__(self, ctx, name, k):
        self.sems = [ctx.sem(f"{name}{i}", 16) for i in range(k)]
        self.k = k
        self.issued = 0

    def issue(self, eng, fn):
        slot = self.issued % self.k
        prior = (self.issued // self.k) * 16
        if prior:
            eng.wait_ge(self.sems[slot].h, prior)
        inst = fn()
        self.sems[slot].tag(inst)
        self.issued += 1
        return inst

    def wait(self, eng, idx):
        if idx < 0:
            return
        assert idx < self.issued + self.k * 100
        s = self.sems[idx % self.k]
        eng.wait_ge(s.h, 16 * (idx // self.k + 1))


class Ctx:
    def __init__(self, nc, es):
        self.nc = nc
        self.es = es
        self._sems = {}

    def sem(self, name, step=1):
        if name not in self._sems:
            self._sems[name] = Sem(self.es.enter_context(self.nc.semaphore(name)), step)
        s = self._sems[name]
        assert s.step == step
        return s

    def init_consts(self, values):
        self.cc = {}
        self.ccr = {}
        tile = self.sb("constcols", [128, max(1, len(values))], F32)
        rtile = self.sb("constrows", [1, max(1, len(values))], F32)
        for i, v in enumerate(values):
            self.cc[v] = tile[:, i:i + 1]
            self.ccr[v] = rtile[:, i:i + 1]

        @self.block.vector
        def _(ve):
            b = self.sem("bar")
            for i, v in enumerate(values):
                b.tag(ve.memset(tile[:, i:i + 1], float(v)))
                b.tag(ve.memset(rtile[:, i:i + 1], float(v)))
        b = self.sem("bar")
        for engname in ("tensor", "scalar", "gpsimd", "sync"):
            sec = getattr(self.block, engname)

            def body(eng):
                eng.wait_ge(b.h, b.n)

            sec(body)

    def sb(self, name, shape, dt):
        return self.es.enter_context(self.nc.sbuf_tensor(name, shape, dt))

    def ps(self, name, shape, dt=F32):
        return self.es.enter_context(self.nc.psum_tensor(name, shape, dt))

    def barrier(self, tag=""):
        bs = self.sem("bar", 1)
        waits = [(s.h, s.n) for nm, s in self._sems.items() if nm != "bar" and s.n > 0]
        base = bs.n
        for engname in ("tensor", "vector", "scalar", "gpsimd", "sync"):
            sec = getattr(self.block, engname)

            def body(eng, _eng=engname):
                for h, n in waits:
                    eng.wait_ge(h, n)
                if _eng != "sync":
                    bs.tag(eng.nop())
                eng.wait_ge(bs.h, base + 4)

            sec(body)
        assert bs.n == base + 4


def ph_a(ctx, x0, xn0_tm):
    nc = ctx.nc
    ntile = L // 128
    aL = DRing(ctx, "aL", 2)
    aN = ctx.sem("aN")
    aR = ctx.sem("aR")
    aP = ctx.sem("aP")
    xS = ctx.sem("xS")
    xV = ctx.sem("xV")
    aS = DRing(ctx, "aS", 2)
    with contextlib.ExitStack() as es:
        xt = [es.enter_context(nc.sbuf_tensor(f"pha_x{i}", [128, D], F32)) for i in range(2)]
        yt = [es.enter_context(nc.sbuf_tensor(f"pha_y{i}", [128, D], BF16)) for i in range(2)]
        st = es.enter_context(nc.sbuf_tensor("pha_st", [128, 2, 10], F32))
        dmp = es.enter_context(nc.sbuf_tensor("pha_dmp", [128, D], BF16))

        @ctx.block.sync
        def _(q):
            for i in range(ntile):
                if i >= 2:
                    q.wait_ge(aP.h, aP.at(i - 1))
                aL.issue(q, lambda: q.dma_start(out=xt[i % 2][:], in_=x0[128 * i:128 * (i + 1), :]))
        assert aL.issued == ntile

        aQ = ctx.sem("aQ")

        @ctx.block.scalar
        def _(s):
            for i in range(ntile):
                p = i % 2
                aL.wait(s, i)
                if i >= 2:
                    aS.wait(s, i - 2)
                xS.tag(s.activation(out=dmp[:], in_=xt[p][:], func=AF.Square,
                                    accum_out=st[:, p, 0:1]))
                xS.use(s)
                aQ.tag(s.activation(out=yt[p][:], in_=xt[p][:], func=AF.Identity,
                                    accum_out=st[:, p, 1:2]))
                s.wait_ge(aN.h, aN.at(i + 1))
                xS.tag(s.activation(out=st[:, p, 8:9], in_=st[:, p, 4:5], func=AF.Ln,
                                    bias=ctx.cc[1e-5]))
                xS.use(s)
                aR.tag(s.activation(out=st[:, p, 9:10], in_=st[:, p, 8:9],
                                    func=AF.Exp, scale=-0.5))

        @ctx.block.vector
        def _(v):
            for i in range(ntile):
                p = i % 2
                v.wait_ge(aQ.h, aQ.at(i + 1))
                xV.tag(v.tensor_scalar(out=st[:, p, 6:7], in0=st[:, p, 1:2],
                                       scalar1=1.0 / D, scalar2=None, op0=ALU.mult))
                xV.use(v)
                xV.tag(v.tensor_tensor(out=st[:, p, 5:6], in0=st[:, p, 6:7],
                                       in1=st[:, p, 6:7], op=ALU.mult))
                xV.use(v)
                aN.tag(v.tensor_scalar(out=st[:, p, 4:5], in0=st[:, p, 0:1],
                                       scalar1=1.0 / D, scalar2=st[:, p, 5:6],
                                       op0=ALU.mult, op1=ALU.subtract))
                v.wait_ge(aR.h, aR.at(i + 1))
                if i >= 2:
                    aS.wait(v, i - 2)
                aP.tag(v.tensor_scalar(out=yt[p][:], in0=xt[p][:],
                                       scalar1=st[:, p, 6:7], scalar2=st[:, p, 9:10],
                                       op0=ALU.subtract, op1=ALU.mult))

        @ctx.block.gpsimd
        def _(g):
            for i in range(ntile):
                g.wait_ge(aP.h, aP.at(i + 1))
                aS.issue(g, lambda: g.dma_start(out=xn0_tm[128 * i:128 * (i + 1), :], in_=yt[i % 2][:]))


def ph_b(ctx, xn0_tm, wqkvT, bqkv, cs_tab, qf, kf, vf):
    import os
    NODMAT = os.environ.get("NODMAT") == "1"
    NOW = os.environ.get("NOW") == "1"
    NOROPE = os.environ.get("NOROPE") == "1"
    nc = ctx.nc
    NC = 48
    NTG = int(os.environ.get("NTG", "4"))
    steps = [(tg, cc) for tg in range(NTG) for cc in range(NC)]
    NS = len(steps)
    bX = ctx.sem("bX", 16)
    bW = DRing(ctx, "bW", 2)
    bM = ctx.sem("bM")
    bE = ctx.sem("bE")
    bR = ctx.sem("bR")
    bS = DRing(ctx, "bS", 4)
    bC = ctx.sem("bC", 16)
    xV = ctx.sem("xV")

    with contextlib.ExitStack() as es:
        xf = [es.enter_context(nc.sbuf_tensor(f"phb_x{i}", [128, 16, 512], BF16)) for i in range(2)]
        wst = [es.enter_context(nc.sbuf_tensor(f"phb_w{i}", [128, 16, 128], BF16)) for i in range(2)]
        bia = es.enter_context(nc.sbuf_tensor("phb_b", [128, 48], F32))
        cstC = es.enter_context(nc.sbuf_tensor("phb_csC", [32, L], BF16))
        cstS = es.enter_context(nc.sbuf_tensor("phb_csS", [32, L], BF16))
        stg = [es.enter_context(nc.sbuf_tensor(f"phb_s{i}", [128, 512], BF16)) for i in range(4)]
        rt1 = es.enter_context(nc.sbuf_tensor("phb_r1", [32, 512], BF16))
        rt2 = es.enter_context(nc.sbuf_tensor("phb_r2", [32, 512], BF16))
        pp = [es.enter_context(nc.psum_tensor(f"phb_p{i}", [128, 512], F32)) for i in range(2)]

        @ctx.block.sync
        def _(q):
            bC.guard_tag(q, lambda: q.dma_start(out=bia[:], in_=bqkv))
            bC.guard_tag(q, lambda: q.dma_start(out=cstC[:], in_=cs_tab[0:32, :]))
            bC.guard_tag(q, lambda: q.dma_start(out=cstS[:], in_=cs_tab[32:64, :]))
            for tg in range(NTG):
                for half in range(2):
                    ts = 2 * tg + half
                    if tg >= 1:
                        q.wait_ge(bM.h, bM.at(2 * NC * tg))
                    if NODMAT:
                        for j in range(3):
                            bX.guard_tag(q, lambda: q.dma_start(out=xf[half][:, 4 * j:4 * j + 4, :],
                                               in_=xn0_tm[128 * j:128 * (j + 1), :]))
                            bX.n -= 16
                        bX.guard_tag(q, lambda: q.dma_start(out=xf[half][:, 12:16, :],
                                           in_=xn0_tm[384:512, :]))
                    else:
                        bX.guard_tag(q, lambda: q.dma_start_transpose(
                            out=xf[half][:], in_=xn0_tm[512 * ts:512 * (ts + 1), :]))
        bX.expect(2 * NTG)

        @ctx.block.gpsimd
        def _(g):
            def cast(si):
                tg, cc = steps[si]
                if si >= 2:
                    g.wait_ge(bM.h, bM.at(2 * si - 2))
                if NOW:
                    bW.issue(g, lambda: g.dma_start(out=wst[si % 2][:, 0, :], in_=wqkvT[0:128, 0:128]))
                else:
                    bW.issue(g, lambda: g.dma_start(
                        out=wst[si % 2][:],
                        in_=wqkvT[:, 128 * cc:128 * (cc + 1)].rearrange("(c p) f -> p c f", p=128)))

            cast(0)
            cast(1)
            for si in range(NS):
                tg, cc = steps[si]
                if si + 2 < NS:
                    cast(si + 2)
                for half in range(2):
                    ts = 2 * tg + half
                    g.wait_ge(bR.h, bR.at(2 * si + half + 1))
                    sg = stg[(2 * si + half) % 4]
                    kind, head = cc % 3, cc // 3
                    dst = (qf, kf, vf)[kind][head, :, 512 * ts:512 * (ts + 1)]
                    bS.issue(g, lambda: g.dma_start(out=dst, in_=sg[:]))
        assert bW.issued == NS
        assert bS.issued == 2 * NS

        @ctx.block.tensor
        def _(t):
            t.wait_ge(bC.h, bC.at(3))
            for si, (tg, cc) in enumerate(steps):
                bW.wait(t, si)
                if cc == 0:
                    t.wait_ge(bX.h, bX.at(2 * (tg + 1)))
                for half in range(2):
                    step2 = 2 * si + half
                    if step2 >= 2:
                        t.wait_ge(bE.h, bE.at(step2 - 1))
                    w = wst[si % 2]
                    for dc in range(15):
                        t.matmul(out=pp[half][:], lhsT=w[:, dc, :], rhs=xf[half][:, dc, :],
                                 start=(dc == 0), stop=False)
                    bM.tag(t.matmul(out=pp[half][:], lhsT=w[:, 15, :],
                                    rhs=xf[half][:, 15, :], start=False, stop=True))
        bM.expect(2 * NS)

        @ctx.block.scalar
        def _(s):
            for si, (tg, cc) in enumerate(steps):
                for half in range(2):
                    step2 = 2 * si + half
                    s.wait_ge(bM.h, bM.at(step2 + 1))
                    if step2 >= 4:
                        bS.wait(s, step2 - 4)
                    sg = stg[step2 % 4]
                    bE.tag(s.activation(out=sg[:], in_=pp[half][:], func=AF.Identity,
                                        bias=bia[:, cc:cc + 1]))
        bE.expect(2 * NS)

        @ctx.block.vector
        def _(v):
            for si, (tg, cc) in enumerate(steps):
                for half in range(2):
                    ts = 2 * tg + half
                    step2 = 2 * si + half
                    v.wait_ge(bE.h, bE.at(step2 + 1))
                    if cc % 3 != 2 and not NOROPE:
                        sg = stg[step2 % 4]
                        tok = slice(512 * ts, 512 * (ts + 1))
                        bR.use(v)
                        xV.tag(v.stream_shuffle(out=rt1[:], in_=sg[0:32, :], mask=SHUF_ROT16))
                        xV.use(v)
                        xV.tag(v.tensor_tensor(out=rt2[:], in0=rt1[:], in1=cstS[:, tok], op=ALU.mult))
                        xV.use(v)
                        xV.tag(v.tensor_tensor(out=rt1[:], in0=sg[0:32, :], in1=cstC[:, tok], op=ALU.mult))
                        xV.use(v)
                        bR.tag(v.tensor_tensor(out=sg[0:32, :], in0=rt1[:], in1=rt2[:], op=ALU.add))
                    else:
                        bR.use(v)
                        bR.tag(v.nop())
        bR.expect(2 * NS)


def ph_c(ctx, qf, kf, vf, maskq, of):
    nc = ctx.nc
    NQT = 8
    cKQ = DRing(ctx, "cKQ", 4)
    cVT = DRing(ctx, "cVT", 2)
    cO = DRing(ctx, "cO", 4)
    cS = ctx.sem("cS")
    cAv = ctx.sem("cAv")
    cL = ctx.sem("cL")
    cE = ctx.sem("cE")
    cK = ctx.sem("cK")
    cB = ctx.sem("cB")
    cC = ctx.sem("cC")
    cR = ctx.sem("cR")
    cV = ctx.sem("cV")
    cI = ctx.sem("cI")

    kvidx = {}
    n = 0
    for h in range(16):
        for qt in range(NQT):
            for kv in range(4 * (qt + 1)):
                kvidx[(h, qt, kv)] = n
                n += 1
    NKV = n
    gq = lambda h, qt: h * NQT + qt
    dgi = {}
    n = 0
    for h in range(16):
        for qt in range(NQT):
            for i in range(4):
                dgi[(h, qt, i)] = n
                n += 1

    with contextlib.ExitStack() as es:
        kb = [es.enter_context(nc.sbuf_tensor(f"phc_k{i}", [128, L], BF16)) for i in range(2)]
        qb = [es.enter_context(nc.sbuf_tensor(f"phc_q{i}", [128, L], BF16)) for i in range(2)]
        vb = [es.enter_context(nc.sbuf_tensor(f"phc_v{i}", [128, 32, 128], BF16)) for i in range(2)]
        esb = [es.enter_context(nc.sbuf_tensor(f"phc_e{i}", [128, 512], BF16)) for i in range(2)]
        msk = es.enter_context(nc.sbuf_tensor("phc_m", [128, 4, 512], BF16))
        onec = es.enter_context(nc.sbuf_tensor("phc_1c", [128, 1], BF16))
        oner = es.enter_context(nc.sbuf_tensor("phc_1r", [1, 128], BF16))
        lsb = [es.enter_context(nc.sbuf_tensor(f"phc_l{i}", [1, 512], BF16)) for i in range(2)]
        rbc = [es.enter_context(nc.sbuf_tensor(f"phc_r{i}", [128, 512], F32)) for i in range(2)]
        osb = [es.enter_context(nc.sbuf_tensor(f"phc_o{i}", [128, 512], BF16)) for i in range(2)]
        psS = [es.enter_context(nc.psum_tensor(f"phc_pS{i}", [128, 512], F32)) for i in range(2)]
        psO = [es.enter_context(nc.psum_tensor(f"phc_pO{i}", [128, 512], F32)) for i in range(2)]
        psL = [es.enter_context(nc.psum_tensor(f"phc_pL{i}", [1, 512], F32)) for i in range(2)]
        psB = [es.enter_context(nc.psum_tensor(f"phc_pB{i}", [128, 512], F32)) for i in range(2)]

        @ctx.block.vector
        def _(v):
            cI.tag(v.memset(onec[:], 1.0))
            cI.tag(v.memset(oner[:], 1.0))

        @ctx.block.sync
        def _(q):
            cKQ.issue(q, lambda: q.dma_start(out=msk[:, :, :],
                                in_=maskq.rearrange("i p f -> p i f")))
            for h in range(16):
                if h >= 2:
                    q.wait_ge(cV.h, cV.at(gq(h - 2, NQT - 1) + 1))
                cKQ.issue(q, lambda: q.dma_start(out=kb[h % 2][:], in_=kf[h]))
                cKQ.issue(q, lambda: q.dma_start(out=qb[h % 2][:], in_=qf[h]))
                cVT.issue(q, lambda: q.dma_start_transpose(out=vb[h % 2][:], in_=vf[h]))

        @ctx.block.tensor
        def _(t):
            t.wait_ge(cI.h, cI.at(2))
            for h in range(16):
                cKQ.wait(t, 1 + 2 * h)
                cKQ.wait(t, 2 + 2 * h)
                cVT.wait(t, h)
                for qt in range(NQT):
                    nkv = 4 * (qt + 1)
                    g = gq(h, qt)

                    def emit_av(kv):
                        gi = kvidx[(h, qt, kv)]
                        t.wait_ge(cE.h, cE.at(gi + 1))
                        if kv >= 4 * qt:
                            t.wait_ge(cK.h, cK.at(dgi[(h, qt, kv - 4 * qt)] + 1))
                        if kv == 0 and g >= 2:
                            t.wait_ge(cV.h, cV.at(g - 2 + 1))
                            t.wait_ge(cB.h, cB.at(g - 2 + 1))
                        cAv.tag(t.matmul(out=psO[g % 2][:], lhsT=vb[h % 2][:, kv, :],
                                         rhs=esb[gi % 2][:], start=(kv == 0),
                                         stop=(kv == nkv - 1)))
                        cL.tag(t.matmul(out=psL[g % 2][:], lhsT=onec[:],
                                        rhs=esb[gi % 2][:], start=(kv == 0),
                                        stop=(kv == nkv - 1)))

                    for kv in range(nkv):
                        gi = kvidx[(h, qt, kv)]
                        if gi >= 2:
                            t.wait_ge(cE.h, cE.at(gi - 1))
                        cS.tag(t.matmul(out=psS[gi % 2][:], lhsT=kb[h % 2][:, 128 * kv:128 * (kv + 1)],
                                        rhs=qb[h % 2][:, 512 * qt:512 * (qt + 1)],
                                        start=True, stop=True))
                        if kv >= 1:
                            emit_av(kv - 1)
                    emit_av(nkv - 1)
                    t.wait_ge(cB.h, cB.at(g + 1))
                    if g >= 2:
                        t.wait_ge(cR.h, cR.at(g - 2 + 1))
                    cC.tag(t.matmul(out=psB[g % 2][:], lhsT=oner[:], rhs=lsb[g % 2][:],
                                    start=True, stop=True))

        @ctx.block.scalar
        def _(s):
            for h in range(16):
                for qt in range(NQT):
                    nkv = 4 * (qt + 1)
                    g = gq(h, qt)
                    for kv in range(nkv):
                        gi = kvidx[(h, qt, kv)]
                        s.wait_ge(cS.h, cS.at(gi + 1))
                        if gi >= 2:
                            s.wait_ge(cL.h, cL.at(gi - 1))
                        cE.tag(s.activation(out=esb[gi % 2][:], in_=psS[gi % 2][:],
                                            func=AF.Exp, scale=INVSQ, bias=ctx.cc[-SOFT_C]))
                    s.wait_ge(cL.h, cL.at(kvidx[(h, qt, nkv - 1)] + 1))
                    cB.tag(s.copy(out=lsb[g % 2][:], in_=psL[g % 2][:]))

        @ctx.block.vector
        def _(v):
            cKQ.wait(v, 0)
            for h in range(16):
                for qt in range(NQT):
                    nkv = 4 * (qt + 1)
                    g = gq(h, qt)
                    for i in range(4):
                        kv = 4 * qt + i
                        gi = kvidx[(h, qt, kv)]
                        v.wait_ge(cE.h, cE.at(gi + 1))
                        cK.tag(v.tensor_tensor(out=esb[gi % 2][:], in0=esb[gi % 2][:],
                                               in1=msk[:, i, :], op=ALU.mult))
                    v.wait_ge(cC.h, cC.at(g + 1))
                    cR.tag(v.reciprocal(out=rbc[g % 2][:], in_=psB[g % 2][:]))
                    v.wait_ge(cR.h, cR.at(g + 1))
                    if g >= 2:
                        cO.wait(v, g - 2)
                    cV.tag(v.tensor_tensor(out=osb[g % 2][:], in0=psO[g % 2][:],
                                           in1=rbc[g % 2][:], op=ALU.mult))

        @ctx.block.gpsimd
        def _(g_):
            for h in range(16):
                for qt in range(NQT):
                    g = gq(h, qt)
                    g_.wait_ge(cV.h, cV.at(g + 1))
                    cO.issue(g_, lambda: g_.dma_start(out=of[h, :, 512 * qt:512 * (qt + 1)],
                                        in_=osb[g % 2][:]))


def ph_d(ctx, of, woutT, bout, s0f):
    nc = ctx.nc
    NQT = 8
    dW = DRing(ctx, "dW", 2)
    dO = DRing(ctx, "dO", 4)
    dM = ctx.sem("dM")
    dE = ctx.sem("dE")
    dS = DRing(ctx, "dS", 4)
    dB = ctx.sem("dB", 16)

    with contextlib.ExitStack() as es:
        ob = [es.enter_context(nc.sbuf_tensor(f"phd_o{i}", [128, 16, 512], BF16)) for i in range(2)]
        wst = [es.enter_context(nc.sbuf_tensor(f"phd_w{i}", [128, 16, 128], BF16)) for i in range(2)]
        bia = es.enter_context(nc.sbuf_tensor("phd_b", [128, 16], F32))
        stg = [es.enter_context(nc.sbuf_tensor(f"phd_s{i}", [128, 512], BF16)) for i in range(4)]
        pp = [es.enter_context(nc.psum_tensor(f"phd_p{i}", [128, 512], F32)) for i in range(2)]

        steps = [(qt, dc) for qt in range(NQT) for dc in range(16)]

        @ctx.block.sync
        def _(q):
            dB.tag(q.dma_start(out=bia[:], in_=bout))
            for qt in range(NQT):
                if qt >= 2:
                    q.wait_ge(dM.h, dM.at(16 * (qt - 1)))
                for h in range(16):
                    dO.issue(q, lambda: q.dma_start(out=ob[qt % 2][:, h, :],
                                       in_=of[h, :, 512 * qt:512 * (qt + 1)]))

        @ctx.block.gpsimd
        def _(g):
            def cast(si):
                qt, dc = steps[si]
                if si >= 2:
                    g.wait_ge(dM.h, dM.at(si - 1))
                dW.issue(g, lambda: g.dma_start(
                    out=wst[si % 2][:],
                    in_=woutT[:, 128 * dc:128 * (dc + 1)].rearrange("(c p) f -> p c f", p=128)))

            cast(0)
            cast(1)
            for si, (qt, dc) in enumerate(steps):
                if si + 2 < len(steps):
                    cast(si + 2)
                g.wait_ge(dE.h, dE.at(si + 1))
                dS.issue(g, lambda: g.dma_start(out=s0f[128 * dc:128 * (dc + 1), 512 * qt:512 * (qt + 1)],
                                   in_=stg[si % 4][:]))

        @ctx.block.tensor
        def _(t):
            for si, (qt, dc) in enumerate(steps):
                dW.wait(t, si)
                if dc == 0:
                    for h in range(16):
                        dO.wait(t, 16 * qt + h)
                if si >= 2:
                    t.wait_ge(dE.h, dE.at(si - 1))
                for h in range(15):
                    t.matmul(out=pp[si % 2][:], lhsT=wst[si % 2][:, h, :],
                             rhs=ob[qt % 2][:, h, :], start=(h == 0), stop=False)
                dM.tag(t.matmul(out=pp[si % 2][:], lhsT=wst[si % 2][:, 15, :],
                                rhs=ob[qt % 2][:, 15, :], start=False, stop=True))

        @ctx.block.scalar
        def _(s):
            s.wait_ge(dB.h, 16)
            for si, (qt, dc) in enumerate(steps):
                s.wait_ge(dM.h, dM.at(si + 1))
                if si >= 4:
                    dS.wait(s, si - 4)
                dE.tag(s.activation(out=stg[si % 4][:], in_=pp[si % 2][:],
                                    func=AF.Identity, bias=bia[:, dc:dc + 1]))


def ph_e0(ctx, wgTq, wuTq, wdTq, sgbc, subc, sdbc, wgq, wuq, wdq, bufs_es=None):
    nc = ctx.nc
    eW = DRing(ctx, "eW", 2)
    eS = DRing(ctx, "eS", 2)
    eC = ctx.sem("eC")
    eB = ctx.sem("eB", 16)
    jobs = ([("g", dc) for dc in range(16)] + [("u", dc) for dc in range(16)]
            + [("d", hc) for hc in range(44)])

    with contextlib.ExitStack() as _es:
        es = bufs_es if bufs_es is not None else _es
        wci = [es.enter_context(nc.sbuf_tensor(f"phe0_i{i}", [128, HID0], F32)) for i in range(2)]
        wco = [es.enter_context(nc.sbuf_tensor(f"phe0_o{i}", [128, HID0], BF16)) for i in range(2)]
        sg = es.enter_context(nc.sbuf_tensor("phe0_sg", [128, HID0], F32))
        su = es.enter_context(nc.sbuf_tensor("phe0_su", [128, HID0], F32))
        sd = es.enter_context(nc.sbuf_tensor("phe0_sd", [128, D], F32))

        @ctx.block.sync
        def _(q):
            eB.tag(q.dma_start(out=sg[:], in_=sgbc))
            eB.tag(q.dma_start(out=su[:], in_=subc))
            eB.tag(q.dma_start(out=sd[:], in_=sdbc))

        @ctx.block.gpsimd
        def _(g):
            def load(ji):
                kind, c = jobs[ji]
                if ji >= 2:
                    g.wait_ge(eC.h, eC.at(ji - 1))
                srct = {"g": wgTq, "u": wuTq, "d": wdTq}[kind]
                w = HID0 if kind != "d" else D
                eW.issue(g, lambda: g.dma_start(out=wci[ji % 2][:, 0:w],
                                                in_=srct[128 * c:128 * (c + 1), :]))

            load(0)
            load(1)
            for ji, (kind, c) in enumerate(jobs):
                if ji + 2 < len(jobs):
                    load(ji + 2)
                g.wait_ge(eC.h, eC.at(ji + 1))
                w = HID0 if kind != "d" else D
                dst = {"g": wgq, "u": wuq, "d": wdq}[kind]
                eS.issue(g, lambda: g.dma_start(out=dst[128 * c:128 * (c + 1), :],
                                                in_=wco[ji % 2][:, 0:w]))

        @ctx.block.vector
        def _(v):
            v.wait_ge(eB.h, eB.at(3))
            for ji, (kind, c) in enumerate(jobs):
                eW.wait(v, ji)
                if ji >= 2:
                    eS.wait(v, ji - 2)
                w = HID0 if kind != "d" else D
                sc = {"g": sg, "u": su, "d": sd}[kind]
                eC.tag(v.scalar_tensor_tensor(out=wco[ji % 2][:, 0:w], in0=wci[ji % 2][:, 0:w],
                                              scalar=-128.0, in1=sc[:, 0:w],
                                              op0=ALU.add, op1=ALU.mult))


def ph_e(ctx, xn0_tm, wgq, wuq, wdq, x0T, s0f, x1f, xn1f):
    nc = ctx.nc
    NHC = 44
    fX = ctx.sem("fX", 16)
    fWg = DRing(ctx, "fWg", 2)
    fWu = DRing(ctx, "fWu", 2)
    fD = DRing(ctx, "fD", 2)
    fR = DRing(ctx, "fR", 4)
    fG = ctx.sem("fG")
    fU = ctx.sem("fU")
    fP = ctx.sem("fP")
    fSi = ctx.sem("fSi")
    fM = ctx.sem("fM")
    fE = ctx.sem("fE")
    fQ = ctx.sem("fQ")
    fV = ctx.sem("fV")
    fC = ctx.sem("fC")
    fT = ctx.sem("fT")
    fF = ctx.sem("fF")
    fL = ctx.sem("fL")
    fB = ctx.sem("fB")
    fBe = ctx.sem("fBe")
    fA = ctx.sem("fA")
    fO1 = DRing(ctx, "fO1", 2)
    fO2 = DRing(ctx, "fO2", 2)
    eI = ctx.sem("eI")
    xV = ctx.sem("xVe")
    xS = ctx.sem("xSe")

    with contextlib.ExitStack() as es:
        xf = es.enter_context(nc.sbuf_tensor("phe_x", [128, 16, 512], BF16))
        wg = [es.enter_context(nc.sbuf_tensor(f"phe_wg{i}", [128, 16, 128], BF16)) for i in range(2)]
        wu = [es.enter_context(nc.sbuf_tensor(f"phe_wu{i}", [128, 16, 128], BF16)) for i in range(2)]
        wd = [es.enter_context(nc.sbuf_tensor(f"phe_wd{i}", [128, NHC, 128], BF16)) for i in range(2)]
        prod = es.enter_context(nc.sbuf_tensor("phe_pr", [128, NHC, 512], BF16))
        sgt = es.enter_context(nc.sbuf_tensor("phe_sg", [128, 512], BF16))
        g0t = es.enter_context(nc.sbuf_tensor("phe_g0", [128, 512], BF16))
        xsq = es.enter_context(nc.sbuf_tensor("phe_xq", [128, 512], BF16))
        x0c = [es.enter_context(nc.sbuf_tensor(f"phe_x0{i}", [128, 512], F32)) for i in range(2)]
        s0c = [es.enter_context(nc.sbuf_tensor(f"phe_s0{i}", [128, 512], BF16)) for i in range(2)]
        x1t = [es.enter_context(nc.sbuf_tensor(f"phe_x1{i}", [128, 512], F32)) for i in range(2)]
        x1b = es.enter_context(nc.sbuf_tensor("phe_x1b", [128, 16, 512], BF16))
        xnt = [es.enter_context(nc.sbuf_tensor(f"phe_xn{i}", [128, 512], BF16)) for i in range(2)]
        rows = es.enter_context(nc.sbuf_tensor("phe_rw", [1, 8, 512], F32))
        rowb = es.enter_context(nc.sbuf_tensor("phe_rb", [1, 2, 512], BF16))
        oner = es.enter_context(nc.sbuf_tensor("phe_1r", [1, 128], BF16))
        onec = es.enter_context(nc.sbuf_tensor("phe_1c", [128, 1], BF16))
        mbc = es.enter_context(nc.sbuf_tensor("phe_mb", [128, 512], F32))
        rbc = es.enter_context(nc.sbuf_tensor("phe_rb2", [128, 512], F32))
        psG = [es.enter_context(nc.psum_tensor(f"phe_pG{i}", [128, 512], F32)) for i in range(2)]
        psU = [es.enter_context(nc.psum_tensor(f"phe_pU{i}", [128, 512], F32)) for i in range(2)]
        psD = psG
        psM = es.enter_context(nc.psum_tensor("phe_pM", [1, 512], F32))
        psQ = es.enter_context(nc.psum_tensor("phe_pQ", [1, 512], F32))
        psB = es.enter_context(nc.psum_tensor("phe_pB", [128, 512], F32))

        @ctx.block.vector
        def _(v):
            eI.tag(v.memset(oner[:], 1.0))
            eI.tag(v.memset(onec[:], 1.0))

        NT8 = 8
        for tt in range(NT8):

            @ctx.block.sync
            def _(q, tt=tt):
                if tt >= 1:
                    q.wait_ge(fU.h, fU.at(NHC * tt))
                if fX.n:
                    q.wait_ge(fX.h, fX.n)
                fX.tag(q.dma_start_transpose(out=xf[:], in_=xn0_tm[512 * tt:512 * (tt + 1), :]))
                for hid in range(NHC):
                    j = tt * NHC + hid
                    if j >= 2:
                        q.wait_ge(fU.h, fU.at(j - 1))
                    fWg.issue(q, lambda: q.dma_start(
                        out=wg[j % 2][:],
                        in_=wgq[:, 128 * hid:128 * (hid + 1)].rearrange("(c p) f -> p c f", p=128)))
                    fWu.issue(q, lambda: q.dma_start(
                        out=wu[j % 2][:],
                        in_=wuq[:, 128 * hid:128 * (hid + 1)].rearrange("(c p) f -> p c f", p=128)))

            @ctx.block.tensor
            def _(t, tt=tt):
                t.wait_ge(eI.h, 2)
                t.wait_ge(fX.h, fX.at(tt + 1))
                for hid in range(NHC):
                    j = tt * NHC + hid
                    fWg.wait(t, j)
                    fWu.wait(t, j)
                    if j >= 2:
                        t.wait_ge(fSi.h, fSi.at(j - 1))
                        t.wait_ge(fP.h, fP.at(j - 1))
                    if hid == 0 and tt >= 1:
                        t.wait_ge(fM.h, fM.at(16 * tt))
                    for dc in range(15):
                        t.matmul(out=psG[j % 2][:], lhsT=wg[j % 2][:, dc, :], rhs=xf[:, dc, :],
                                 start=(dc == 0), stop=False)
                    fG.tag(t.matmul(out=psG[j % 2][:], lhsT=wg[j % 2][:, 15, :], rhs=xf[:, 15, :],
                                    start=False, stop=True))
                    for dc in range(15):
                        t.matmul(out=psU[j % 2][:], lhsT=wu[j % 2][:, dc, :], rhs=xf[:, dc, :],
                                 start=(dc == 0), stop=False)
                    fU.tag(t.matmul(out=psU[j % 2][:], lhsT=wu[j % 2][:, 15, :], rhs=xf[:, 15, :],
                                    start=False, stop=True))

            @ctx.block.scalar
            def _(s, tt=tt):
                for hid in range(NHC):
                    j = tt * NHC + hid
                    s.wait_ge(fG.h, fG.at(j + 1))
                    if j >= 1:
                        s.wait_ge(fP.h, fP.at(j))
                    fSi.tag(s.activation(out=sgt[:], in_=psG[j % 2][:], func=AF.Silu))

            @ctx.block.vector
            def _(v, tt=tt):
                for hid in range(NHC):
                    j = tt * NHC + hid
                    v.wait_ge(fSi.h, fSi.at(j + 1))
                    v.wait_ge(fU.h, fU.at(j + 1))
                    if hid == 0 and tt >= 1:
                        v.wait_ge(fM.h, fM.at(16 * tt))
                    fP.tag(v.tensor_tensor(out=prod[:, hid, :], in0=sgt[:], in1=psU[j % 2][:],
                                           op=ALU.mult))

            @ctx.block.sync
            def _(q, tt=tt):
                for dcol in range(16):
                    si = tt * 16 + dcol
                    if si >= 2:
                        q.wait_ge(fM.h, fM.at(si - 1))
                        q.wait_ge(fV.h, fV.at(2 * si - 2))
                    fD.issue(q, lambda: q.dma_start(
                        out=wd[si % 2][:],
                        in_=wdq[:, 128 * dcol:128 * (dcol + 1)].rearrange("(c p) f -> p c f", p=128)))
                    fR.issue(q, lambda: q.dma_start(
                        out=x0c[si % 2][:], in_=x0T[128 * dcol:128 * (dcol + 1),
                                                    512 * tt:512 * (tt + 1)]))
                    fR.issue(q, lambda: q.dma_start(
                        out=s0c[si % 2][:], in_=s0f[128 * dcol:128 * (dcol + 1),
                                                    512 * tt:512 * (tt + 1)]))

            @ctx.block.tensor
            def _(t, tt=tt):
                t.wait_ge(fP.h, fP.at(NHC * (tt + 1)))
                for dcol in range(16):
                    si = tt * 16 + dcol
                    fD.wait(t, si)
                    if si >= 2:
                        t.wait_ge(fE.h, fE.at(si - 1))
                    for hid in range(NHC - 1):
                        t.matmul(out=psD[si % 2][:], lhsT=wd[si % 2][:, hid, :],
                                 rhs=prod[:, hid, :], start=(hid == 0), stop=False)
                    fM.tag(t.matmul(out=psD[si % 2][:], lhsT=wd[si % 2][:, NHC - 1, :],
                                    rhs=prod[:, NHC - 1, :], start=False, stop=True))
                    t.wait_ge(fC.h, fC.at(si + 1))
                    if dcol == 0 and tt >= 1:
                        t.wait_ge(fF.h, fF.at(tt))
                    fT.tag(t.matmul(out=psM[:], lhsT=onec[:], rhs=x1b[:, dcol, :],
                                    start=(dcol == 0), stop=(dcol == 15)))
                    fT.tag(t.matmul(out=psQ[:], lhsT=onec[:], rhs=xsq[:],
                                    start=(dcol == 0), stop=(dcol == 15)))

            @ctx.block.scalar
            def _(s, tt=tt):
                for dcol in range(16):
                    si = tt * 16 + dcol
                    s.wait_ge(fM.h, fM.at(si + 1))
                    if si >= 1:
                        s.wait_ge(fV.h, fV.at(2 * si - 1))
                    fE.tag(s.activation(out=g0t[:], in_=psD[si % 2][:], func=AF.Silu))
                    s.wait_ge(fV.h, fV.at(2 * si + 2))
                    if dcol == 0 and tt >= 1:
                        s.wait_ge(fA.h, fA.at(16 * tt))
                    xS.tag(s.copy(out=x1b[:, dcol, :], in_=x1t[si % 2][:]))
                    xS.use(s)
                    if dcol >= 1:
                        s.wait_ge(fT.h, fT.at(2 * si))
                    fC.tag(s.activation(out=xsq[:], in_=x1t[si % 2][:], func=AF.Square))

            @ctx.block.vector
            def _(v, tt=tt):
                for dcol in range(16):
                    si = tt * 16 + dcol
                    v.wait_ge(fE.h, fE.at(si + 1))
                    fR.wait(v, 2 * si)
                    fR.wait(v, 2 * si + 1)
                    if si >= 2:
                        fO1.wait(v, si - 2)
                        v.wait_ge(fC.h, fC.at(si - 1))
                    fV.tag(v.tensor_tensor(out=x1t[si % 2][:], in0=s0c[si % 2][:],
                                           in1=g0t[:], op=ALU.mult))
                    v.wait_ge(fV.h, fV.n)
                    fV.tag(v.tensor_tensor(out=x1t[si % 2][:], in0=x1t[si % 2][:],
                                           in1=x0c[si % 2][:], op=ALU.add))

            @ctx.block.gpsimd
            def _(g, tt=tt):
                for dcol in range(16):
                    si = tt * 16 + dcol
                    g.wait_ge(fV.h, fV.at(2 * si + 2))
                    fO1.issue(g, lambda: g.dma_start(
                        out=x1f[128 * dcol:128 * (dcol + 1), 512 * tt:512 * (tt + 1)],
                        in_=x1t[si % 2][:]))

            @ctx.block.vector
            def _(v, tt=tt):
                v.wait_ge(fT.h, fT.at(32 * (tt + 1)))
                r = rows
                xV.tag(v.tensor_scalar(out=r[:, 0, :], in0=psM[:], scalar1=1.0 / D,
                                       scalar2=None, op0=ALU.mult))
                xV.use(v)
                xV.tag(v.tensor_tensor(out=r[:, 1, :], in0=r[:, 0, :], in1=r[:, 0, :],
                                       op=ALU.mult))
                xV.use(v)
                xV.tag(v.tensor_scalar(out=r[:, 2, :], in0=psQ[:],
                                       scalar1=1.0 / D, scalar2=None, op0=ALU.mult))
                xV.use(v)
                xV.tag(v.tensor_tensor(out=r[:, 2, :], in0=r[:, 2, :], in1=r[:, 1, :],
                                       op=ALU.subtract))
                xV.use(v)
                fF.tag(v.tensor_copy(out=rowb[:, 0, :], in_=r[:, 0, :]))

            @ctx.block.scalar
            def _(s, tt=tt):
                s.wait_ge(fF.h, fF.at(tt + 1))
                xS.tag(s.activation(out=rows[:, 3, :], in_=rows[:, 2, :], func=AF.Ln,
                                    bias=ctx.ccr[1e-5]))
                xS.use(s)
                fL.tag(s.activation(out=rowb[:, 1, :], in_=rows[:, 3, :], func=AF.Exp,
                                    scale=-0.5))

            @ctx.block.tensor
            def _(t, tt=tt):
                t.wait_ge(fL.h, fL.at(tt + 1))
                if tt >= 1:
                    t.wait_ge(fBe.h, fBe.at(2 * tt))
                fB.tag(t.matmul(out=psB[:], lhsT=oner[:], rhs=rowb[:, 0, :],
                                start=True, stop=True))
                t.wait_ge(fBe.h, fBe.at(2 * tt + 1))
                fB.tag(t.matmul(out=psB[:], lhsT=oner[:], rhs=rowb[:, 1, :],
                                start=True, stop=True))

            @ctx.block.scalar
            def _(s, tt=tt):
                s.wait_ge(fB.h, fB.at(2 * tt + 1))
                if tt >= 1:
                    s.wait_ge(fA.h, fA.at(16 * tt))
                fBe.tag(s.copy(out=mbc[:], in_=psB[:]))
                s.wait_ge(fB.h, fB.at(2 * (tt + 1)))
                fBe.tag(s.copy(out=rbc[:], in_=psB[:]))

            @ctx.block.vector
            def _(v, tt=tt):
                v.wait_ge(fBe.h, fBe.at(2 * (tt + 1)))
                for dcol in range(16):
                    si = tt * 16 + dcol
                    if si >= 2:
                        fO2.wait(v, si - 2)
                    xV.tag(v.tensor_tensor(out=xnt[si % 2][:], in0=x1b[:, dcol, :],
                                           in1=mbc[:], op=ALU.subtract))
                    xV.use(v)
                    fA.tag(v.tensor_tensor(out=xnt[si % 2][:], in0=xnt[si % 2][:],
                                           in1=rbc[:], op=ALU.mult))

            @ctx.block.gpsimd
            def _(g, tt=tt):
                for dcol in range(16):
                    si = tt * 16 + dcol
                    g.wait_ge(fA.h, fA.at(si + 1))
                    fO2.issue(g, lambda: g.dma_start(
                        out=xn1f[128 * dcol:128 * (dcol + 1), 512 * tt:512 * (tt + 1)],
                        in_=xnt[si % 2][:]))


def ph_f(ctx, xn1x_pre, xn1x_own, wf, cst, gatedT, bufs_es=None):
    import os as _os
    FST = int(_os.environ.get("FST", "5"))
    FCHK = int(_os.environ.get("FCHK", "9"))
    nc = ctx.nc
    NG = 8
    ev = {}

    with contextlib.ExitStack() as _es:
        es = bufs_es if bufs_es is not None else _es
        sbl = lambda n, s, d: es.enter_context(nc.sbuf_tensor(n, s, d))
        psl = lambda n, s, d=F32: es.enter_context(nc.psum_tensor(n, s, d))
        xft = sbl("phf_x", [128, 16, 513], BF16)
        dl = sbl("phf_dl", [128, 16, 512], BF16)
        xi = sbl("phf_xi", [128, 16, 512], BF16)
        zt = sbl("phf_zt", [128, 512], BF16)
        zt2 = sbl("phf_zt2", [32, 512], BF16)
        wt1 = sbl("phf_wt1", [64, 512], BF16)
        wx1s = sbl("phf_wx1", [128, 16, 160], BF16)
        wx2s = sbl("phf_wx2", [128, 2048], BF16)
        wx2s2 = sbl("phf_wx2b", [32, 2048], BF16)
        ww2s = sbl("phf_ww2", [64, 1024], BF16)
        wpj = [sbl(f"phf_wp{i}", [128, 16, 128], BF16) for i in range(2)]
        ww1b = sbl("phf_w1b", [128, 16, 64], BF16)
        wmv = [sbl(f"phf_wm{i}", [128, 16, 256], BF16) for i in range(2)]
        mucol = sbl("phf_mu", [128, 16], F32)
        bwsb = sbl("phf_bw", [128, 8], F32)
        bonsb = sbl("phf_bo", [128, 8, 4], BF16)
        gnwb = sbl("phf_gw", [128, 256], BF16)
        gnbb = sbl("phf_gb", [128, 256], BF16)
        pmaskb = sbl("phf_pm", [128, 4, 32], BF16)
        idn = sbl("phf_id", [128, 128], BF16)
        one1 = sbl("phf_11", [1, 128], BF16)
        pflag = sbl("phf_fl", [128, 1], F32)
        ewt = sbl("phf_ew", [128, 8, 512], BF16)
        rkm = sbl("phf_rk", [128, 8, 512], BF16)
        kkm = sbl("phf_kk", [128, 8, 512], BF16)
        abuf = sbl("phf_ab", [128, 8, 4, 34], F32)
        ecol = sbl("phf_ec", [128, 8, 16], F32)
        ecr = sbl("phf_ecr", [128, 4], F32)
        ecr2 = sbl("phf_ecr2", [128, 4], F32)
        emc = sbl("phf_emc", [128, 8, 16], F32)
        aarg = sbl("phf_as", [128, 3, 4, 32], F32)
        exf = sbl("phf_exf", [128, 3, 128], BF16)
        ktf = sbl("phf_kt", [128, 128], BF16)
        ktp_z = sbl("phf_tpz", [128, 4, 128], BF16)
        ktf_z = sbl("phf_ktz", [128, 4, 128], BF16)
        mskc = sbl("phf_msk", [128, 4], F32)
        rbk = sbl("phf_rbk", [128, 128], BF16)
        rstash = sbl("phf_rs", [128, 8, 4, 128], BF16)
        pmsb = sbl("phf_psb", [128, 8, 4, 4, 32], BF16)
        pmbd = sbl("phf_pbd", [128, 4, 128], BF16)
        dstash = sbl("phf_ds", [128, 8, 4, 4], F32)
        ubuf = sbl("phf_ub", [128, 8, 16, 64], BF16)
        straj = sbl("phf_st", [128, 17, 64], F32)
        sbf_bd = sbl("phf_sbf", [128, 4, 256], BF16)
        scar = sbl("phf_sc", [128, 8, 64], F32)
        vpm = sbl("phf_vp", [128, 4, 8, 256], BF16)
        gpm = sbl("phf_gp", [128, 4, 8, 256], BF16)
        osb = sbl("phf_os", [128, 256], F32)
        sqf = sbl("phf_sq", [128, 256], F32)
        onb = sbl("phf_on", [128, 256], BF16)
        sgl = sbl("phf_sl", [128, 256], BF16)
        t1g = sbl("phf_g1", [128, 256], BF16)
        gtin = sbl("phf_gi", [128, 256], BF16)
        gst = sbl("phf_gst", [128, 16], F32)
        gtb = [sbl(f"phf_gt{i}", [128, 128], BF16) for i in range(2)]
        pA = [psl(f"phf_pA{i}", [128, 512]) for i in range(2)]
        pT = psl("phf_pT", [128, 512])
        pTb = psl("phf_pTb", [128, 128], BF16)
        pU = psl("phf_pU", [128, 512])
        pI = psl("phf_pI", [128, 512])
        pN = [psl(f"phf_pN{i}", [128, 512]) for i in range(2)]

        gL = DRing(ctx, "gL", 4)
        gW = DRing(ctx, "gW", 4)
        gO = DRing(ctx, "gO", 4)
        gC = ctx.sem("gC", 16)
        gC2 = ctx.sem("gC2", 16)
        gT = ctx.sem("gT")
        gS = ctx.sem("gS")
        gV = ctx.sem("gV")

        @ctx.block.sync
        def _(q):
            for name, dst in (("mucol", mucol), ("bwsb", bwsb),
                              ("idn", idn), ("pflag", pflag)):
                gC.tag(q.dma_start(out=dst[:], in_=cst[name]))
            gC.tag(q.dma_start(out=pmaskb[:],
                               in_=cst["pmask"].rearrange("p (h s) -> p h s", h=4)))

        @ctx.block.gpsimd
        def _(g):
            for name, dst in (("gnwb", gnwb), ("gnbb", gnbb),
                              ("wx2s", wx2s), ("wx2s2", wx2s2), ("ww2s", ww2s)):
                gC2.tag(g.dma_start(out=dst[:], in_=cst[name]))
            gC2.tag(g.dma_start(out=bonsb[:],
                                in_=cst["bonus"].rearrange("p (g j) -> p g j", g=8)))

        @ctx.block.vector
        def _(v):
            gV.tag(v.memset(one1[:], 1.0))
            gV.tag(v.memset(abuf[:], 0.0))
            gV.tag(v.memset(scar[:], 0.0))
            gV.tag(v.memset(ktf_z[:], 0.0))
            gV.tag(v.memset(sbf_bd[:], 0.0))
            gV.tag(v.memset(pmbd[:], 0.0))
            gV.tag(v.memset(mskc[:], 0.0))
            for cc in range(4):
                v.wait_ge(gV.h, gV.n)
                gV.tag(v.memset(mskc[32 * cc:32 * (cc + 1), cc:cc + 1], 1.0))

        def wait_consts(eng):
            eng.wait_ge(gC.h, gC.at(5))
            eng.wait_ge(gC2.h, gC2.at(6))
            eng.wait_ge(gV.h, 11)

        units = [("pre", u) for u in range(4)] + [("own", u) for u in range(4)]

        for ui, (kind, u) in enumerate(units):
            pre = kind == "pre"
            xn1x = xn1x_pre if pre else xn1x_own
            uend_t = ev.get(("uend_t", ui - 1), 0)
            uend_v = ev.get(("uend_v", ui - 1), 0)
            uend_s = ev.get(("uend_s", ui - 1), 0)

            @ctx.block.sync
            def _(q, ui=ui, u=u, xn1x=xn1x, uend_t=uend_t, uend_v=uend_v):
                if ui == 0:
                    wait_consts(q)
                q.wait_ge(gT.h, uend_t)
                q.wait_ge(gV.h, uend_v)
                for dc in range(16):
                    gL.issue(q, lambda: q.dma_start(
                        out=xft[:, dc, :],
                        in_=xn1x[128 * dc:128 * (dc + 1), 512 * u:512 * u + 513]))

            @ctx.block.vector
            def _(v, ui=ui):
                wait_consts(v)
                for dc in range(16):
                    gL.wait(v, 16 * ui + dc)
                    gV.tag(v.tensor_tensor(out=dl[:, dc, :], in0=xft[:, dc, 0:512],
                                           in1=xft[:, dc, 1:513], op=ALU.subtract))
                    v.wait_ge(gV.h, gV.n)
                    gV.tag(v.scalar_tensor_tensor(out=xi[:, dc, :], in0=dl[:, dc, :],
                                                  scalar=mucol[:, dc:dc + 1],
                                                  in1=xft[:, dc, 1:513],
                                                  op0=ALU.mult, op1=ALU.add))
            ev[("zin_v", ui)] = gV.n

            @ctx.block.gpsimd
            def _(g, ui=ui, uend_t=uend_t):
                g.wait_ge(gT.h, uend_t)
                gW.issue(g, lambda: g.dma_start(
                    out=wx1s[:], in_=wf["wx1"].rearrange("(c p) f -> p c f", p=128)))
            wx1i = gW.issued - 1

            @ctx.block.tensor
            def _(t, ui=ui, wx1i=wx1i):
                if ui == 0:
                    wait_consts(t)
                gW.wait(t, wx1i)
                t.wait_ge(gV.h, ev[("zin_v", ui)])
                t.wait_ge(gS.h, ev.get(("tanh_s", ui - 1), 0))
                for dc in range(15):
                    t.matmul(out=pA[0][:, 0:512], lhsT=wx1s[:, dc, 0:128], rhs=xi[:, dc, :],
                             start=(dc == 0), stop=False)
                gT.tag(t.matmul(out=pA[0][:, 0:512], lhsT=wx1s[:, 15, 0:128], rhs=xi[:, 15, :],
                                start=False, stop=True))
                for dc in range(15):
                    t.matmul(out=pA[1][0:32, 0:512], lhsT=wx1s[:, dc, 128:160], rhs=xi[:, dc, :],
                             start=(dc == 0), stop=False)
                gT.tag(t.matmul(out=pA[1][0:32, 0:512], lhsT=wx1s[:, 15, 128:160], rhs=xi[:, 15, :],
                                start=False, stop=True))
            ev[("z_t", ui)] = gT.n

            @ctx.block.scalar
            def _(s, ui=ui):
                s.wait_ge(gT.h, ev[("z_t", ui)])
                gS.tag(s.activation(out=zt[:], in_=pA[0][:, 0:512], func=AF.Tanh))
                gS.tag(s.activation(out=zt2[:], in_=pA[1][0:32, 0:512], func=AF.Tanh))
            ev[("tanh_s", ui)] = gS.n

            specs = [(i, lab) for i, lab in ((0, "r"), (1, "w"), (2, "k"), (3, "v"), (4, "g"))
                     if not (pre and lab in ("r", "g"))]
            prev_proj_t = ev[("z_t", ui)]
            for sidx, (i, lab) in enumerate(specs):
                vbase = ctx.sem("gV").n
                prev_spec_s = ctx.sem("gS").n

                @ctx.block.tensor
                def _(t, i=i, ui=ui, prev_proj_t=prev_proj_t, vbase=vbase,
                      prev_spec_s=prev_spec_s):
                    t.wait_ge(gS.h, ev[("tanh_s", ui)])
                    t.wait_ge(gS.h, prev_spec_s)
                    t.wait_ge(gT.h, prev_proj_t)
                    for dc in range(16):
                        if dc >= 2:
                            t.wait_ge(gV.h, vbase + 2 * (dc - 2) + 2)
                        lhsT = (wx2s[32 * i:32 * (i + 1), 128 * dc:128 * (dc + 1)] if i < 4
                                else wx2s2[:, 128 * dc:128 * (dc + 1)])
                        rhs = zt[32 * i:32 * (i + 1), :] if i < 4 else zt2[:, :]
                        ev[("mus_t", ui, i, dc)] = gT.n + 1
                        gT.tag(t.matmul(out=pA[dc % 2][:, 0:512], lhsT=lhsT, rhs=rhs,
                                        start=True, stop=True,
                                        tile_position=(32 * i, 0) if i < 4 else None))

                @ctx.block.vector
                def _(v, i=i, ui=ui, sidx=sidx):
                    for dc in range(16):
                        v.wait_ge(gT.h, ev[("mus_t", ui, i, dc)])
                        v.wait_ge(gT.h, ev.get(("proj_t", ui, sidx - 1), 0))
                        gV.tag(v.tensor_tensor(out=xi[:, dc, :], in0=pA[dc % 2][:, 0:512],
                                               in1=dl[:, dc, :], op=ALU.mult))
                        v.wait_ge(gV.h, gV.n)
                        gV.tag(v.tensor_tensor(out=xi[:, dc, :], in0=xi[:, dc, :],
                                               in1=xft[:, dc, 1:513], op=ALU.add))
                ev[("xiall_v", ui, i)] = gV.n

                if lab in ("r", "k"):
                    dst = rkm if lab == "r" else kkm
                    for g8 in range(NG):
                        @ctx.block.gpsimd
                        def _(g, lab=lab, g8=g8, ui=ui):
                            g.wait_ge(gT.h, ev.get(("wuse_t", gW.issued - 2), 0))
                            wsl = wf["wr" if lab == "r" else "wk"]
                            gW.issue(g, lambda: g.dma_start(
                                out=wpj[g8 % 2][:],
                                in_=wsl[:, 128 * g8:128 * (g8 + 1)].rearrange(
                                    "(c p) f -> p c f", p=128)))
                        wi = gW.issued - 1

                        @ctx.block.tensor
                        def _(t, g8=g8, wi=wi, ui=ui, i=i):
                            gW.wait(t, wi)
                            t.wait_ge(gV.h, ev[("xiall_v", ui, i)])
                            if g8 >= 2:
                                t.wait_ge(gS.h, ev[("pjev_s", ui, i, g8 - 2)])
                            ev[("wuse_t", wi)] = gT.n + 1
                            for dc in range(15):
                                t.matmul(out=pA[g8 % 2][:, 0:512], lhsT=wpj[g8 % 2][:, dc, :],
                                         rhs=xi[:, dc, :], start=(dc == 0), stop=False)
                            gT.tag(t.matmul(out=pA[g8 % 2][:, 0:512], lhsT=wpj[g8 % 2][:, 15, :],
                                            rhs=xi[:, 15, :], start=False, stop=True))
                        ev[("pj_t", ui, i, g8)] = gT.n

                        @ctx.block.scalar
                        def _(s, g8=g8, dst=dst, ui=ui, i=i):
                            s.wait_ge(gT.h, ev[("pj_t", ui, i, g8)])
                            ev[("pjev_s", ui, i, g8)] = gS.n + 1
                            gS.tag(s.copy(out=dst[:, g8, :], in_=pA[g8 % 2][:, 0:512]))

                elif lab == "w":
                    @ctx.block.gpsimd
                    def _(g, ui=ui):
                        g.wait_ge(gT.h, ev.get(("w1_t", ui - 1), 0))
                        gW.issue(g, lambda: g.dma_start(
                            out=ww1b[:],
                            in_=wf["ww1"].rearrange("(c p) f -> p c f", p=128)))
                    wi = gW.issued - 1

                    @ctx.block.tensor
                    def _(t, wi=wi, ui=ui, i=i):
                        gW.wait(t, wi)
                        t.wait_ge(gV.h, ev[("xiall_v", ui, i)])
                        ev[("wuse_t", wi)] = gT.n + 1
                        for dc in range(15):
                            t.matmul(out=pA[0][0:64, 0:512], lhsT=ww1b[:, dc, :],
                                     rhs=xi[:, dc, :], start=(dc == 0), stop=False)
                        gT.tag(t.matmul(out=pA[0][0:64, 0:512], lhsT=ww1b[:, 15, :],
                                        rhs=xi[:, 15, :], start=False, stop=True))
                    ev[("w1_t", ui)] = gT.n

                    @ctx.block.scalar
                    def _(s, ui=ui):
                        s.wait_ge(gT.h, ev[("w1_t", ui)])
                        gS.tag(s.activation(out=wt1[:], in_=pA[0][0:64, 0:512], func=AF.Tanh))
                    ev[("wt1_s", ui)] = gS.n

                    sbase = ctx.sem("gS").n

                    @ctx.block.tensor
                    def _(t, ui=ui, sbase=sbase):
                        t.wait_ge(gS.h, ev[("wt1_s", ui)])
                        for g8 in range(NG):
                            if g8 >= 2:
                                t.wait_ge(gS.h, sbase + (g8 - 2) + 1)
                            ev[("w2_t", ui, g8)] = gT.n + 1
                            gT.tag(t.matmul(out=pA[g8 % 2][:, 0:512],
                                            lhsT=ww2s[:, 128 * g8:128 * (g8 + 1)],
                                            rhs=wt1[:], start=True, stop=True))

                    @ctx.block.scalar
                    def _(s, ui=ui, sbase=sbase):
                        assert gS.n == sbase
                        for g8 in range(NG):
                            s.wait_ge(gT.h, ev[("w2_t", ui, g8)])
                            gS.tag(s.activation(out=ewt[:, g8, :], in_=pA[g8 % 2][:, 0:512],
                                                func=AF.Exp, bias=bwsb[:, g8:g8 + 1]))
                    ev[("ew_s", ui)] = gS.n

                else:
                    dstp = vpm if lab == "v" else gpm
                    for g8 in range(NG):
                        @ctx.block.gpsimd
                        def _(g, lab=lab, g8=g8, ui=ui):
                            g.wait_ge(gT.h, ev.get(("wuse_t", gW.issued - 2), 0))
                            wsl = wf["wv" if lab == "v" else "wg"]
                            gW.issue(g, lambda: g.dma_start(
                                out=wmv[g8 % 2][:],
                                in_=wsl[:, 256 * g8:256 * (g8 + 1)].rearrange(
                                    "(c p) f -> p c f", p=128)))
                        wi = gW.issued - 1

                        sbase = ctx.sem("gS").n

                        @ctx.block.tensor
                        def _(t, g8=g8, wi=wi, ui=ui, i=i, sbase=sbase):
                            gW.wait(t, wi)
                            t.wait_ge(gV.h, ev[("xiall_v", ui, i)])
                            ev[("wuse_t", wi)] = gT.n + 4
                            for tsub in range(4):
                                k = g8 * 4 + tsub
                                if k >= 2:
                                    j = k - 2
                                    if j >= 4 * g8:
                                        t.wait_ge(gS.h, sbase + (j - 4 * g8) + 1)
                                    else:
                                        t.wait_ge(gS.h, ev[("pmev_s", ui, i, j)])
                                for dc in range(15):
                                    t.matmul(out=pA[tsub % 2][:, 0:256],
                                             lhsT=xi[:, dc, 128 * tsub:128 * (tsub + 1)],
                                             rhs=wmv[g8 % 2][:, dc, :],
                                             start=(dc == 0), stop=False)
                                ev[("pm_t", ui, i, k)] = gT.n + 1
                                gT.tag(t.matmul(out=pA[tsub % 2][:, 0:256],
                                                lhsT=xi[:, 15, 128 * tsub:128 * (tsub + 1)],
                                                rhs=wmv[g8 % 2][:, 15, :],
                                                start=False, stop=True))

                        @ctx.block.scalar
                        def _(s, g8=g8, dstp=dstp, ui=ui, i=i):
                            for tsub in range(4):
                                k = g8 * 4 + tsub
                                s.wait_ge(gT.h, ev[("pm_t", ui, i, k)])
                                ev[("pmev_s", ui, i, k)] = gS.n + 1
                                gS.tag(s.copy(out=dstp[:, tsub, g8, :],
                                              in_=pA[tsub % 2][:, 0:256]))

                ev[("proj_t", ui, sidx)] = gT.n
            ev[("projall_s", ui)] = gS.n
            ev[("projall_t", ui)] = gT.n

            for g8 in range(NG):
                for c4 in range(4):
                    @ctx.block.vector
                    def _(v, g8=g8, c4=c4, ui=ui, pre=pre):
                        if g8 == 0 and c4 == 0:
                            v.wait_ge(gS.h, ev[("projall_s", ui)])
                        v.wait_ge(gV.h, gV.n)
                        v.wait_ge(gS.h, ev.get(("exp_s", ui, g8, c4 - 1),
                                               ev.get(("exp_s", ui, g8 - 1, 3), 0)))
                        for cc in range(4):
                            gV.tag(v.tensor_tensor_scan(
                                out=abuf[:, g8, cc, 1:33],
                                data0=ewt[:, g8, 128 * c4 + 32 * cc:128 * c4 + 32 * (cc + 1)],
                                data1=ewt[:, g8, 128 * c4 + 32 * cc:128 * c4 + 32 * (cc + 1)],
                                initial=0.0, op0=ALU.add, op1=ALU.bypass))
                        v.wait_ge(gV.h, gV.n)
                        gV.tag(v.tensor_scalar(out=ecr[:], in0=abuf[:, g8, :, 32],
                                               scalar1=-1.0, scalar2=None, op0=ALU.mult))
                        v.wait_ge(gV.h, gV.n)
                        gV.tag(v.tensor_scalar(out=ecr2[:], in0=abuf[:, g8, :, 16],
                                               scalar1=-1.0, scalar2=None, op0=ALU.mult))
                        for cc in range(4):
                            gV.tag(v.tensor_scalar(out=aarg[:, 2, cc, :],
                                                   in0=abuf[:, g8, cc, 1:33],
                                                   scalar1=1.0, scalar2=abuf[:, g8, cc, 32:33],
                                                   op0=ALU.mult, op1=ALU.subtract))
                            if not pre:
                                gV.tag(v.tensor_scalar(out=aarg[:, 0, cc, :],
                                                       in0=abuf[:, g8, cc, 0:32],
                                                       scalar1=-1.0, scalar2=abuf[:, g8, cc, 16:17],
                                                       op0=ALU.mult, op1=ALU.add))
                                gV.tag(v.tensor_scalar(out=aarg[:, 1, cc, :],
                                                       in0=abuf[:, g8, cc, 1:33],
                                                       scalar1=1.0, scalar2=abuf[:, g8, cc, 16:17],
                                                       op0=ALU.mult, op1=ALU.subtract))
                        ev[("aarg_v", ui, g8, c4)] = gV.n

                    @ctx.block.scalar
                    def _(s, g8=g8, c4=c4, ui=ui, pre=pre):
                        s.wait_ge(gV.h, ev[("aarg_v", ui, g8, c4)])
                        s.wait_ge(gV.h, ev.get(("ktf2_v", ui, g8, c4 - 1),
                                               ev.get(("ktf2_v", ui, g8 - 1, 3),
                                                      ev.get(("ktf_v", ui, g8, c4 - 1),
                                                             ev.get(("ktf_v", ui, g8 - 1, 3), 0)))))
                        gS.tag(s.activation(out=ecol[:, g8, 4 * c4:4 * (c4 + 1)],
                                            in_=ecr[:], func=AF.Exp))
                        gS.tag(s.activation(out=emc[:, g8, 4 * c4:4 * (c4 + 1)],
                                            in_=ecr2[:], func=AF.Exp))
                        if pre:
                            ev[("exp_s", ui, g8, c4)] = gS.n + 1
                            gS.tag(s.activation(out=exf[:, 2, :], in_=aarg[:, 2, :, :],
                                                func=AF.Exp))
                        else:
                            gS.tag(s.activation(out=exf[:, 0, :], in_=aarg[:, 0, :, :],
                                                func=AF.Exp))
                            gS.tag(s.activation(out=exf[:, 1, :], in_=aarg[:, 1, :, :],
                                                func=AF.Exp))
                            ev[("exp_s", ui, g8, c4)] = gS.n + 1
                            gS.tag(s.activation(out=exf[:, 2, :], in_=aarg[:, 2, :, :],
                                                func=AF.Exp))
                    if FST < 3:
                        ev[("uev_v", ui, g8, c4)] = ctx.sem("gV").n
                        ev[("exp_s", ui, g8, c4)] = ev.get(("exp_s", ui, g8, c4), ctx.sem("gS").n)
                        continue

                    @ctx.block.vector
                    def _(v, g8=g8, c4=c4, ui=ui, pre=pre):
                        v.wait_ge(gS.h, ev[("exp_s", ui, g8, c4)])
                        v.wait_ge(gT.h, ev.get(("kT_t", ui, g8, c4 - 1),
                                               ev.get(("kT_t", ui, g8 - 1, 3), 0)))
                        gV.tag(v.tensor_tensor(out=ktf[:], in0=kkm[:, g8, 128 * c4:128 * (c4 + 1)],
                                               in1=exf[:, 2, :], op=ALU.mult))
                        if not pre:
                            v.wait_ge(gV.h, gV.n)
                            gV.tag(v.tensor_tensor(out=rstash[:, g8, c4, :],
                                                   in0=rkm[:, g8, 128 * c4:128 * (c4 + 1)],
                                                   in1=exf[:, 0, :], op=ALU.mult))
                        ev[("ktf_v", ui, g8, c4)] = gV.n

                    @ctx.block.tensor
                    def _(t, g8=g8, c4=c4, ui=ui):
                        t.wait_ge(gV.h, ev[("ktf_v", ui, g8, c4)])
                        t.wait_ge(gV.h, ev.get(("ktpev_v", ui, g8, c4 - 1),
                                               ev.get(("ktpev_v", ui, g8 - 1, 3), 0)))
                        ev[("kT_t", ui, g8, c4)] = gT.n + 1
                        gT.tag(t.transpose(out=pTb[:], in_=ktf[:], identity=idn[:]))

                    @ctx.block.vector
                    def _(v, g8=g8, c4=c4, ui=ui):
                        v.wait_ge(gT.h, ev[("kT_t", ui, g8, c4)])
                        v.wait_ge(gT.h, ev.get(("u_t", ui, g8, c4 - 1),
                                               ev.get(("u_t", ui, g8 - 1, 3), 0)))
                        for cc in range(4):
                            gV.tag(v.tensor_scalar(out=ktp_z[:, cc, :], in0=pTb[:],
                                                   scalar1=mskc[:, cc:cc + 1],
                                                   scalar2=None, op0=ALU.mult))
                        ev[("ktpev_v", ui, g8, c4)] = gV.n

                    if FCHK < 2:
                        ev[("uev_v", ui, g8, c4)] = ctx.sem("gV").n
                        ev[("u_t", ui, g8, c4)] = ctx.sem("gT").n
                        continue

                    vbase_u = ctx.sem("gV").n

                    @ctx.block.tensor
                    def _(t, g8=g8, c4=c4, ui=ui, vbase_u=vbase_u):
                        t.wait_ge(gV.h, vbase_u)
                        for cc in range(4):
                            dst = pU if cc < 2 else pT
                            ev[("u_t", ui, g8, c4, cc)] = gT.n + 1
                            gT.tag(t.matmul(out=dst[:, 256 * (cc % 2):256 * (cc % 2) + 256],
                                            lhsT=ktp_z[:, cc, :],
                                            rhs=vpm[:, c4, g8, :],
                                            start=True, stop=True))
                        ev[("u_t", ui, g8, c4)] = gT.n

                    @ctx.block.vector
                    def _(v, g8=g8, c4=c4, ui=ui, pre=pre, vbase_u=vbase_u):
                        assert ctx.sem("gV").n == vbase_u
                        for cc in range(4):
                            if cc % 2 == 0:
                                v.wait_ge(gT.h, ev[("u_t", ui, g8, c4, cc + 1)])
                            src = pU if cc < 2 else pT
                            for j in range(4):
                                gV.tag(v.tensor_scalar(
                                    out=ubuf[32 * j:32 * (j + 1), g8, 4 * c4 + cc, :],
                                    in0=src[32 * j:32 * (j + 1),
                                            256 * (cc % 2) + 64 * j:256 * (cc % 2) + 64 * j + 64],
                                    scalar1=(pflag[32 * j:32 * (j + 1), 0:1] if pre else 1.0),
                                    scalar2=None, op0=ALU.mult))
                        ev[("uev_v", ui, g8, c4)] = gV.n
                    if FCHK < 3:
                        continue

                    if not pre:
                        @ctx.block.vector
                        def _(v, g8=g8, c4=c4, ui=ui):
                            v.wait_ge(gV.h, ev[("ktf_v", ui, g8, c4)])
                            v.wait_ge(gT.h, ev.get(("p_t", ui, g8, c4 - 1),
                                                   ev.get(("p_t", ui, g8 - 1, 3), 0)))
                            v.wait_ge(gT.h, ev.get(("d_t", ui, g8, c4 - 1),
                                                   ev.get(("d_t", ui, g8 - 1, 3), 0)))
                            for hp in range(4):
                                gV.tag(v.tensor_tensor(
                                    out=ktf_z[32 * hp:32 * (hp + 1), hp, :],
                                    in0=kkm[32 * hp:32 * (hp + 1), g8, 128 * c4:128 * (c4 + 1)],
                                    in1=exf[32 * hp:32 * (hp + 1), 1, :], op=ALU.mult))
                            gV.tag(v.tensor_tensor(out=rbk[:],
                                                   in0=kkm[:, g8, 128 * c4:128 * (c4 + 1)],
                                                   in1=rkm[:, g8, 128 * c4:128 * (c4 + 1)],
                                                   op=ALU.mult))
                            ev[("ktf2_v", ui, g8, c4)] = gV.n

                        @ctx.block.tensor
                        def _(t, g8=g8, c4=c4, ui=ui):
                            t.wait_ge(gV.h, ev[("ktf2_v", ui, g8, c4)])
                            t.wait_ge(gV.h, ev.get(("pev_v", ui, g8, c4 - 1),
                                                   ev.get(("pev_v", ui, g8 - 1, 3), 0)))
                            for hp in range(4):
                                mm = t.matmul(out=pT[:, 128 * hp:128 * (hp + 1)],
                                              lhsT=ktf_z[:, hp, :],
                                              rhs=rstash[:, g8, c4, :],
                                              start=True, stop=True)
                                if hp == 3:
                                    gT.tag(mm)
                            ev[("p_t", ui, g8, c4)] = gT.n
                            if FCHK >= 4:
                                t.wait_ge(gV.h, ev.get(("dcev_v", ui, g8, c4 - 1),
                                                       ev.get(("dcev_v", ui, g8 - 1, 3), 0)))
                                gT.tag(t.matmul(out=pI[:, 256:260], lhsT=rbk[:],
                                                rhs=bonsb[:, g8, :], start=True, stop=True))
                                ev[("d_t", ui, g8, c4)] = gT.n

                        @ctx.block.vector
                        def _(v, g8=g8, c4=c4, ui=ui):
                            v.wait_ge(gT.h, ev[("p_t", ui, g8, c4)])
                            for cc in range(4):
                                for hp in range(4):
                                    gV.tag(v.tensor_tensor(
                                        out=pmsb[32 * cc:32 * (cc + 1), g8, c4, hp, :],
                                        in0=pT[32 * cc:32 * (cc + 1),
                                               128 * hp + 32 * cc:128 * hp + 32 * (cc + 1)],
                                        in1=pmaskb[32 * cc:32 * (cc + 1), hp, :],
                                        op=ALU.mult))
                            ev[("pev_v", ui, g8, c4)] = gV.n
                            if FCHK >= 4:
                                v.wait_ge(gT.h, ev[("d_t", ui, g8, c4)])
                                ev[("dcev_v", ui, g8, c4)] = gV.n + 1
                                gV.tag(v.tensor_copy(out=dstash[:, g8, c4, :], in_=pI[:, 256:260]))

            for g8 in range(NG):
                if FST < 4:
                    ev[("scan_v", ui, g8)] = ctx.sem("gV").n
                    continue

                @ctx.block.vector
                def _(v, g8=g8, ui=ui, pre=pre):
                    v.wait_ge(gV.h, ev[("uev_v", ui, g8, 3)])
                    gV.tag(v.tensor_copy(out=straj[:, 0, :], in_=scar[:, g8, :]))
                    for c in range(16):
                        v.wait_ge(gV.h, gV.n)
                        gV.tag(v.scalar_tensor_tensor(
                            out=straj[:, c + 1, :], in0=straj[:, c, :],
                            scalar=ecol[:, g8, c:c + 1], in1=ubuf[:, g8, c, :],
                            op0=ALU.mult, op1=ALU.add))
                    v.wait_ge(gV.h, gV.n)
                    gV.tag(v.tensor_copy(out=scar[:, g8, :], in_=straj[:, 16, :]))
                ev[("scan_v", ui, g8)] = gV.n

                if not pre and FST >= 5:
                    for c4 in range(4):
                        @ctx.block.vector
                        def _(v, g8=g8, c4=c4, ui=ui):
                            v.wait_ge(gV.h, ev[("scan_v", ui, g8)])
                            v.wait_ge(gT.h, ev.get(("oi_t", ui, g8, c4 - 1),
                                                   ev.get(("oi_t", ui, g8 - 1, 3), 0)))
                            v.wait_ge(gV.h, ev[("pev_v", ui, g8, c4)])
                            for cc in range(4):
                                for j in range(4):
                                    gV.tag(v.tensor_scalar(
                                        out=sbf_bd[32 * j:32 * (j + 1), cc, 64 * j:64 * (j + 1)],
                                        in0=straj[32 * j:32 * (j + 1), 4 * c4 + cc, :],
                                        scalar1=emc[32 * j:32 * (j + 1), g8,
                                                    4 * c4 + cc:4 * c4 + cc + 1],
                                        scalar2=None, op0=ALU.mult))
                            for cc in range(4):
                                gV.tag(v.tensor_copy(
                                    out=pmbd[32 * cc:32 * (cc + 1), :, 32 * cc:32 * (cc + 1)],
                                    in_=pmsb[32 * cc:32 * (cc + 1), g8, c4, :, :]))
                            ev[("sbf_v", ui, g8, c4)] = gV.n

                        @ctx.block.tensor
                        def _(t, g8=g8, c4=c4, ui=ui):
                            t.wait_ge(gV.h, ev[("sbf_v", ui, g8, c4)])
                            t.wait_ge(gV.h, ev.get(("oev_v", ui, g8, c4 - 1),
                                                   ev.get(("oev_v", ui, g8 - 1, 3), 0)))
                            for hp in range(4):
                                mm = t.matmul(
                                    out=pI[:, 64 * hp:64 * (hp + 1)],
                                    lhsT=pmbd[:, hp, :],
                                    rhs=vpm[:, c4, g8, 64 * hp:64 * (hp + 1)],
                                    start=True, stop=True)
                                if hp == 3:
                                    gT.tag(mm)
                            for cc in range(4):
                                mm = t.matmul(
                                    out=pN[cc // 2][:, 256 * (cc % 2):256 * (cc % 2) + 256],
                                    lhsT=rstash[:, g8, c4, :],
                                    rhs=sbf_bd[:, cc, :],
                                    start=True, stop=True)
                                if cc == 3:
                                    gT.tag(mm)
                        ev[("oi_t", ui, g8, c4)] = gT.n

                        @ctx.block.vector
                        def _(v, g8=g8, c4=c4, ui=ui):
                            v.wait_ge(gT.h, ev[("oi_t", ui, g8, c4)])
                            v.wait_ge(gV.h, ev[("dcev_v", ui, g8, c4)])
                            for hp in range(4):
                                v.wait_ge(gV.h, gV.n)
                                gV.tag(v.scalar_tensor_tensor(
                                    out=osb[:, 64 * hp:64 * (hp + 1)],
                                    in0=vpm[:, c4, g8, 64 * hp:64 * (hp + 1)],
                                    scalar=dstash[:, g8, c4, hp:hp + 1],
                                    in1=pI[:, 64 * hp:64 * (hp + 1)],
                                    op0=ALU.mult, op1=ALU.add))
                            for cc in range(4):
                                v.wait_ge(gV.h, gV.n)
                                gV.tag(v.tensor_tensor(
                                    out=osb[32 * cc:32 * (cc + 1), :],
                                    in0=osb[32 * cc:32 * (cc + 1), :],
                                    in1=pN[cc // 2][32 * cc:32 * (cc + 1),
                                                    256 * (cc % 2):256 * (cc % 2) + 256],
                                    op=ALU.add))
                            v.wait_ge(gV.h, gV.n)
                            ev[("oev_v", ui, g8, c4)] = gV.n + 1
                            gV.tag(v.tensor_reduce(out=gst[:, 0:4], in_=osb.ap().rearrange(
                                "p (h v) -> p h v", h=4), axis=AX.X, op=ALU.add))

                        @ctx.block.scalar
                        def _(s, g8=g8, c4=c4, ui=ui):
                            s.wait_ge(gV.h, ev[("oev_v", ui, g8, c4)])
                            ev[("sq_s", ui, g8, c4)] = gS.n + 1
                            gS.tag(s.activation(out=sqf[:], in_=osb[:], func=AF.Square))

                        @ctx.block.vector
                        def _(v, g8=g8, c4=c4, ui=ui):
                            v.wait_ge(gS.h, ev[("sq_s", ui, g8, c4)])
                            gV.tag(v.tensor_reduce(out=gst[:, 4:8], in_=sqf.ap().rearrange(
                                "p (h v) -> p h v", h=4), axis=AX.X, op=ALU.add))
                            v.wait_ge(gV.h, gV.n)
                            gV.tag(v.tensor_scalar(out=gst[:, 0:4], in0=gst[:, 0:4],
                                                   scalar1=1.0 / 64, scalar2=None, op0=ALU.mult))
                            v.wait_ge(gV.h, gV.n)
                            gV.tag(v.tensor_tensor(out=gst[:, 8:12], in0=gst[:, 0:4],
                                                   in1=gst[:, 0:4], op=ALU.mult))
                            v.wait_ge(gV.h, gV.n)
                            gV.tag(v.tensor_scalar(out=gst[:, 4:8], in0=gst[:, 4:8],
                                                   scalar1=1.0 / 64, scalar2=None, op0=ALU.mult))
                            v.wait_ge(gV.h, gV.n)
                            ev[("var_v", ui, g8, c4)] = gV.n + 1
                            gV.tag(v.tensor_tensor(out=gst[:, 4:8], in0=gst[:, 4:8],
                                                   in1=gst[:, 8:12], op=ALU.subtract))

                        @ctx.block.scalar
                        def _(s, g8=g8, c4=c4, ui=ui):
                            s.wait_ge(gV.h, ev[("var_v", ui, g8, c4)])
                            gS.tag(s.activation(out=gst[:, 12:16], in_=gst[:, 4:8],
                                                func=AF.Ln, bias=ctx.cc[1e-5]))
                            s.wait_ge(gS.h, gS.n)
                            ev[("rstd_s", ui, g8, c4)] = gS.n + 1
                            gS.tag(s.activation(out=gst[:, 8:12], in_=gst[:, 12:16],
                                                func=AF.Exp, scale=-0.5))
                            ev[("sgl_s", ui, g8, c4)] = gS.n + 1
                            gS.tag(s.activation(out=sgl[:], in_=gpm[:, c4, g8, :],
                                                func=AF.Silu))

                        @ctx.block.vector
                        def _(v, g8=g8, c4=c4, ui=ui):
                            v.wait_ge(gS.h, ev[("rstd_s", ui, g8, c4)])
                            for hp in range(4):
                                v.wait_ge(gV.h, gV.n)
                                gV.tag(v.tensor_scalar(out=onb[:, 64 * hp:64 * (hp + 1)],
                                                       in0=osb[:, 64 * hp:64 * (hp + 1)],
                                                       scalar1=gst[:, hp:hp + 1],
                                                       scalar2=gst[:, 8 + hp:9 + hp],
                                                       op0=ALU.subtract, op1=ALU.mult))
                            v.wait_ge(gS.h, ev[("sgl_s", ui, g8, c4)])
                            v.wait_ge(gV.h, gV.n)
                            gV.tag(v.tensor_tensor(out=t1g[:], in0=sgl[:], in1=gnwb[:],
                                                   op=ALU.mult))
                            v.wait_ge(gV.h, gV.n)
                            gV.tag(v.tensor_tensor(out=gtin[:], in0=onb[:], in1=t1g[:],
                                                   op=ALU.mult))
                            v.wait_ge(gV.h, gV.n)
                            gV.tag(v.tensor_tensor(out=t1g[:], in0=sgl[:], in1=gnbb[:],
                                                   op=ALU.mult))
                            v.wait_ge(gV.h, gV.n)
                            ev[("gated_v", ui, g8, c4)] = gV.n + 1
                            gV.tag(v.tensor_tensor(out=gtin[:], in0=gtin[:], in1=t1g[:],
                                                   op=ALU.add))

                        sbaseg = ctx.sem("gS").n

                        @ctx.block.tensor
                        def _(t, g8=g8, c4=c4, ui=ui, sbaseg=sbaseg):
                            t.wait_ge(gV.h, ev[("gated_v", ui, g8, c4)])
                            t.wait_ge(gV.h, ev.get(("ktpev_v", ui, 7, 3), 0))
                            for hh in range(2):
                                if hh == 0:
                                    t.wait_ge(gS.h, ev.get(("gtev_prev", ui, g8, c4), 0))
                                else:
                                    t.wait_ge(gS.h, sbaseg + 1)
                                ev[("gTx", ui, g8, c4, hh)] = gT.n + 1
                                gT.tag(t.transpose(out=pTb[:], in_=gtin[:, 128 * hh:128 * (hh + 1)],
                                                   identity=idn[:]))

                        @ctx.block.scalar
                        def _(s, g8=g8, c4=c4, ui=ui):
                            for hh in range(2):
                                s.wait_ge(gT.h, ev[("gTx", ui, g8, c4, hh)])
                                if gO.issued >= 2 - hh:
                                    gO.wait(s, gO.issued - 2 + hh)
                                ev[("gtev_s", ui, g8, c4, hh)] = gS.n + 1
                                gS.tag(s.copy(out=gtb[hh][:], in_=pTb[:]))
                            ev[("gtev_prev", ui, g8, c4 + 1)] = ev[("gtev_s", ui, g8, c4, 1)]
                            if c4 == 3:
                                ev[("gtev_prev", ui, g8 + 1, 0)] = ev[("gtev_s", ui, g8, c4, 1)]

                        @ctx.block.gpsimd
                        def _(g, g8=g8, c4=c4, ui=ui, u=u):
                            for hh in range(2):
                                g.wait_ge(gS.h, ev[("gtev_s", ui, g8, c4, hh)])
                                row = g8 * 256 + hh * 128
                                col = u * 512 + c4 * 128
                                gO.issue(g, lambda: g.dma_start(
                                    out=gatedT[row:row + 128, col:col + 128],
                                    in_=gtb[hh][:]))

            ev[("uend_t", ui)] = gT.n
            ev[("uend_v", ui)] = gV.n
            ev[("uend_s", ui)] = gS.n

        @ctx.block.gpsimd
        def _(g):
            for k in range(max(0, gO.issued - 4), gO.issued):
                gO.wait(g, k)


def ph_g0(ctx, upT, downT, upS, dnS, bufs_es=None):
    nc = ctx.nc
    hW = DRing(ctx, "hW", 2)
    hS = DRing(ctx, "hS", 2)
    hC = ctx.sem("hC")
    jobs = ([("u", c, j) for c in range(16) for j in range(4)]
            + [("d", c, 0) for c in range(64)])
    with contextlib.ExitStack() as _es:
        es = bufs_es if bufs_es is not None else _es
        wci = [es.enter_context(nc.sbuf_tensor(f"phg0_i{i}", [128, D], F32)) for i in range(2)]
        wco = [es.enter_context(nc.sbuf_tensor(f"phg0_o{i}", [128, D], BF16)) for i in range(2)]

        def span(kind, c, j):
            srct = upT if kind == "u" else downT
            dst = upS if kind == "u" else dnS
            return (srct[128 * c:128 * (c + 1), 2048 * j:2048 * (j + 1)],
                    dst[128 * c:128 * (c + 1), 2048 * j:2048 * (j + 1)])

        @ctx.block.sync
        def _(q):
            def load(ji):
                kind, c, j = jobs[ji]
                if ji >= 2:
                    q.wait_ge(hC.h, hC.at(ji - 1))
                srcap, _ = span(kind, c, j)
                hW.issue(q, lambda: q.dma_start(out=wci[ji % 2][:], in_=srcap))
            load(0)
            load(1)
            for ji in range(len(jobs)):
                if ji + 2 < len(jobs):
                    load(ji + 2)

        @ctx.block.scalar
        def _(s):
            for ji, (kind, c, j) in enumerate(jobs):
                hW.wait(s, ji)
                if ji >= 2:
                    hS.wait(s, ji - 2)
                hC.tag(s.activation(out=wco[ji % 2][:], in_=wci[ji % 2][:], func=AF.Sign))

        @ctx.block.gpsimd
        def _(g):
            for ji, (kind, c, j) in enumerate(jobs):
                g.wait_ge(hC.h, hC.at(ji + 1))
                _, dstap = span(kind, c, j)
                hS.issue(g, lambda: g.dma_start(out=dstap, in_=wco[ji % 2][:]))


def ph_g(ctx, gatedT, wo, xn1o, upS, dnS, upsc, dnsc, osc, x1o, out):
    nc = ctx.nc
    NHC1 = HID1 // 128
    ev = {}
    kG = DRing(ctx, "kG", 4)
    kW = DRing(ctx, "kW", 2)
    kX = DRing(ctx, "kX", 4)
    kU = DRing(ctx, "kU", 2)
    kD = DRing(ctx, "kD", 2)
    kR = DRing(ctx, "kR", 2)
    kO = DRing(ctx, "kO", 2)
    kC = ctx.sem("kC", 16)
    kM = ctx.sem("kM")
    kE = ctx.sem("kE")
    kMu = ctx.sem("kMu")
    kRe = ctx.sem("kRe")
    kMd = ctx.sem("kMd")
    kF = ctx.sem("kF")
    kSi = ctx.sem("kSi")
    kV = ctx.sem("kV")
    kI = ctx.sem("kI")

    with contextlib.ExitStack() as es:
        gb = es.enter_context(nc.sbuf_tensor("phg_g", [128, 16, 512], BF16))
        wst = [es.enter_context(nc.sbuf_tensor(f"phg_w{i}", [128, 16, 128], BF16)) for i in range(2)]
        xb = es.enter_context(nc.sbuf_tensor("phg_x", [128, 16, 512], BF16))
        wu = [es.enter_context(nc.sbuf_tensor(f"phg_wu{i}", [128, 16, 128], BF16)) for i in range(2)]
        wd = [es.enter_context(nc.sbuf_tensor(f"phg_wd{i}", [128, NHC1, 128], BF16)) for i in range(2)]
        s1b = es.enter_context(nc.sbuf_tensor("phg_s1", [128, 16, 512], BF16))
        prod = es.enter_context(nc.sbuf_tensor("phg_pr", [128, NHC1, 512], BF16))
        scu = es.enter_context(nc.sbuf_tensor("phg_su", [128, NHC1], F32))
        scd = es.enter_context(nc.sbuf_tensor("phg_sd", [128, 16], F32))
        osc_s = es.enter_context(nc.sbuf_tensor("phg_os", [1, 1], F32))
        oner = es.enter_context(nc.sbuf_tensor("phg_1r", [1, 128], F32))
        oscol = es.enter_context(nc.sbuf_tensor("phg_oc", [128, 1], F32))
        f1t = es.enter_context(nc.sbuf_tensor("phg_f1", [128, 512], BF16))
        g1t = es.enter_context(nc.sbuf_tensor("phg_g1", [128, 512], BF16))
        x1c = [es.enter_context(nc.sbuf_tensor(f"phg_x1{i}", [128, 512], F32)) for i in range(2)]
        x2t = [es.enter_context(nc.sbuf_tensor(f"phg_x2{i}", [128, 512], F32)) for i in range(2)]
        pW = [es.enter_context(nc.psum_tensor(f"phg_pW{i}", [128, 512], F32)) for i in range(2)]
        pF = [es.enter_context(nc.psum_tensor(f"phg_pF{i}", [128, 512], F32)) for i in range(2)]
        pOs = es.enter_context(nc.psum_tensor("phg_pOs", [128, 1], F32))

        @ctx.block.sync
        def _(q):
            kC.tag(q.dma_start(out=scu[:], in_=upsc))
            kC.tag(q.dma_start(out=scd[:], in_=dnsc))
            kC.tag(q.dma_start(out=osc_s[:], in_=osc))

        @ctx.block.vector
        def _(v):
            kI.tag(v.memset(oner[:], 1.0))

        @ctx.block.tensor
        def _(t):
            t.wait_ge(kC.h, kC.at(3))
            t.wait_ge(kI.h, 1)
            kI.tag(t.matmul(out=pOs[:], lhsT=oner[:], rhs=osc_s[:], start=True, stop=True))

        @ctx.block.vector
        def _(v):
            v.wait_ge(kI.h, 2)
            kI.tag(v.tensor_copy(out=oscol[:], in_=pOs[:]))

        for qt in range(4):
            @ctx.block.sync
            def _(q, qt=qt):
                for fc in range(16):
                    if qt >= 1:
                        q.wait_ge(kM.h, kM.at(16 * qt))
                    kG.issue(q, lambda: q.dma_start(
                        out=gb[:, fc, :], in_=gatedT[128 * fc:128 * (fc + 1),
                                                     512 * qt:512 * (qt + 1)]))

            @ctx.block.gpsimd
            def _(g, qt=qt):
                for dc in range(16):
                    si = 16 * qt + dc
                    if si >= 2:
                        g.wait_ge(kM.h, kM.at(si - 1))
                    kW.issue(g, lambda: g.dma_start(
                        out=wst[si % 2][:],
                        in_=wo[:, 128 * dc:128 * (dc + 1)].rearrange("(c p) f -> p c f", p=128)))

            @ctx.block.tensor
            def _(t, qt=qt):
                for dc in range(16):
                    si = 16 * qt + dc
                    kW.wait(t, si)
                    if dc == 0:
                        for fc in range(16):
                            kG.wait(t, 16 * qt + fc)
                    if si >= 2:
                        t.wait_ge(kE.h, kE.at(si - 1))
                    for fc in range(15):
                        t.matmul(out=pW[si % 2][:], lhsT=wst[si % 2][:, fc, :],
                                 rhs=gb[:, fc, :], start=(fc == 0), stop=False)
                    kM.tag(t.matmul(out=pW[si % 2][:], lhsT=wst[si % 2][:, 15, :],
                                    rhs=gb[:, 15, :], start=False, stop=True))

            @ctx.block.scalar
            def _(s, qt=qt):
                for dc in range(16):
                    si = 16 * qt + dc
                    s.wait_ge(kM.h, kM.at(si + 1))
                    if qt >= 1 and dc == 0:
                        s.wait_ge(kV.h, kV.at(32 * qt))
                    kE.tag(s.copy(out=s1b[:, dc, :], in_=pW[si % 2][:]))
            ev[("s1_s", qt)] = ctx.sem("kE").n

            @ctx.block.sync
            def _(q, qt=qt):
                for dc in range(16):
                    if qt >= 1:
                        q.wait_ge(kMu.h, kMu.at(NHC1 * qt))
                    kX.issue(q, lambda: q.dma_start(
                        out=xb[:, dc, :], in_=xn1o[128 * dc:128 * (dc + 1),
                                                   1 + 512 * qt:1 + 512 * (qt + 1)]))
                for hid in range(NHC1):
                    j = NHC1 * qt + hid
                    if j >= 2:
                        q.wait_ge(kMu.h, kMu.at(j - 1))
                    kU.issue(q, lambda: q.dma_start(
                        out=wu[j % 2][:],
                        in_=upS[:, 128 * hid:128 * (hid + 1)].rearrange("(c p) f -> p c f", p=128)))

            @ctx.block.tensor
            def _(t, qt=qt):
                for fc in range(16):
                    kX.wait(t, 16 * qt + fc)
                for hid in range(NHC1):
                    j = NHC1 * qt + hid
                    kU.wait(t, j)
                    if j >= 2:
                        t.wait_ge(kRe.h, kRe.at(j - 1))
                    if hid == 0 and qt >= 1:
                        t.wait_ge(kMd.h, kMd.at(16 * qt))
                    for dc in range(15):
                        t.matmul(out=pF[j % 2][:], lhsT=wu[j % 2][:, dc, :],
                                 rhs=xb[:, dc, :], start=(dc == 0), stop=False)
                    kMu.tag(t.matmul(out=pF[j % 2][:], lhsT=wu[j % 2][:, 15, :],
                                     rhs=xb[:, 15, :], start=False, stop=True))

            @ctx.block.vector
            def _(v, qt=qt):
                for hid in range(NHC1):
                    j = NHC1 * qt + hid
                    v.wait_ge(kMu.h, kMu.at(j + 1))
                    kRe.tag(v.tensor_scalar(out=prod[:, hid, :], in0=pF[j % 2][:],
                                            scalar1=scu[:, hid:hid + 1], scalar2=0.0,
                                            op0=ALU.mult, op1=ALU.max))

            @ctx.block.sync
            def _(q, qt=qt):
                for dc in range(16):
                    si = 16 * qt + dc
                    if si >= 2:
                        q.wait_ge(kMd.h, kMd.at(si - 1))
                        q.wait_ge(kV.h, kV.at(2 * si - 2))
                    kD.issue(q, lambda: q.dma_start(
                        out=wd[si % 2][:],
                        in_=dnS[:, 128 * dc:128 * (dc + 1)].rearrange("(c p) f -> p c f", p=128)))
                    kR.issue(q, lambda: q.dma_start(
                        out=x1c[si % 2][:], in_=x1o[128 * dc:128 * (dc + 1),
                                                    512 * qt:512 * (qt + 1)]))

            @ctx.block.tensor
            def _(t, qt=qt):
                t.wait_ge(kRe.h, kRe.at(NHC1 * (qt + 1)))
                for dc in range(16):
                    si = 16 * qt + dc
                    kD.wait(t, si)
                    if si >= 2:
                        t.wait_ge(kF.h, kF.at(si - 1))
                    for hid in range(NHC1 - 1):
                        t.matmul(out=pF[si % 2][:], lhsT=wd[si % 2][:, hid, :],
                                 rhs=prod[:, hid, :], start=(hid == 0), stop=False)
                    kMd.tag(t.matmul(out=pF[si % 2][:], lhsT=wd[si % 2][:, NHC1 - 1, :],
                                     rhs=prod[:, NHC1 - 1, :], start=False, stop=True))

            @ctx.block.vector
            def _(v, qt=qt):
                v.wait_ge(kI.h, 3)

                def f1_step(si, dc):
                    v.wait_ge(kMd.h, kMd.at(si + 1))
                    if si >= 1:
                        v.wait_ge(kSi.h, kSi.at(si))
                    kF.tag(v.tensor_scalar(out=f1t[:], in0=pF[si % 2][:],
                                           scalar1=scd[:, dc:dc + 1], scalar2=oscol[:, 0:1],
                                           op0=ALU.mult, op1=ALU.mult))

                def resid_step(si, dc):
                    v.wait_ge(kSi.h, kSi.at(si + 1))
                    kR.wait(v, si)
                    if si >= 2:
                        kO.wait(v, si - 2)
                    kV.tag(v.tensor_tensor(out=x2t[si % 2][:], in0=s1b[:, dc, :],
                                           in1=g1t[:], op=ALU.mult))
                    v.wait_ge(kV.h, kV.n)
                    kV.tag(v.tensor_tensor(out=x2t[si % 2][:], in0=x2t[si % 2][:],
                                           in1=x1c[si % 2][:], op=ALU.add))

                for dc in range(16):
                    si = 16 * qt + dc
                    f1_step(si, dc)
                    if dc >= 1:
                        resid_step(si - 1, dc - 1)
                resid_step(16 * qt + 15, 15)

            @ctx.block.scalar
            def _(s, qt=qt):
                for dc in range(16):
                    si = 16 * qt + dc
                    s.wait_ge(kF.h, kF.at(si + 1))
                    if si >= 1:
                        s.wait_ge(kV.h, kV.at(2 * si))
                    kSi.tag(s.activation(out=g1t[:], in_=f1t[:], func=AF.Silu))

            @ctx.block.gpsimd
            def _(g, qt=qt):
                for dc in range(16):
                    si = 16 * qt + dc
                    g.wait_ge(kV.h, kV.at(2 * si + 2))
                    kO.issue(g, lambda: g.dma_start(
                        out=out[128 * dc:128 * (dc + 1), 512 * qt:512 * (qt + 1)],
                        in_=x2t[si % 2][:]))

        @ctx.block.gpsimd
        def _(g):
            for k in range(max(0, kO.issued - 2), kO.issued):
                kO.wait(g, k)



def build_l1():
    nc = bass.Bass("TRN2", target_bir_lowering=False,
                   detect_race_conditions=False)
    ei = lambda n, s, d: nc.dram_tensor(n, s, d, kind="ExternalInput").ap()
    x0 = ei("x0", [L, D], F32)
    x0T = ei("x0T", [D, L], F32)
    wqkvT = ei("wqkvT", [D, 6144], F32)
    bqkv = ei("bqkv", [128, 48], F32)
    cs_tab = ei("cs_tab", [64, L], BF16)
    maskq = ei("maskq", [4, 128, 512], BF16)
    woutT = ei("woutT", [D, D], F32)
    bout = ei("bout", [128, 16], F32)
    wgTq = ei("wgTq", [D, HID0], I32)
    wuTq = ei("wuTq", [D, HID0], I32)
    wdTq = ei("wdTq", [HID0, D], I32)
    sgbc = ei("sgbc", [128, HID0], F32)
    subc = ei("subc", [128, HID0], F32)
    sdbc = ei("sdbc", [128, D], F32)
    xn0_tm = nc.dram_tensor("xn0_tm", [L, D], BF16).ap()
    qf = nc.dram_tensor("qf", [16, 128, L], BF16).ap()
    kf = nc.dram_tensor("kf", [16, 128, L], BF16).ap()
    vf = nc.dram_tensor("vf", [16, 128, L], BF16).ap()
    of = nc.dram_tensor("of", [16, 128, L], BF16).ap()
    s0f = nc.dram_tensor("s0f", [D, L], BF16).ap()
    wgq = nc.dram_tensor("wgq", [D, HID0], BF16).ap()
    wuq = nc.dram_tensor("wuq", [D, HID0], BF16).ap()
    wdq = nc.dram_tensor("wdq", [HID0, D], BF16).ap()
    x1f = nc.dram_tensor("x1f", [D, L], F32, kind="ExternalOutput").ap()
    xn1f = nc.dram_tensor("xn1f", [D, L], BF16, kind="ExternalOutput").ap()

    es = contextlib.ExitStack()
    with es:
        ctx = Ctx(nc, es)
        ctx.block = es.enter_context(nc.Block())
        ctx.init_consts([1e-5, -12.0])
        es0 = contextlib.ExitStack()
        ph_e0(ctx, wgTq, wuTq, wdTq, sgbc, subc, sdbc, wgq, wuq, wdq, bufs_es=es0)
        ph_a(ctx, x0, xn0_tm)
        ctx.barrier()
        ph_b(ctx, xn0_tm, wqkvT, bqkv, cs_tab, qf, kf, vf)
        ctx.barrier()
        es0.close()
        ph_c(ctx, qf, kf, vf, maskq, of)
        ctx.barrier()
        ph_d(ctx, of, woutT, bout, s0f)
        ctx.barrier()
        ph_e(ctx, xn0_tm, wgq, wuq, wdq, x0T, s0f, x1f, xn1f)
        ctx.barrier()
    return nc


def build_l2():
    nc = bass.Bass("TRN2", target_bir_lowering=False,
                   detect_race_conditions=False)
    ei = lambda n, s, d: nc.dram_tensor(n, s, d, kind="ExternalInput").ap()
    xn1pre = ei("xn1pre", [D, 2049], BF16)
    xn1own = ei("xn1own", [D, 2049], BF16)
    x1o = ei("x1o", [D, 2048], F32)
    wf = {k: ei(k, s, F32) for k, s in (
        ("wx1", [D, 160]), ("ww1", [D, 64]), ("wr", [D, 1024]), ("wk", [D, 1024]),
        ("wv", [D, 2048]), ("wg", [D, 2048]))}
    cst = {}
    for k, s, dt in (("mucol", [128, 16], F32), ("bwsb", [128, 8], F32),
                     ("pmask", [128, 128], BF16), ("idn", [128, 128], BF16),
                     ("pflag", [128, 1], F32), ("bonus", [128, 32], F32),
                     ("gnwb", [128, 256], F32), ("gnbb", [128, 256], F32),
                     ("wx2s", [128, 2048], F32), ("wx2s2", [32, 2048], F32),
                     ("ww2s", [64, 1024], F32)):
        cst[k] = ei(k, s, dt)
    wo = ei("wo", [D, D], F32)
    upT = ei("upT", [D, HID1], F32)
    downT = ei("downT", [HID1, D], F32)
    upsc = ei("upsc", [128, 64], F32)
    dnsc = ei("dnsc", [128, 16], F32)
    osc = ei("osc", [1, 1], F32)
    gatedT = nc.dram_tensor("gatedT", [D, 2048], BF16).ap()
    upS = nc.dram_tensor("upS", [D, HID1], BF16).ap()
    dnS = nc.dram_tensor("dnS", [HID1, D], BF16).ap()
    out = nc.dram_tensor("out", [D, 2048], F32, kind="ExternalOutput").ap()

    es = contextlib.ExitStack()
    with es:
        ctx = Ctx(nc, es)
        ctx.block = es.enter_context(nc.Block())
        ctx.init_consts([1e-5, -12.0])
        ph_g0(ctx, upT, downT, upS, dnS)
        ctx.barrier()
        ph_f(ctx, xn1pre, xn1own, wf, cst, gatedT)
        ctx.barrier()
        ph_g(ctx, gatedT, wo, xn1own, upS, dnS, upsc, dnsc, osc, x1o, out)
        ctx.barrier()
    return nc


def _bonus_bd(raw):
    r = raw.reshape(8, 4, 32)
    bd = np.zeros((4, 32, 8, 4), np.float32)
    for j in range(4):
        bd[j, :, :, j] = r[:, j, :].T
    return np.ascontiguousarray(bd.reshape(128, 32)).astype(np.float32)


def _as_f32(a):
    a = np.asarray(a)
    return a.view(np.float32) if (a.dtype != np.float32 and a.dtype.itemsize == 4
                                  and a.dtype != np.int32) else a.astype(np.float32)


def _bf_view(a):
    a = np.asarray(a)
    return a.view(BF) if a.dtype.itemsize == 2 and a.dtype != BF else a


def kernel(**ins):
    cs = np.ascontiguousarray

    x = np.asarray(ins["L0_attn_x"], np.float32)
    theta = 1.0 / 10000.0 ** (np.arange(0, 32, 2, dtype=np.float64) / 32)
    t = np.arange(L, dtype=np.float64)
    ang = t[None, :] * np.concatenate([theta, theta])[:, None]
    Cc = np.cos(ang)
    Ss = np.sin(ang)
    Ss[0:16] *= -1.0
    cs_tab = np.concatenate([Cc, Ss], 0).astype(BF)
    mq = np.zeros((4, 128, 512), np.float32)
    for i in range(4):
        for r in range(128):
            mq[i, r, 128 * i + r:] = 1.0
    d1 = {
        "wqkvT": cs(np.asarray(ins["L0_attn_W_qkv"], np.float32).T),
        "bqkv": cs(np.asarray(ins["L0_attn_b_qkv"], np.float32).reshape(48, 128).T),
        "cs_tab": cs_tab, "maskq": mq.astype(BF),
        "woutT": cs(np.asarray(ins["L0_attn_W_out"], np.float32).T),
        "bout": cs(np.asarray(ins["L0_attn_b_out"], np.float32).reshape(16, 128).T),
        "wgTq": cs(np.asarray(ins["L0_ffn_wq_gate"], np.int32).T),
        "wuTq": cs(np.asarray(ins["L0_ffn_wq_up"], np.int32).T),
        "wdTq": cs(np.asarray(ins["L0_ffn_wq_down"], np.int32).T),
        "sgbc": cs(np.broadcast_to(np.asarray(ins["L0_ffn_s_gate"], np.float32),
                                   (128, HID0))),
        "subc": cs(np.broadcast_to(np.asarray(ins["L0_ffn_s_up"], np.float32),
                                   (128, HID0))),
        "sdbc": cs(np.broadcast_to(np.asarray(ins["L0_ffn_s_down"], np.float32),
                                   (128, D))),
    }
    in_maps1 = []
    for c in range(8):
        m = dict(d1)
        m["x0"] = cs(x[c // 2])
        m["x0T"] = cs(x[c // 2].T)
        in_maps1.append(m)

    nc1 = build_l1()
    res1 = run_bass_kernel_spmd(nc1, in_maps1, core_ids=list(range(8)))

    d2 = {
        "wx1": np.asarray(ins["L1_attn_Wx1"], np.float32),
        "ww1": np.asarray(ins["L1_attn_Ww1"], np.float32),
        "wr": np.asarray(ins["L1_attn_Wr"], np.float32),
        "wk": np.asarray(ins["L1_attn_Wk"], np.float32),
        "wv": np.asarray(ins["L1_attn_Wv"], np.float32),
        "wg": np.asarray(ins["L1_attn_Wg"], np.float32),
        "mucol": cs(np.asarray(ins["L1_attn_mu_x"], np.float32).reshape(16, 128).T),
        "bwsb": cs(np.asarray(ins["L1_attn_bw"], np.float32).reshape(8, 128).T),
        "pmask": cs(np.tile(np.triu(np.ones((32, 32), np.float32), 1), (4, 4))).astype(BF),
        "idn": np.eye(128, dtype=np.float32).astype(BF),
        "bonus": _bonus_bd(np.asarray(ins["L1_attn_bonus"], np.float32)),
        "gnwb": cs(np.tile(np.asarray(ins["L1_attn_gn_w"], np.float32), (128, 4))),
        "gnbb": cs(np.tile(np.asarray(ins["L1_attn_gn_b"], np.float32), (128, 4))),
        "wx2s": cs(np.asarray(ins["L1_attn_Wx2"], np.float32).reshape(160, 2048)[0:128]),
        "wx2s2": cs(np.asarray(ins["L1_attn_Wx2"], np.float32).reshape(160, 2048)[128:160]),
        "ww2s": np.asarray(ins["L1_attn_Ww2"], np.float32),
        "wo": np.asarray(ins["L1_attn_Wo"], np.float32),
        "upT": cs(np.asarray(ins["L1_ffn_up_weight"], np.float32).T),
        "downT": cs(np.asarray(ins["L1_ffn_down_weight"], np.float32).T),
        "upsc": cs(np.asarray(ins["L1_ffn_up_scale"], np.float32).reshape(64, 128).T),
        "dnsc": cs(np.asarray(ins["L1_ffn_down_scale"], np.float32).reshape(16, 128).T),
        "osc": np.asarray(ins["L1_ffn_output_scale"], np.float32).reshape(1, 1),
    }
    in_maps2 = []
    for c in range(8):
        half = c % 2
        t0 = 2048 * half
        xn1 = _bf_view(res1.results[c]["xn1f"])
        x1 = _as_f32(res1.results[c]["x1f"])
        xpad = np.concatenate([np.zeros((D, 1), BF), np.asarray(xn1, BF)], 1)
        m = dict(d2)
        m["xn1pre"] = cs(xpad[:, 0:2049])
        m["xn1own"] = cs(xpad[:, t0:t0 + 2049])
        m["x1o"] = cs(x1[:, t0:t0 + 2048])
        m["pflag"] = np.full((128, 1), float(half), np.float32)
        in_maps2.append(m)

    nc2 = build_l2()
    res2 = run_bass_kernel_spmd(nc2, in_maps2, core_ids=list(range(8)))

    out = np.empty((B, L, D), np.float32)
    for c in range(8):
        b, half = c // 2, c % 2
        o = _as_f32(res2.results[c]["out"])
        out[b, 2048 * half:2048 * (half + 1), :] = o.T
    return out


# version 03
# speedup vs baseline: 1.0200x; 1.0200x over previous
import contextlib
import numpy as np
import ml_dtypes
import concourse.bass as bass
import concourse.mybir as mybir
from concourse.bass_utils import run_bass_kernel_spmd

F32 = mybir.dt.float32
BF16 = mybir.dt.bfloat16
I32 = mybir.dt.int32
AF = mybir.ActivationFunctionType
ALU = mybir.AluOpType
AX = mybir.AxisListType

B, L, D = 4, 4096, 2048
NH0, DH0, DR = 16, 128, 32
HID0 = 5632
NH1, HQK, HV = 32, 32, 64
HID1 = 8192
SOFT_C = 12.0
INVSQ = float(1.0 / np.sqrt(128.0))
BF = ml_dtypes.bfloat16
SHUF_ROT16 = [(j + 16) % 32 for j in range(32)]


class Sem:

    def __init__(self, h, step, base=0):
        self.h = h
        self.n = base
        self.base = base
        self.step = step

    def tag(self, inst):
        inst.then_inc(self.h, self.step)
        self.n += self.step
        return inst

    def guard_tag(self, eng, fn):
        if self.n:
            eng.wait_ge(self.h, self.n)
        inst = fn()
        self.tag(inst)
        return inst

    def use(self, eng):
        eng.wait_ge(self.h, self.n)

    def expect(self, k):
        assert self.n == self.base + k * self.step, (self.n, self.base, k, self.step)

    def at(self, k):
        return self.base + k * self.step


class DRing:

    def __init__(self, ctx, name, k, cls="hw"):
        self.sems = [ctx.sem(f"{name}{i}", 16, cls=cls) for i in range(k)]
        self.base = [s.n for s in self.sems]
        self.k = k
        self.issued = 0

    def issue(self, eng, fn):
        slot = self.issued % self.k
        prior = (self.issued // self.k) * 16
        if prior:
            eng.wait_ge(self.sems[slot].h, self.base[slot] + prior)
        inst = fn()
        self.sems[slot].tag(inst)
        self.issued += 1
        return inst

    def wait(self, eng, idx):
        if idx < 0:
            return
        assert idx < self.issued + self.k * 100
        s = self.sems[idx % self.k]
        eng.wait_ge(s.h, self.base[idx % self.k] + 16 * (idx // self.k + 1))


class Ctx:

    def __init__(self, nc, es):
        self.nc = nc
        self.es = es
        self._sems = {}
        self._phys_free = {}
        self._keep = set()

    def sem(self, name, step=1, cls=None):
        key = "c" if step == 1 else (cls or "hw")
        if name not in self._sems:
            pool = self._phys_free.setdefault(key, [])
            if pool and key != "sw":
                h, cnt = pool.pop()
            else:
                self._nalloc = getattr(self, "_nalloc", 0) + 1
                h, cnt = self.es.enter_context(
                    self.nc.semaphore(f"s{self._nalloc}_{name}")), 0
            self._sems[name] = Sem(h, step, base=cnt)
            self._sems[name].cls = key
        s = self._sems[name]
        assert s.step == step
        return s

    def new_phase(self):
        for name in list(self._sems):
            if name in self._keep:
                continue
            s = self._sems.pop(name)
            if s.cls != "sw":
                self._phys_free.setdefault(s.cls, []).append((s.h, s.n))

    def init_consts(self, values):
        self.cc = {}
        self.ccr = {}
        tile = self.sb("constcols", [128, max(1, len(values))], F32)
        rtile = self.sb("constrows", [1, max(1, len(values))], F32)
        for i, v in enumerate(values):
            self.cc[v] = tile[:, i:i + 1]
            self.ccr[v] = rtile[:, i:i + 1]

        @self.block.vector
        def _(ve):
            b = self.sem("bar")
            for i, v in enumerate(values):
                b.tag(ve.memset(tile[:, i:i + 1], float(v)))
                b.tag(ve.memset(rtile[:, i:i + 1], float(v)))
        b = self.sem("bar")
        for engname in ("tensor", "scalar", "gpsimd", "sync"):
            sec = getattr(self.block, engname)

            def body(eng):
                eng.wait_ge(b.h, b.n)

            sec(body)

    def sb(self, name, shape, dt):
        return self.es.enter_context(self.nc.sbuf_tensor(name, shape, dt))

    def ps(self, name, shape, dt=F32):
        return self.es.enter_context(self.nc.psum_tensor(name, shape, dt))

    def barrier(self, tag=""):
        bs = self.sem("bar", 1)
        waits = [(s.h, s.n) for nm, s in self._sems.items() if nm != "bar" and s.n > 0]
        base = bs.n
        for engname in ("tensor", "vector", "scalar", "gpsimd", "sync"):
            sec = getattr(self.block, engname)

            def body(eng, _eng=engname):
                for h, n in waits:
                    eng.wait_ge(h, n)
                if _eng != "sync":
                    bs.tag(eng.nop())
                eng.wait_ge(bs.h, base + 4)

            sec(body)
        assert bs.n == base + 4


def ph_a(ctx, x0, xn0_tm):
    nc = ctx.nc
    ntile = L // 128
    aL = DRing(ctx, "aL", 2)
    aN = ctx.sem("aN")
    aR = ctx.sem("aR")
    aP = ctx.sem("aP")
    xS = ctx.sem("xS")
    xV = ctx.sem("xV")
    aS = DRing(ctx, "aS", 2, cls="sw")
    with contextlib.ExitStack() as es:
        xt = [es.enter_context(nc.sbuf_tensor(f"pha_x{i}", [128, D], F32)) for i in range(2)]
        yt = [es.enter_context(nc.sbuf_tensor(f"pha_y{i}", [128, D], BF16)) for i in range(2)]
        st = es.enter_context(nc.sbuf_tensor("pha_st", [128, 2, 10], F32))
        dmp = es.enter_context(nc.sbuf_tensor("pha_dmp", [128, D], BF16))

        @ctx.block.sync
        def _(q):
            for i in range(ntile):
                if i >= 2:
                    q.wait_ge(aP.h, aP.at(i - 1))
                aL.issue(q, lambda: q.dma_start(out=xt[i % 2][:], in_=x0[128 * i:128 * (i + 1), :]))
        assert aL.issued == ntile

        aQ = ctx.sem("aQ")

        @ctx.block.scalar
        def _(s):
            for i in range(ntile):
                p = i % 2
                aL.wait(s, i)
                if i >= 2:
                    aS.wait(s, i - 2)
                xS.tag(s.activation(out=dmp[:], in_=xt[p][:], func=AF.Square,
                                    accum_out=st[:, p, 0:1]))
                xS.use(s)
                aQ.tag(s.activation(out=yt[p][:], in_=xt[p][:], func=AF.Identity,
                                    accum_out=st[:, p, 1:2]))
                s.wait_ge(aN.h, aN.at(i + 1))
                xS.tag(s.activation(out=st[:, p, 8:9], in_=st[:, p, 4:5], func=AF.Ln,
                                    bias=ctx.cc[1e-5]))
                xS.use(s)
                aR.tag(s.activation(out=st[:, p, 9:10], in_=st[:, p, 8:9],
                                    func=AF.Exp, scale=-0.5))

        @ctx.block.vector
        def _(v):
            for i in range(ntile):
                p = i % 2
                v.wait_ge(aQ.h, aQ.at(i + 1))
                xV.tag(v.tensor_scalar(out=st[:, p, 6:7], in0=st[:, p, 1:2],
                                       scalar1=1.0 / D, scalar2=None, op0=ALU.mult))
                xV.use(v)
                xV.tag(v.tensor_tensor(out=st[:, p, 5:6], in0=st[:, p, 6:7],
                                       in1=st[:, p, 6:7], op=ALU.mult))
                xV.use(v)
                aN.tag(v.tensor_scalar(out=st[:, p, 4:5], in0=st[:, p, 0:1],
                                       scalar1=1.0 / D, scalar2=st[:, p, 5:6],
                                       op0=ALU.mult, op1=ALU.subtract))
                v.wait_ge(aR.h, aR.at(i + 1))
                if i >= 2:
                    aS.wait(v, i - 2)
                aP.tag(v.tensor_scalar(out=yt[p][:], in0=xt[p][:],
                                       scalar1=st[:, p, 6:7], scalar2=st[:, p, 9:10],
                                       op0=ALU.subtract, op1=ALU.mult))

        @ctx.block.gpsimd
        def _(g):
            for i in range(ntile):
                g.wait_ge(aP.h, aP.at(i + 1))
                aS.issue(g, lambda: g.dma_start(out=xn0_tm[128 * i:128 * (i + 1), :], in_=yt[i % 2][:]))


def ph_b(ctx, xn0_tm, wqkvT, bqkv, cs_tab, qf, kf, vf):
    import os
    NODMAT = os.environ.get("NODMAT") == "1"
    NOW = os.environ.get("NOW") == "1"
    NOROPE = os.environ.get("NOROPE") == "1"
    nc = ctx.nc
    NC = 48
    NTG = int(os.environ.get("NTG", "4"))
    steps = [(tg, cc) for tg in range(NTG) for cc in range(NC)]
    NS = len(steps)
    bX = ctx.sem("bX", 16)
    bW = DRing(ctx, "bW", 2, cls="sw")
    bM = ctx.sem("bM")
    bE = ctx.sem("bE")
    bR = ctx.sem("bR")
    bS = DRing(ctx, "bS", 4, cls="sw")
    bC = ctx.sem("bC", 16)
    xV = ctx.sem("xV")

    with contextlib.ExitStack() as es:
        xf = [es.enter_context(nc.sbuf_tensor(f"phb_x{i}", [128, 16, 512], BF16)) for i in range(2)]
        wst = [es.enter_context(nc.sbuf_tensor(f"phb_w{i}", [128, 16, 128], BF16)) for i in range(2)]
        bia = es.enter_context(nc.sbuf_tensor("phb_b", [128, 48], F32))
        cstC = es.enter_context(nc.sbuf_tensor("phb_csC", [32, L], BF16))
        cstS = es.enter_context(nc.sbuf_tensor("phb_csS", [32, L], BF16))
        stg = [es.enter_context(nc.sbuf_tensor(f"phb_s{i}", [128, 512], BF16)) for i in range(4)]
        rt1 = es.enter_context(nc.sbuf_tensor("phb_r1", [32, 512], BF16))
        rt2 = es.enter_context(nc.sbuf_tensor("phb_r2", [32, 512], BF16))
        pp = [es.enter_context(nc.psum_tensor(f"phb_p{i}", [128, 512], F32)) for i in range(2)]

        @ctx.block.sync
        def _(q):
            bC.guard_tag(q, lambda: q.dma_start(out=bia[:], in_=bqkv))
            bC.guard_tag(q, lambda: q.dma_start(out=cstC[:], in_=cs_tab[0:32, :]))
            bC.guard_tag(q, lambda: q.dma_start(out=cstS[:], in_=cs_tab[32:64, :]))
            for tg in range(NTG):
                for half in range(2):
                    ts = 2 * tg + half
                    if tg >= 1:
                        q.wait_ge(bM.h, bM.at(2 * NC * tg))
                    if NODMAT:
                        for j in range(3):
                            bX.guard_tag(q, lambda: q.dma_start(out=xf[half][:, 4 * j:4 * j + 4, :],
                                               in_=xn0_tm[128 * j:128 * (j + 1), :]))
                            bX.n -= 16
                        bX.guard_tag(q, lambda: q.dma_start(out=xf[half][:, 12:16, :],
                                           in_=xn0_tm[384:512, :]))
                    else:
                        bX.guard_tag(q, lambda: q.dma_start_transpose(
                            out=xf[half][:], in_=xn0_tm[512 * ts:512 * (ts + 1), :]))
        bX.expect(2 * NTG)

        @ctx.block.gpsimd
        def _(g):
            def cast(si):
                tg, cc = steps[si]
                if si >= 2:
                    g.wait_ge(bM.h, bM.at(2 * si - 2))
                if NOW:
                    bW.issue(g, lambda: g.dma_start(out=wst[si % 2][:, 0, :], in_=wqkvT[0:128, 0:128]))
                else:
                    bW.issue(g, lambda: g.dma_start(
                        out=wst[si % 2][:],
                        in_=wqkvT[:, 128 * cc:128 * (cc + 1)].rearrange("(c p) f -> p c f", p=128)))

            cast(0)
            cast(1)
            for si in range(NS):
                tg, cc = steps[si]
                if si + 2 < NS:
                    cast(si + 2)
                for half in range(2):
                    ts = 2 * tg + half
                    g.wait_ge(bR.h, bR.at(2 * si + half + 1))
                    sg = stg[(2 * si + half) % 4]
                    kind, head = cc % 3, cc // 3
                    dst = (qf, kf, vf)[kind][head, :, 512 * ts:512 * (ts + 1)]
                    bS.issue(g, lambda: g.dma_start(out=dst, in_=sg[:]))
        assert bW.issued == NS
        assert bS.issued == 2 * NS

        @ctx.block.tensor
        def _(t):
            t.wait_ge(bC.h, bC.at(3))
            for si, (tg, cc) in enumerate(steps):
                bW.wait(t, si)
                if cc == 0:
                    t.wait_ge(bX.h, bX.at(2 * (tg + 1)))
                for half in range(2):
                    step2 = 2 * si + half
                    if step2 >= 2:
                        t.wait_ge(bE.h, bE.at(step2 - 1))
                    w = wst[si % 2]
                    for dc in range(15):
                        t.matmul(out=pp[half][:], lhsT=w[:, dc, :], rhs=xf[half][:, dc, :],
                                 start=(dc == 0), stop=False)
                    bM.tag(t.matmul(out=pp[half][:], lhsT=w[:, 15, :],
                                    rhs=xf[half][:, 15, :], start=False, stop=True))
        bM.expect(2 * NS)

        @ctx.block.scalar
        def _(s):
            for si, (tg, cc) in enumerate(steps):
                for half in range(2):
                    step2 = 2 * si + half
                    s.wait_ge(bM.h, bM.at(step2 + 1))
                    if step2 >= 4:
                        bS.wait(s, step2 - 4)
                    sg = stg[step2 % 4]
                    bE.tag(s.activation(out=sg[:], in_=pp[half][:], func=AF.Identity,
                                        bias=bia[:, cc:cc + 1]))
        bE.expect(2 * NS)

        @ctx.block.vector
        def _(v):
            for si, (tg, cc) in enumerate(steps):
                for half in range(2):
                    ts = 2 * tg + half
                    step2 = 2 * si + half
                    v.wait_ge(bE.h, bE.at(step2 + 1))
                    if cc % 3 != 2 and not NOROPE:
                        sg = stg[step2 % 4]
                        tok = slice(512 * ts, 512 * (ts + 1))
                        bR.use(v)
                        xV.tag(v.stream_shuffle(out=rt1[:], in_=sg[0:32, :], mask=SHUF_ROT16))
                        xV.use(v)
                        xV.tag(v.tensor_tensor(out=rt2[:], in0=rt1[:], in1=cstS[:, tok], op=ALU.mult))
                        xV.use(v)
                        xV.tag(v.tensor_tensor(out=rt1[:], in0=sg[0:32, :], in1=cstC[:, tok], op=ALU.mult))
                        xV.use(v)
                        bR.tag(v.tensor_tensor(out=sg[0:32, :], in0=rt1[:], in1=rt2[:], op=ALU.add))
                    else:
                        bR.use(v)
                        bR.tag(v.nop())
        bR.expect(2 * NS)


def ph_c(ctx, qf, kf, vf, maskq, of):
    nc = ctx.nc
    NQT = 8
    cKQ = DRing(ctx, "cKQ", 4)
    cVT = DRing(ctx, "cVT", 2)
    cO = DRing(ctx, "cO", 4, cls="sw")
    cS = ctx.sem("cS")
    cAv = ctx.sem("cAv")
    cL = ctx.sem("cL")
    cE = ctx.sem("cE")
    cK = ctx.sem("cK")
    cB = ctx.sem("cB")
    cC = ctx.sem("cC")
    cR = ctx.sem("cR")
    cV = ctx.sem("cV")
    cI = ctx.sem("cI")

    kvidx = {}
    n = 0
    for h in range(16):
        for qt in range(NQT):
            for kv in range(4 * (qt + 1)):
                kvidx[(h, qt, kv)] = n
                n += 1
    NKV = n
    gq = lambda h, qt: h * NQT + qt
    dgi = {}
    n = 0
    for h in range(16):
        for qt in range(NQT):
            for i in range(4):
                dgi[(h, qt, i)] = n
                n += 1

    with contextlib.ExitStack() as es:
        kb = [es.enter_context(nc.sbuf_tensor(f"phc_k{i}", [128, L], BF16)) for i in range(2)]
        qb = [es.enter_context(nc.sbuf_tensor(f"phc_q{i}", [128, L], BF16)) for i in range(2)]
        vb = [es.enter_context(nc.sbuf_tensor(f"phc_v{i}", [128, 32, 128], BF16)) for i in range(2)]
        esb = [es.enter_context(nc.sbuf_tensor(f"phc_e{i}", [128, 512], BF16)) for i in range(2)]
        msk = es.enter_context(nc.sbuf_tensor("phc_m", [128, 4, 512], BF16))
        onec = es.enter_context(nc.sbuf_tensor("phc_1c", [128, 1], BF16))
        oner = es.enter_context(nc.sbuf_tensor("phc_1r", [1, 128], BF16))
        lsb = [es.enter_context(nc.sbuf_tensor(f"phc_l{i}", [1, 512], BF16)) for i in range(2)]
        rbc = [es.enter_context(nc.sbuf_tensor(f"phc_r{i}", [128, 512], F32)) for i in range(2)]
        osb = [es.enter_context(nc.sbuf_tensor(f"phc_o{i}", [128, 512], BF16)) for i in range(2)]
        psS = [es.enter_context(nc.psum_tensor(f"phc_pS{i}", [128, 512], F32)) for i in range(2)]
        psO = [es.enter_context(nc.psum_tensor(f"phc_pO{i}", [128, 512], F32)) for i in range(2)]
        psL = [es.enter_context(nc.psum_tensor(f"phc_pL{i}", [1, 512], F32)) for i in range(2)]
        psB = [es.enter_context(nc.psum_tensor(f"phc_pB{i}", [128, 512], F32)) for i in range(2)]

        @ctx.block.vector
        def _(v):
            cI.tag(v.memset(onec[:], 1.0))
            cI.tag(v.memset(oner[:], 1.0))

        @ctx.block.sync
        def _(q):
            cKQ.issue(q, lambda: q.dma_start(out=msk[:, :, :],
                                in_=maskq.rearrange("i p f -> p i f")))
            for h in range(16):
                if h >= 2:
                    q.wait_ge(cV.h, cV.at(gq(h - 2, NQT - 1) + 1))
                cKQ.issue(q, lambda: q.dma_start(out=kb[h % 2][:], in_=kf[h]))
                cKQ.issue(q, lambda: q.dma_start(out=qb[h % 2][:], in_=qf[h]))
                cVT.issue(q, lambda: q.dma_start_transpose(out=vb[h % 2][:], in_=vf[h]))

        @ctx.block.tensor
        def _(t):
            t.wait_ge(cI.h, cI.at(2))
            for h in range(16):
                cKQ.wait(t, 1 + 2 * h)
                cKQ.wait(t, 2 + 2 * h)
                cVT.wait(t, h)
                for qt in range(NQT):
                    nkv = 4 * (qt + 1)
                    g = gq(h, qt)

                    def emit_av(kv):
                        gi = kvidx[(h, qt, kv)]
                        t.wait_ge(cE.h, cE.at(gi + 1))
                        if kv >= 4 * qt:
                            t.wait_ge(cK.h, cK.at(dgi[(h, qt, kv - 4 * qt)] + 1))
                        if kv == 0 and g >= 2:
                            t.wait_ge(cV.h, cV.at(g - 2 + 1))
                            t.wait_ge(cB.h, cB.at(g - 2 + 1))
                        cAv.tag(t.matmul(out=psO[g % 2][:], lhsT=vb[h % 2][:, kv, :],
                                         rhs=esb[gi % 2][:], start=(kv == 0),
                                         stop=(kv == nkv - 1)))
                        cL.tag(t.matmul(out=psL[g % 2][:], lhsT=onec[:],
                                        rhs=esb[gi % 2][:], start=(kv == 0),
                                        stop=(kv == nkv - 1)))

                    for kv in range(nkv):
                        gi = kvidx[(h, qt, kv)]
                        if gi >= 2:
                            t.wait_ge(cE.h, cE.at(gi - 1))
                        cS.tag(t.matmul(out=psS[gi % 2][:], lhsT=kb[h % 2][:, 128 * kv:128 * (kv + 1)],
                                        rhs=qb[h % 2][:, 512 * qt:512 * (qt + 1)],
                                        start=True, stop=True))
                        if kv >= 1:
                            emit_av(kv - 1)
                    emit_av(nkv - 1)
                    t.wait_ge(cB.h, cB.at(g + 1))
                    if g >= 2:
                        t.wait_ge(cR.h, cR.at(g - 2 + 1))
                    cC.tag(t.matmul(out=psB[g % 2][:], lhsT=oner[:], rhs=lsb[g % 2][:],
                                    start=True, stop=True))

        @ctx.block.scalar
        def _(s):
            for h in range(16):
                for qt in range(NQT):
                    nkv = 4 * (qt + 1)
                    g = gq(h, qt)
                    for kv in range(nkv):
                        gi = kvidx[(h, qt, kv)]
                        s.wait_ge(cS.h, cS.at(gi + 1))
                        if gi >= 2:
                            s.wait_ge(cL.h, cL.at(gi - 1))
                        cE.tag(s.activation(out=esb[gi % 2][:], in_=psS[gi % 2][:],
                                            func=AF.Exp, scale=INVSQ, bias=ctx.cc[-SOFT_C]))
                    s.wait_ge(cL.h, cL.at(kvidx[(h, qt, nkv - 1)] + 1))
                    cB.tag(s.copy(out=lsb[g % 2][:], in_=psL[g % 2][:]))

        @ctx.block.vector
        def _(v):
            cKQ.wait(v, 0)
            for h in range(16):
                for qt in range(NQT):
                    nkv = 4 * (qt + 1)
                    g = gq(h, qt)
                    for i in range(4):
                        kv = 4 * qt + i
                        gi = kvidx[(h, qt, kv)]
                        v.wait_ge(cE.h, cE.at(gi + 1))
                        cK.tag(v.tensor_tensor(out=esb[gi % 2][:], in0=esb[gi % 2][:],
                                               in1=msk[:, i, :], op=ALU.mult))
                    v.wait_ge(cC.h, cC.at(g + 1))
                    cR.tag(v.reciprocal(out=rbc[g % 2][:], in_=psB[g % 2][:]))
                    v.wait_ge(cR.h, cR.at(g + 1))
                    if g >= 2:
                        cO.wait(v, g - 2)
                    cV.tag(v.tensor_tensor(out=osb[g % 2][:], in0=psO[g % 2][:],
                                           in1=rbc[g % 2][:], op=ALU.mult))

        @ctx.block.gpsimd
        def _(g_):
            for h in range(16):
                for qt in range(NQT):
                    g = gq(h, qt)
                    g_.wait_ge(cV.h, cV.at(g + 1))
                    cO.issue(g_, lambda: g_.dma_start(out=of[h, :, 512 * qt:512 * (qt + 1)],
                                        in_=osb[g % 2][:]))


def ph_d(ctx, of, woutT, bout, s0f):
    nc = ctx.nc
    NQT = 8
    dW = DRing(ctx, "dW", 2, cls="sw")
    dO = DRing(ctx, "dO", 4)
    dM = ctx.sem("dM")
    dE = ctx.sem("dE")
    dS = DRing(ctx, "dS", 4, cls="sw")
    dB = ctx.sem("dB", 16)

    with contextlib.ExitStack() as es:
        ob = [es.enter_context(nc.sbuf_tensor(f"phd_o{i}", [128, 16, 512], BF16)) for i in range(2)]
        wst = [es.enter_context(nc.sbuf_tensor(f"phd_w{i}", [128, 16, 128], BF16)) for i in range(2)]
        bia = es.enter_context(nc.sbuf_tensor("phd_b", [128, 16], F32))
        stg = [es.enter_context(nc.sbuf_tensor(f"phd_s{i}", [128, 512], BF16)) for i in range(4)]
        pp = [es.enter_context(nc.psum_tensor(f"phd_p{i}", [128, 512], F32)) for i in range(2)]

        steps = [(qt, dc) for qt in range(NQT) for dc in range(16)]

        @ctx.block.sync
        def _(q):
            dB.tag(q.dma_start(out=bia[:], in_=bout))
            for qt in range(NQT):
                if qt >= 2:
                    q.wait_ge(dM.h, dM.at(16 * (qt - 1)))
                for h in range(16):
                    dO.issue(q, lambda: q.dma_start(out=ob[qt % 2][:, h, :],
                                       in_=of[h, :, 512 * qt:512 * (qt + 1)]))

        @ctx.block.gpsimd
        def _(g):
            def cast(si):
                qt, dc = steps[si]
                if si >= 2:
                    g.wait_ge(dM.h, dM.at(si - 1))
                dW.issue(g, lambda: g.dma_start(
                    out=wst[si % 2][:],
                    in_=woutT[:, 128 * dc:128 * (dc + 1)].rearrange("(c p) f -> p c f", p=128)))

            cast(0)
            cast(1)
            for si, (qt, dc) in enumerate(steps):
                if si + 2 < len(steps):
                    cast(si + 2)
                g.wait_ge(dE.h, dE.at(si + 1))
                dS.issue(g, lambda: g.dma_start(out=s0f[128 * dc:128 * (dc + 1), 512 * qt:512 * (qt + 1)],
                                   in_=stg[si % 4][:]))

        @ctx.block.tensor
        def _(t):
            for si, (qt, dc) in enumerate(steps):
                dW.wait(t, si)
                if dc == 0:
                    for h in range(16):
                        dO.wait(t, 16 * qt + h)
                if si >= 2:
                    t.wait_ge(dE.h, dE.at(si - 1))
                for h in range(15):
                    t.matmul(out=pp[si % 2][:], lhsT=wst[si % 2][:, h, :],
                             rhs=ob[qt % 2][:, h, :], start=(h == 0), stop=False)
                dM.tag(t.matmul(out=pp[si % 2][:], lhsT=wst[si % 2][:, 15, :],
                                rhs=ob[qt % 2][:, 15, :], start=False, stop=True))

        @ctx.block.scalar
        def _(s):
            s.wait_ge(dB.h, dB.at(1))
            for si, (qt, dc) in enumerate(steps):
                s.wait_ge(dM.h, dM.at(si + 1))
                if si >= 4:
                    dS.wait(s, si - 4)
                dE.tag(s.activation(out=stg[si % 4][:], in_=pp[si % 2][:],
                                    func=AF.Identity, bias=bia[:, dc:dc + 1]))


def ph_e0(ctx, wgTq, wuTq, wdTq, sgbc, subc, sdbc, wgq, wuq, wdq, bufs_es=None):
    nc = ctx.nc
    eW = DRing(ctx, "eW", 2, cls="sw")
    eS = DRing(ctx, "eS", 2, cls="sw")
    eC = ctx.sem("eC")
    eB = ctx.sem("eB", 16)
    jobs = ([("g", dc) for dc in range(16)] + [("u", dc) for dc in range(16)]
            + [("d", hc) for hc in range(44)])

    with contextlib.ExitStack() as _es:
        es = bufs_es if bufs_es is not None else _es
        wci = [es.enter_context(nc.sbuf_tensor(f"phe0_i{i}", [128, HID0], F32)) for i in range(2)]
        wco = [es.enter_context(nc.sbuf_tensor(f"phe0_o{i}", [128, HID0], BF16)) for i in range(2)]
        sg = es.enter_context(nc.sbuf_tensor("phe0_sg", [128, HID0], F32))
        su = es.enter_context(nc.sbuf_tensor("phe0_su", [128, HID0], F32))
        sd = es.enter_context(nc.sbuf_tensor("phe0_sd", [128, D], F32))

        @ctx.block.sync
        def _(q):
            eB.tag(q.dma_start(out=sg[:], in_=sgbc))
            eB.tag(q.dma_start(out=su[:], in_=subc))
            eB.tag(q.dma_start(out=sd[:], in_=sdbc))

        @ctx.block.gpsimd
        def _(g):
            def load(ji):
                kind, c = jobs[ji]
                if ji >= 2:
                    g.wait_ge(eC.h, eC.at(ji - 1))
                srct = {"g": wgTq, "u": wuTq, "d": wdTq}[kind]
                w = HID0 if kind != "d" else D
                eW.issue(g, lambda: g.dma_start(out=wci[ji % 2][:, 0:w],
                                                in_=srct[128 * c:128 * (c + 1), :]))

            load(0)
            load(1)
            for ji, (kind, c) in enumerate(jobs):
                if ji + 2 < len(jobs):
                    load(ji + 2)
                g.wait_ge(eC.h, eC.at(ji + 1))
                w = HID0 if kind != "d" else D
                dst = {"g": wgq, "u": wuq, "d": wdq}[kind]
                eS.issue(g, lambda: g.dma_start(out=dst[128 * c:128 * (c + 1), :],
                                                in_=wco[ji % 2][:, 0:w]))

        @ctx.block.vector
        def _(v):
            v.wait_ge(eB.h, eB.at(3))
            for ji, (kind, c) in enumerate(jobs):
                eW.wait(v, ji)
                if ji >= 2:
                    eS.wait(v, ji - 2)
                w = HID0 if kind != "d" else D
                sc = {"g": sg, "u": su, "d": sd}[kind]
                eC.tag(v.scalar_tensor_tensor(out=wco[ji % 2][:, 0:w], in0=wci[ji % 2][:, 0:w],
                                              scalar=-128.0, in1=sc[:, 0:w],
                                              op0=ALU.add, op1=ALU.mult))


def ph_e(ctx, xn0_tm, wgq, wuq, wdq, x0T, s0f, x1f, xn1f):
    nc = ctx.nc
    NHC = 44
    fX = ctx.sem("fX", 16)
    fWg = DRing(ctx, "fWg", 2)
    fWu = DRing(ctx, "fWu", 2)
    fD = DRing(ctx, "fD", 2)
    fR = DRing(ctx, "fR", 4)
    fG = ctx.sem("fG")
    fU = ctx.sem("fU")
    fP = ctx.sem("fP")
    fSi = ctx.sem("fSi")
    fM = ctx.sem("fM")
    fE = ctx.sem("fE")
    fQ = ctx.sem("fQ")
    fV = ctx.sem("fV")
    fC = ctx.sem("fC")
    fT = ctx.sem("fT")
    fF = ctx.sem("fF")
    fL = ctx.sem("fL")
    fB = ctx.sem("fB")
    fBe = ctx.sem("fBe")
    fA = ctx.sem("fA")
    fO1 = DRing(ctx, "fO1", 2, cls="sw")
    fO2 = DRing(ctx, "fO2", 2, cls="sw")
    eI = ctx.sem("eI")
    xV = ctx.sem("xVe")
    xS = ctx.sem("xSe")

    with contextlib.ExitStack() as es:
        xf = es.enter_context(nc.sbuf_tensor("phe_x", [128, 16, 512], BF16))
        wg = [es.enter_context(nc.sbuf_tensor(f"phe_wg{i}", [128, 16, 128], BF16)) for i in range(2)]
        wu = [es.enter_context(nc.sbuf_tensor(f"phe_wu{i}", [128, 16, 128], BF16)) for i in range(2)]
        wd = [es.enter_context(nc.sbuf_tensor(f"phe_wd{i}", [128, NHC, 128], BF16)) for i in range(2)]
        prod = es.enter_context(nc.sbuf_tensor("phe_pr", [128, NHC, 512], BF16))
        sgt = es.enter_context(nc.sbuf_tensor("phe_sg", [128, 512], BF16))
        g0t = es.enter_context(nc.sbuf_tensor("phe_g0", [128, 512], BF16))
        xsq = es.enter_context(nc.sbuf_tensor("phe_xq", [128, 512], BF16))
        x0c = [es.enter_context(nc.sbuf_tensor(f"phe_x0{i}", [128, 512], F32)) for i in range(2)]
        s0c = [es.enter_context(nc.sbuf_tensor(f"phe_s0{i}", [128, 512], BF16)) for i in range(2)]
        x1t = [es.enter_context(nc.sbuf_tensor(f"phe_x1{i}", [128, 512], F32)) for i in range(2)]
        x1b = es.enter_context(nc.sbuf_tensor("phe_x1b", [128, 16, 512], BF16))
        xnt = [es.enter_context(nc.sbuf_tensor(f"phe_xn{i}", [128, 512], BF16)) for i in range(2)]
        rows = es.enter_context(nc.sbuf_tensor("phe_rw", [1, 8, 512], F32))
        rowb = es.enter_context(nc.sbuf_tensor("phe_rb", [1, 2, 512], BF16))
        oner = es.enter_context(nc.sbuf_tensor("phe_1r", [1, 128], BF16))
        onec = es.enter_context(nc.sbuf_tensor("phe_1c", [128, 1], BF16))
        mbc = es.enter_context(nc.sbuf_tensor("phe_mb", [128, 512], F32))
        rbc = es.enter_context(nc.sbuf_tensor("phe_rb2", [128, 512], F32))
        psG = [es.enter_context(nc.psum_tensor(f"phe_pG{i}", [128, 512], F32)) for i in range(2)]
        psU = [es.enter_context(nc.psum_tensor(f"phe_pU{i}", [128, 512], F32)) for i in range(2)]
        psD = psG
        psM = es.enter_context(nc.psum_tensor("phe_pM", [1, 512], F32))
        psQ = es.enter_context(nc.psum_tensor("phe_pQ", [1, 512], F32))
        psB = es.enter_context(nc.psum_tensor("phe_pB", [128, 512], F32))

        @ctx.block.vector
        def _(v):
            eI.tag(v.memset(oner[:], 1.0))
            eI.tag(v.memset(onec[:], 1.0))

        NT8 = 8
        for tt in range(NT8):

            @ctx.block.sync
            def _(q, tt=tt):
                if tt >= 1:
                    q.wait_ge(fU.h, fU.at(NHC * tt))
                if fX.n:
                    q.wait_ge(fX.h, fX.n)
                fX.tag(q.dma_start_transpose(out=xf[:], in_=xn0_tm[512 * tt:512 * (tt + 1), :]))
                for hid in range(NHC):
                    j = tt * NHC + hid
                    if j >= 2:
                        q.wait_ge(fU.h, fU.at(j - 1))
                    fWg.issue(q, lambda: q.dma_start(
                        out=wg[j % 2][:],
                        in_=wgq[:, 128 * hid:128 * (hid + 1)].rearrange("(c p) f -> p c f", p=128)))
                    fWu.issue(q, lambda: q.dma_start(
                        out=wu[j % 2][:],
                        in_=wuq[:, 128 * hid:128 * (hid + 1)].rearrange("(c p) f -> p c f", p=128)))

            @ctx.block.tensor
            def _(t, tt=tt):
                t.wait_ge(eI.h, eI.at(2))
                t.wait_ge(fX.h, fX.at(tt + 1))
                for hid in range(NHC):
                    j = tt * NHC + hid
                    fWg.wait(t, j)
                    fWu.wait(t, j)
                    if j >= 2:
                        t.wait_ge(fSi.h, fSi.at(j - 1))
                        t.wait_ge(fP.h, fP.at(j - 1))
                    if hid == 0 and tt >= 1:
                        t.wait_ge(fM.h, fM.at(16 * tt))
                    for dc in range(15):
                        t.matmul(out=psG[j % 2][:], lhsT=wg[j % 2][:, dc, :], rhs=xf[:, dc, :],
                                 start=(dc == 0), stop=False)
                    fG.tag(t.matmul(out=psG[j % 2][:], lhsT=wg[j % 2][:, 15, :], rhs=xf[:, 15, :],
                                    start=False, stop=True))
                    for dc in range(15):
                        t.matmul(out=psU[j % 2][:], lhsT=wu[j % 2][:, dc, :], rhs=xf[:, dc, :],
                                 start=(dc == 0), stop=False)
                    fU.tag(t.matmul(out=psU[j % 2][:], lhsT=wu[j % 2][:, 15, :], rhs=xf[:, 15, :],
                                    start=False, stop=True))

            @ctx.block.scalar
            def _(s, tt=tt):
                for hid in range(NHC):
                    j = tt * NHC + hid
                    s.wait_ge(fG.h, fG.at(j + 1))
                    if j >= 1:
                        s.wait_ge(fP.h, fP.at(j))
                    fSi.tag(s.activation(out=sgt[:], in_=psG[j % 2][:], func=AF.Silu))

            @ctx.block.vector
            def _(v, tt=tt):
                for hid in range(NHC):
                    j = tt * NHC + hid
                    v.wait_ge(fSi.h, fSi.at(j + 1))
                    v.wait_ge(fU.h, fU.at(j + 1))
                    if hid == 0 and tt >= 1:
                        v.wait_ge(fM.h, fM.at(16 * tt))
                    fP.tag(v.tensor_tensor(out=prod[:, hid, :], in0=sgt[:], in1=psU[j % 2][:],
                                           op=ALU.mult))

            @ctx.block.sync
            def _(q, tt=tt):
                for dcol in range(16):
                    si = tt * 16 + dcol
                    if si >= 2:
                        q.wait_ge(fM.h, fM.at(si - 1))
                        q.wait_ge(fV.h, fV.at(2 * si - 2))
                    fD.issue(q, lambda: q.dma_start(
                        out=wd[si % 2][:],
                        in_=wdq[:, 128 * dcol:128 * (dcol + 1)].rearrange("(c p) f -> p c f", p=128)))
                    fR.issue(q, lambda: q.dma_start(
                        out=x0c[si % 2][:], in_=x0T[128 * dcol:128 * (dcol + 1),
                                                    512 * tt:512 * (tt + 1)]))
                    fR.issue(q, lambda: q.dma_start(
                        out=s0c[si % 2][:], in_=s0f[128 * dcol:128 * (dcol + 1),
                                                    512 * tt:512 * (tt + 1)]))

            @ctx.block.tensor
            def _(t, tt=tt):
                t.wait_ge(fP.h, fP.at(NHC * (tt + 1)))
                for dcol in range(16):
                    si = tt * 16 + dcol
                    fD.wait(t, si)
                    if si >= 2:
                        t.wait_ge(fE.h, fE.at(si - 1))
                    for hid in range(NHC - 1):
                        t.matmul(out=psD[si % 2][:], lhsT=wd[si % 2][:, hid, :],
                                 rhs=prod[:, hid, :], start=(hid == 0), stop=False)
                    fM.tag(t.matmul(out=psD[si % 2][:], lhsT=wd[si % 2][:, NHC - 1, :],
                                    rhs=prod[:, NHC - 1, :], start=False, stop=True))
                    t.wait_ge(fC.h, fC.at(si + 1))
                    if dcol == 0 and tt >= 1:
                        t.wait_ge(fF.h, fF.at(tt))
                    fT.tag(t.matmul(out=psM[:], lhsT=onec[:], rhs=x1b[:, dcol, :],
                                    start=(dcol == 0), stop=(dcol == 15)))
                    fT.tag(t.matmul(out=psQ[:], lhsT=onec[:], rhs=xsq[:],
                                    start=(dcol == 0), stop=(dcol == 15)))

            @ctx.block.scalar
            def _(s, tt=tt):
                for dcol in range(16):
                    si = tt * 16 + dcol
                    s.wait_ge(fM.h, fM.at(si + 1))
                    if si >= 1:
                        s.wait_ge(fV.h, fV.at(2 * si - 1))
                    fE.tag(s.activation(out=g0t[:], in_=psD[si % 2][:], func=AF.Silu))
                    s.wait_ge(fV.h, fV.at(2 * si + 2))
                    if dcol == 0 and tt >= 1:
                        s.wait_ge(fA.h, fA.at(16 * tt))
                    xS.tag(s.copy(out=x1b[:, dcol, :], in_=x1t[si % 2][:]))
                    xS.use(s)
                    if dcol >= 1:
                        s.wait_ge(fT.h, fT.at(2 * si))
                    fC.tag(s.activation(out=xsq[:], in_=x1t[si % 2][:], func=AF.Square))

            @ctx.block.vector
            def _(v, tt=tt):
                for dcol in range(16):
                    si = tt * 16 + dcol
                    v.wait_ge(fE.h, fE.at(si + 1))
                    fR.wait(v, 2 * si)
                    fR.wait(v, 2 * si + 1)
                    if si >= 2:
                        fO1.wait(v, si - 2)
                        v.wait_ge(fC.h, fC.at(si - 1))
                    fV.tag(v.tensor_tensor(out=x1t[si % 2][:], in0=s0c[si % 2][:],
                                           in1=g0t[:], op=ALU.mult))
                    v.wait_ge(fV.h, fV.n)
                    fV.tag(v.tensor_tensor(out=x1t[si % 2][:], in0=x1t[si % 2][:],
                                           in1=x0c[si % 2][:], op=ALU.add))

            @ctx.block.gpsimd
            def _(g, tt=tt):
                for dcol in range(16):
                    si = tt * 16 + dcol
                    g.wait_ge(fV.h, fV.at(2 * si + 2))
                    fO1.issue(g, lambda: g.dma_start(
                        out=x1f[128 * dcol:128 * (dcol + 1), 512 * tt:512 * (tt + 1)],
                        in_=x1t[si % 2][:]))

            @ctx.block.vector
            def _(v, tt=tt):
                v.wait_ge(fT.h, fT.at(32 * (tt + 1)))
                r = rows
                xV.tag(v.tensor_scalar(out=r[:, 0, :], in0=psM[:], scalar1=1.0 / D,
                                       scalar2=None, op0=ALU.mult))
                xV.use(v)
                xV.tag(v.tensor_tensor(out=r[:, 1, :], in0=r[:, 0, :], in1=r[:, 0, :],
                                       op=ALU.mult))
                xV.use(v)
                xV.tag(v.tensor_scalar(out=r[:, 2, :], in0=psQ[:],
                                       scalar1=1.0 / D, scalar2=None, op0=ALU.mult))
                xV.use(v)
                xV.tag(v.tensor_tensor(out=r[:, 2, :], in0=r[:, 2, :], in1=r[:, 1, :],
                                       op=ALU.subtract))
                xV.use(v)
                fF.tag(v.tensor_copy(out=rowb[:, 0, :], in_=r[:, 0, :]))

            @ctx.block.scalar
            def _(s, tt=tt):
                s.wait_ge(fF.h, fF.at(tt + 1))
                xS.tag(s.activation(out=rows[:, 3, :], in_=rows[:, 2, :], func=AF.Ln,
                                    bias=ctx.ccr[1e-5]))
                xS.use(s)
                fL.tag(s.activation(out=rowb[:, 1, :], in_=rows[:, 3, :], func=AF.Exp,
                                    scale=-0.5))

            @ctx.block.tensor
            def _(t, tt=tt):
                t.wait_ge(fL.h, fL.at(tt + 1))
                if tt >= 1:
                    t.wait_ge(fBe.h, fBe.at(2 * tt))
                fB.tag(t.matmul(out=psB[:], lhsT=oner[:], rhs=rowb[:, 0, :],
                                start=True, stop=True))
                t.wait_ge(fBe.h, fBe.at(2 * tt + 1))
                fB.tag(t.matmul(out=psB[:], lhsT=oner[:], rhs=rowb[:, 1, :],
                                start=True, stop=True))

            @ctx.block.scalar
            def _(s, tt=tt):
                s.wait_ge(fB.h, fB.at(2 * tt + 1))
                if tt >= 1:
                    s.wait_ge(fA.h, fA.at(16 * tt))
                fBe.tag(s.copy(out=mbc[:], in_=psB[:]))
                s.wait_ge(fB.h, fB.at(2 * (tt + 1)))
                fBe.tag(s.copy(out=rbc[:], in_=psB[:]))

            @ctx.block.vector
            def _(v, tt=tt):
                v.wait_ge(fBe.h, fBe.at(2 * (tt + 1)))
                for dcol in range(16):
                    si = tt * 16 + dcol
                    if si >= 2:
                        fO2.wait(v, si - 2)
                    xV.tag(v.tensor_tensor(out=xnt[si % 2][:], in0=x1b[:, dcol, :],
                                           in1=mbc[:], op=ALU.subtract))
                    xV.use(v)
                    fA.tag(v.tensor_tensor(out=xnt[si % 2][:], in0=xnt[si % 2][:],
                                           in1=rbc[:], op=ALU.mult))

            @ctx.block.gpsimd
            def _(g, tt=tt):
                for dcol in range(16):
                    si = tt * 16 + dcol
                    g.wait_ge(fA.h, fA.at(si + 1))
                    fO2.issue(g, lambda: g.dma_start(
                        out=xn1f[128 * dcol:128 * (dcol + 1), 512 * tt:512 * (tt + 1)],
                        in_=xnt[si % 2][:]))


def ph_f(ctx, xn1x_pre, xn1x_own, wf, cst, gatedT, bufs_es=None):
    import os as _os
    FST = int(_os.environ.get("FST", "5"))
    FCHK = int(_os.environ.get("FCHK", "9"))
    nc = ctx.nc
    NG = 8
    ev = {}

    with contextlib.ExitStack() as _es:
        es = bufs_es if bufs_es is not None else _es
        sbl = lambda n, s, d: es.enter_context(nc.sbuf_tensor(n, s, d))
        psl = lambda n, s, d=F32: es.enter_context(nc.psum_tensor(n, s, d))
        xft = sbl("phf_x", [128, 16, 513], BF16)
        dl = sbl("phf_dl", [128, 16, 512], BF16)
        xi = sbl("phf_xi", [128, 16, 512], BF16)
        zt = sbl("phf_zt", [128, 512], BF16)
        zt2 = sbl("phf_zt2", [32, 512], BF16)
        wt1 = sbl("phf_wt1", [64, 512], BF16)
        wx1s = sbl("phf_wx1", [128, 16, 160], BF16)
        wx2s = sbl("phf_wx2", [128, 2048], BF16)
        wx2s2 = sbl("phf_wx2b", [32, 2048], BF16)
        ww2s = sbl("phf_ww2", [64, 1024], BF16)
        wpj = [sbl(f"phf_wp{i}", [128, 16, 128], BF16) for i in range(2)]
        ww1b = sbl("phf_w1b", [128, 16, 64], BF16)
        wmv = [sbl(f"phf_wm{i}", [128, 16, 256], BF16) for i in range(2)]
        mucol = sbl("phf_mu", [128, 16], F32)
        bwsb = sbl("phf_bw", [128, 8], F32)
        bonsb = sbl("phf_bo", [128, 8, 4], BF16)
        gnwb = sbl("phf_gw", [128, 256], BF16)
        gnbb = sbl("phf_gb", [128, 256], BF16)
        pmaskb = sbl("phf_pm", [128, 4, 32], BF16)
        idn = sbl("phf_id", [128, 128], BF16)
        one1 = sbl("phf_11", [1, 128], BF16)
        pflag = sbl("phf_fl", [128, 1], F32)
        ewt = sbl("phf_ew", [128, 8, 512], BF16)
        rkm = sbl("phf_rk", [128, 8, 512], BF16)
        kkm = sbl("phf_kk", [128, 8, 512], BF16)
        abuf = sbl("phf_ab", [128, 8, 4, 33], F32)
        ecol = sbl("phf_ec", [128, 8, 16], F32)
        ecr = sbl("phf_ecr", [128, 4], F32)
        ecr2 = sbl("phf_ecr2", [128, 4], F32)
        emc = sbl("phf_emc", [128, 8, 16], F32)
        aarg = sbl("phf_as", [128, 3, 4, 32], F32)
        exf = sbl("phf_exf", [128, 3, 128], BF16)
        ktf = sbl("phf_kt", [128, 128], BF16)
        ktp_z = sbl("phf_tpz", [128, 4, 128], BF16)
        ktf_z = sbl("phf_ktz", [128, 4, 128], BF16)
        mskc = sbl("phf_msk", [128, 4], F32)
        rbk = sbl("phf_rbk", [128, 128], BF16)
        rstash = sbl("phf_rs", [128, 8, 4, 128], BF16)
        pmsb = sbl("phf_psb", [128, 8, 4, 4, 32], BF16)
        pmbd = sbl("phf_pbd", [128, 4, 128], BF16)
        dstash = sbl("phf_ds", [128, 8, 4, 4], F32)
        ubuf = sbl("phf_ub", [128, 8, 16, 64], BF16)
        straj = sbl("phf_st", [128, 17, 64], F32)
        sbf_bd = sbl("phf_sbf", [128, 4, 256], BF16)
        scar = sbl("phf_sc", [128, 8, 64], F32)
        vpm = sbl("phf_vp", [128, 4, 8, 256], BF16)
        gpm = sbl("phf_gp", [128, 4, 8, 256], BF16)
        osb = sbl("phf_os", [128, 2, 256], F32)
        sqf = sbl("phf_sq", [128, 256], F32)
        onb = sbl("phf_on", [128, 256], BF16)
        sgl = sbl("phf_sl", [128, 256], BF16)
        t1g = sbl("phf_g1", [128, 256], BF16)
        gtin = sbl("phf_gi", [128, 256], BF16)
        gst = sbl("phf_gst", [128, 32], F32)
        gtb = sbl("phf_gt", [128, 2, 128], BF16)
        pA = [psl(f"phf_pA{i}", [128, 512]) for i in range(2)]
        pT = psl("phf_pT", [128, 512])
        pTb = psl("phf_pTb", [128, 3, 128], BF16)
        pU = psl("phf_pU", [128, 512])
        pI = psl("phf_pI", [128, 512])
        pN = [psl(f"phf_pN{i}", [128, 512]) for i in range(2)]

        gL = DRing(ctx, "gL", 4)
        gW = DRing(ctx, "gW", 4, cls="sw")
        gO = DRing(ctx, "gO", 4, cls="sw")
        gC = ctx.sem("gC", 16)
        gC2 = ctx.sem("gC2", 16, cls="sw")
        gT = ctx.sem("gT")
        gS = ctx.sem("gS")
        gV = ctx.sem("gV")

        @ctx.block.sync
        def _(q):
            for name, dst in (("mucol", mucol), ("bwsb", bwsb),
                              ("idn", idn), ("pflag", pflag)):
                gC.tag(q.dma_start(out=dst[:], in_=cst[name]))
            gC.tag(q.dma_start(out=pmaskb[:],
                               in_=cst["pmask"].rearrange("p (h s) -> p h s", h=4)))

        @ctx.block.gpsimd
        def _(g):
            for name, dst in (("gnwb", gnwb), ("gnbb", gnbb),
                              ("wx2s", wx2s), ("wx2s2", wx2s2), ("ww2s", ww2s)):
                gC2.tag(g.dma_start(out=dst[:], in_=cst[name]))
            gC2.tag(g.dma_start(out=bonsb[:],
                                in_=cst["bonus"].rearrange("p (g j) -> p g j", g=8)))

        @ctx.block.vector
        def _(v):
            gV.tag(v.memset(one1[:], 1.0))
            gV.tag(v.memset(abuf[:], 0.0))
            gV.tag(v.memset(scar[:], 0.0))
            gV.tag(v.memset(ktf_z[:], 0.0))
            gV.tag(v.memset(sbf_bd[:], 0.0))
            gV.tag(v.memset(pmbd[:], 0.0))
            gV.tag(v.memset(mskc[:], 0.0))
            for cc in range(4):
                v.wait_ge(gV.h, gV.n)
                gV.tag(v.memset(mskc[32 * cc:32 * (cc + 1), cc:cc + 1], 1.0))

        def wait_consts(eng):
            eng.wait_ge(gC.h, gC.at(5))
            eng.wait_ge(gC2.h, gC2.at(6))
            eng.wait_ge(gV.h, gV.at(11))

        units = [("pre", u) for u in range(4)] + [("own", u) for u in range(4)]

        for ui, (kind, u) in enumerate(units):
            pre = kind == "pre"
            xn1x = xn1x_pre if pre else xn1x_own
            uend_t = ev.get(("uend_t", ui - 1), 0)
            uend_v = ev.get(("uend_v", ui - 1), 0)
            uend_s = ev.get(("uend_s", ui - 1), 0)

            @ctx.block.sync
            def _(q, ui=ui, u=u, xn1x=xn1x, uend_t=uend_t, uend_v=uend_v):
                if ui == 0:
                    wait_consts(q)
                q.wait_ge(gT.h, uend_t)
                q.wait_ge(gV.h, uend_v)
                for dc in range(16):
                    gL.issue(q, lambda: q.dma_start(
                        out=xft[:, dc, :],
                        in_=xn1x[128 * dc:128 * (dc + 1), 512 * u:512 * u + 513]))

            @ctx.block.vector
            def _(v, ui=ui):
                wait_consts(v)
                for dc in range(16):
                    gL.wait(v, 16 * ui + dc)
                    gV.tag(v.tensor_tensor(out=dl[:, dc, :], in0=xft[:, dc, 0:512],
                                           in1=xft[:, dc, 1:513], op=ALU.subtract))
                    v.wait_ge(gV.h, gV.n)
                    gV.tag(v.scalar_tensor_tensor(out=xi[:, dc, :], in0=dl[:, dc, :],
                                                  scalar=mucol[:, dc:dc + 1],
                                                  in1=xft[:, dc, 1:513],
                                                  op0=ALU.mult, op1=ALU.add))
            ev[("zin_v", ui)] = gV.n

            @ctx.block.gpsimd
            def _(g, ui=ui, uend_t=uend_t):
                g.wait_ge(gT.h, uend_t)
                gW.issue(g, lambda: g.dma_start(
                    out=wx1s[:], in_=wf["wx1"].rearrange("(c p) f -> p c f", p=128)))
            wx1i = gW.issued - 1

            @ctx.block.tensor
            def _(t, ui=ui, wx1i=wx1i):
                if ui == 0:
                    wait_consts(t)
                gW.wait(t, wx1i)
                t.wait_ge(gV.h, ev[("zin_v", ui)])
                t.wait_ge(gS.h, ev.get(("tanh_s", ui - 1), 0))
                for dc in range(15):
                    t.matmul(out=pA[0][:, 0:512], lhsT=wx1s[:, dc, 0:128], rhs=xi[:, dc, :],
                             start=(dc == 0), stop=False)
                gT.tag(t.matmul(out=pA[0][:, 0:512], lhsT=wx1s[:, 15, 0:128], rhs=xi[:, 15, :],
                                start=False, stop=True))
                for dc in range(15):
                    t.matmul(out=pA[1][0:32, 0:512], lhsT=wx1s[:, dc, 128:160], rhs=xi[:, dc, :],
                             start=(dc == 0), stop=False)
                gT.tag(t.matmul(out=pA[1][0:32, 0:512], lhsT=wx1s[:, 15, 128:160], rhs=xi[:, 15, :],
                                start=False, stop=True))
            ev[("z_t", ui)] = gT.n

            @ctx.block.scalar
            def _(s, ui=ui):
                s.wait_ge(gT.h, ev[("z_t", ui)])
                gS.tag(s.activation(out=zt[:], in_=pA[0][:, 0:512], func=AF.Tanh))
                gS.tag(s.activation(out=zt2[:], in_=pA[1][0:32, 0:512], func=AF.Tanh))
            ev[("tanh_s", ui)] = gS.n

            specs = [(i, lab) for i, lab in ((0, "r"), (1, "w"), (2, "k"), (3, "v"), (4, "g"))
                     if not (pre and lab in ("r", "g"))]
            prev_proj_t = ev[("z_t", ui)]
            for sidx, (i, lab) in enumerate(specs):
                vbase = ctx.sem("gV").n
                prev_spec_s = ctx.sem("gS").n

                @ctx.block.tensor
                def _(t, i=i, ui=ui, prev_proj_t=prev_proj_t, vbase=vbase,
                      prev_spec_s=prev_spec_s):
                    t.wait_ge(gS.h, ev[("tanh_s", ui)])
                    t.wait_ge(gS.h, prev_spec_s)
                    t.wait_ge(gT.h, prev_proj_t)
                    for dc in range(16):
                        if dc >= 2:
                            t.wait_ge(gV.h, vbase + 2 * (dc - 2) + 2)
                        lhsT = (wx2s[32 * i:32 * (i + 1), 128 * dc:128 * (dc + 1)] if i < 4
                                else wx2s2[:, 128 * dc:128 * (dc + 1)])
                        rhs = zt[32 * i:32 * (i + 1), :] if i < 4 else zt2[:, :]
                        ev[("mus_t", ui, i, dc)] = gT.n + 1
                        gT.tag(t.matmul(out=pA[dc % 2][:, 0:512], lhsT=lhsT, rhs=rhs,
                                        start=True, stop=True,
                                        tile_position=(32 * i, 0) if i < 4 else None))

                @ctx.block.vector
                def _(v, i=i, ui=ui, sidx=sidx):
                    for dc in range(16):
                        v.wait_ge(gT.h, ev[("mus_t", ui, i, dc)])
                        v.wait_ge(gT.h, ev.get(("proj_t", ui, sidx - 1), 0))
                        gV.tag(v.tensor_tensor(out=xi[:, dc, :], in0=pA[dc % 2][:, 0:512],
                                               in1=dl[:, dc, :], op=ALU.mult))
                        v.wait_ge(gV.h, gV.n)
                        gV.tag(v.tensor_tensor(out=xi[:, dc, :], in0=xi[:, dc, :],
                                               in1=xft[:, dc, 1:513], op=ALU.add))
                ev[("xiall_v", ui, i)] = gV.n

                if lab in ("r", "k"):
                    dst = rkm if lab == "r" else kkm
                    for g8 in range(NG):
                        @ctx.block.gpsimd
                        def _(g, lab=lab, g8=g8, ui=ui):
                            g.wait_ge(gT.h, ev.get(("wuse_t", gW.issued - 2), 0))
                            wsl = wf["wr" if lab == "r" else "wk"]
                            gW.issue(g, lambda: g.dma_start(
                                out=wpj[g8 % 2][:],
                                in_=wsl[:, 128 * g8:128 * (g8 + 1)].rearrange(
                                    "(c p) f -> p c f", p=128)))
                        wi = gW.issued - 1

                        @ctx.block.tensor
                        def _(t, g8=g8, wi=wi, ui=ui, i=i):
                            gW.wait(t, wi)
                            t.wait_ge(gV.h, ev[("xiall_v", ui, i)])
                            if g8 >= 2:
                                t.wait_ge(gS.h, ev[("pjev_s", ui, i, g8 - 2)])
                            ev[("wuse_t", wi)] = gT.n + 1
                            for dc in range(15):
                                t.matmul(out=pA[g8 % 2][:, 0:512], lhsT=wpj[g8 % 2][:, dc, :],
                                         rhs=xi[:, dc, :], start=(dc == 0), stop=False)
                            gT.tag(t.matmul(out=pA[g8 % 2][:, 0:512], lhsT=wpj[g8 % 2][:, 15, :],
                                            rhs=xi[:, 15, :], start=False, stop=True))
                        ev[("pj_t", ui, i, g8)] = gT.n

                        @ctx.block.scalar
                        def _(s, g8=g8, dst=dst, ui=ui, i=i):
                            s.wait_ge(gT.h, ev[("pj_t", ui, i, g8)])
                            ev[("pjev_s", ui, i, g8)] = gS.n + 1
                            gS.tag(s.copy(out=dst[:, g8, :], in_=pA[g8 % 2][:, 0:512]))

                elif lab == "w":
                    @ctx.block.gpsimd
                    def _(g, ui=ui):
                        g.wait_ge(gT.h, ev.get(("w1_t", ui - 1), 0))
                        gW.issue(g, lambda: g.dma_start(
                            out=ww1b[:],
                            in_=wf["ww1"].rearrange("(c p) f -> p c f", p=128)))
                    wi = gW.issued - 1

                    @ctx.block.tensor
                    def _(t, wi=wi, ui=ui, i=i):
                        gW.wait(t, wi)
                        t.wait_ge(gV.h, ev[("xiall_v", ui, i)])
                        ev[("wuse_t", wi)] = gT.n + 1
                        for dc in range(15):
                            t.matmul(out=pA[0][0:64, 0:512], lhsT=ww1b[:, dc, :],
                                     rhs=xi[:, dc, :], start=(dc == 0), stop=False)
                        gT.tag(t.matmul(out=pA[0][0:64, 0:512], lhsT=ww1b[:, 15, :],
                                        rhs=xi[:, 15, :], start=False, stop=True))
                    ev[("w1_t", ui)] = gT.n

                    @ctx.block.scalar
                    def _(s, ui=ui):
                        s.wait_ge(gT.h, ev[("w1_t", ui)])
                        gS.tag(s.activation(out=wt1[:], in_=pA[0][0:64, 0:512], func=AF.Tanh))
                    ev[("wt1_s", ui)] = gS.n

                    sbase = ctx.sem("gS").n

                    @ctx.block.tensor
                    def _(t, ui=ui, sbase=sbase):
                        t.wait_ge(gS.h, ev[("wt1_s", ui)])
                        for g8 in range(NG):
                            if g8 >= 2:
                                t.wait_ge(gS.h, sbase + (g8 - 2) + 1)
                            ev[("w2_t", ui, g8)] = gT.n + 1
                            gT.tag(t.matmul(out=pA[g8 % 2][:, 0:512],
                                            lhsT=ww2s[:, 128 * g8:128 * (g8 + 1)],
                                            rhs=wt1[:], start=True, stop=True))

                    @ctx.block.scalar
                    def _(s, ui=ui, sbase=sbase):
                        assert gS.n == sbase
                        for g8 in range(NG):
                            s.wait_ge(gT.h, ev[("w2_t", ui, g8)])
                            gS.tag(s.activation(out=ewt[:, g8, :], in_=pA[g8 % 2][:, 0:512],
                                                func=AF.Exp, bias=bwsb[:, g8:g8 + 1]))
                    ev[("ew_s", ui)] = gS.n

                else:
                    dstp = vpm if lab == "v" else gpm
                    for g8 in range(NG):
                        @ctx.block.gpsimd
                        def _(g, lab=lab, g8=g8, ui=ui):
                            g.wait_ge(gT.h, ev.get(("wuse_t", gW.issued - 2), 0))
                            wsl = wf["wv" if lab == "v" else "wg"]
                            gW.issue(g, lambda: g.dma_start(
                                out=wmv[g8 % 2][:],
                                in_=wsl[:, 256 * g8:256 * (g8 + 1)].rearrange(
                                    "(c p) f -> p c f", p=128)))
                        wi = gW.issued - 1

                        sbase = ctx.sem("gS").n

                        @ctx.block.tensor
                        def _(t, g8=g8, wi=wi, ui=ui, i=i, sbase=sbase):
                            gW.wait(t, wi)
                            t.wait_ge(gV.h, ev[("xiall_v", ui, i)])
                            ev[("wuse_t", wi)] = gT.n + 4
                            for tsub in range(4):
                                k = g8 * 4 + tsub
                                if k >= 2:
                                    j = k - 2
                                    if j >= 4 * g8:
                                        t.wait_ge(gS.h, sbase + (j - 4 * g8) + 1)
                                    else:
                                        t.wait_ge(gS.h, ev[("pmev_s", ui, i, j)])
                                for dc in range(15):
                                    t.matmul(out=pA[tsub % 2][:, 0:256],
                                             lhsT=xi[:, dc, 128 * tsub:128 * (tsub + 1)],
                                             rhs=wmv[g8 % 2][:, dc, :],
                                             start=(dc == 0), stop=False)
                                ev[("pm_t", ui, i, k)] = gT.n + 1
                                gT.tag(t.matmul(out=pA[tsub % 2][:, 0:256],
                                                lhsT=xi[:, 15, 128 * tsub:128 * (tsub + 1)],
                                                rhs=wmv[g8 % 2][:, 15, :],
                                                start=False, stop=True))

                        @ctx.block.scalar
                        def _(s, g8=g8, dstp=dstp, ui=ui, i=i):
                            for tsub in range(4):
                                k = g8 * 4 + tsub
                                s.wait_ge(gT.h, ev[("pm_t", ui, i, k)])
                                ev[("pmev_s", ui, i, k)] = gS.n + 1
                                gS.tag(s.copy(out=dstp[:, tsub, g8, :],
                                              in_=pA[tsub % 2][:, 0:256]))

                ev[("proj_t", ui, sidx)] = gT.n
            ev[("projall_s", ui)] = gS.n
            ev[("projall_t", ui)] = gT.n

            for g8 in range(NG):
                for c4 in range(4):
                    @ctx.block.vector
                    def _(v, g8=g8, c4=c4, ui=ui, pre=pre):
                        if g8 == 0 and c4 == 0:
                            v.wait_ge(gS.h, ev[("projall_s", ui)])
                        v.wait_ge(gV.h, gV.n)
                        v.wait_ge(gS.h, ev.get(("exp_s", ui, g8, c4 - 1),
                                               ev.get(("exp_s", ui, g8 - 1, 3), 0)))
                        for cc in range(4):
                            gV.tag(v.tensor_tensor_scan(
                                out=abuf[:, g8, cc, 1:33],
                                data0=ewt[:, g8, 128 * c4 + 32 * cc:128 * c4 + 32 * (cc + 1)],
                                data1=ewt[:, g8, 128 * c4 + 32 * cc:128 * c4 + 32 * (cc + 1)],
                                initial=0.0, op0=ALU.add, op1=ALU.bypass))
                        v.wait_ge(gV.h, gV.n)
                        gV.tag(v.tensor_scalar(out=ecr[:], in0=abuf[:, g8, :, 32],
                                               scalar1=-1.0, scalar2=None, op0=ALU.mult))
                        v.wait_ge(gV.h, gV.n)
                        gV.tag(v.tensor_scalar(out=ecr2[:], in0=abuf[:, g8, :, 16],
                                               scalar1=-1.0, scalar2=None, op0=ALU.mult))
                        for cc in range(4):
                            gV.tag(v.tensor_scalar(out=aarg[:, 2, cc, :],
                                                   in0=abuf[:, g8, cc, 1:33],
                                                   scalar1=1.0, scalar2=abuf[:, g8, cc, 32:33],
                                                   op0=ALU.mult, op1=ALU.subtract))
                            if not pre:
                                gV.tag(v.tensor_scalar(out=aarg[:, 0, cc, :],
                                                       in0=abuf[:, g8, cc, 0:32],
                                                       scalar1=-1.0, scalar2=abuf[:, g8, cc, 16:17],
                                                       op0=ALU.mult, op1=ALU.add))
                                gV.tag(v.tensor_scalar(out=aarg[:, 1, cc, :],
                                                       in0=abuf[:, g8, cc, 1:33],
                                                       scalar1=1.0, scalar2=abuf[:, g8, cc, 16:17],
                                                       op0=ALU.mult, op1=ALU.subtract))
                        ev[("aarg_v", ui, g8, c4)] = gV.n

                    @ctx.block.scalar
                    def _(s, g8=g8, c4=c4, ui=ui, pre=pre):
                        s.wait_ge(gV.h, ev[("aarg_v", ui, g8, c4)])
                        s.wait_ge(gV.h, ev.get(("ktf2_v", ui, g8, c4 - 1),
                                               ev.get(("ktf2_v", ui, g8 - 1, 3),
                                                      ev.get(("ktf_v", ui, g8, c4 - 1),
                                                             ev.get(("ktf_v", ui, g8 - 1, 3), 0)))))
                        gS.tag(s.activation(out=ecol[:, g8, 4 * c4:4 * (c4 + 1)],
                                            in_=ecr[:], func=AF.Exp))
                        gS.tag(s.activation(out=emc[:, g8, 4 * c4:4 * (c4 + 1)],
                                            in_=ecr2[:], func=AF.Exp))
                        if pre:
                            ev[("exp_s", ui, g8, c4)] = gS.n + 1
                            gS.tag(s.activation(out=exf[:, 2, :], in_=aarg[:, 2, :, :],
                                                func=AF.Exp))
                        else:
                            gS.tag(s.activation(out=exf[:, 0, :], in_=aarg[:, 0, :, :],
                                                func=AF.Exp))
                            gS.tag(s.activation(out=exf[:, 1, :], in_=aarg[:, 1, :, :],
                                                func=AF.Exp))
                            ev[("exp_s", ui, g8, c4)] = gS.n + 1
                            gS.tag(s.activation(out=exf[:, 2, :], in_=aarg[:, 2, :, :],
                                                func=AF.Exp))
                    if FST < 3:
                        ev[("uev_v", ui, g8, c4)] = ctx.sem("gV").n
                        ev[("exp_s", ui, g8, c4)] = ev.get(("exp_s", ui, g8, c4), ctx.sem("gS").n)
                        continue

                    @ctx.block.vector
                    def _(v, g8=g8, c4=c4, ui=ui, pre=pre):
                        v.wait_ge(gS.h, ev[("exp_s", ui, g8, c4)])
                        v.wait_ge(gT.h, ev.get(("kT_t", ui, g8, c4 - 1),
                                               ev.get(("kT_t", ui, g8 - 1, 3), 0)))
                        gV.tag(v.tensor_tensor(out=ktf[:], in0=kkm[:, g8, 128 * c4:128 * (c4 + 1)],
                                               in1=exf[:, 2, :], op=ALU.mult))
                        if not pre:
                            v.wait_ge(gV.h, gV.n)
                            gV.tag(v.tensor_tensor(out=rstash[:, g8, c4, :],
                                                   in0=rkm[:, g8, 128 * c4:128 * (c4 + 1)],
                                                   in1=exf[:, 0, :], op=ALU.mult))
                        ev[("ktf_v", ui, g8, c4)] = gV.n

                    @ctx.block.tensor
                    def _(t, g8=g8, c4=c4, ui=ui):
                        t.wait_ge(gV.h, ev[("ktf_v", ui, g8, c4)])
                        t.wait_ge(gV.h, ev.get(("ktpev_v", ui, g8, c4 - 1),
                                               ev.get(("ktpev_v", ui, g8 - 1, 3), 0)))
                        if g8 == 0 and c4 == 0:
                            t.wait_ge(gS.h, ev.get(("asmS_last", ui - 1), 0))
                        ev[("kT_t", ui, g8, c4)] = gT.n + 1
                        gT.tag(t.transpose(out=pTb[:, 0, :], in_=ktf[:], identity=idn[:]))

                    @ctx.block.vector
                    def _(v, g8=g8, c4=c4, ui=ui):
                        v.wait_ge(gT.h, ev[("kT_t", ui, g8, c4)])
                        v.wait_ge(gT.h, ev.get(("u_t", ui, g8, c4 - 1),
                                               ev.get(("u_t", ui, g8 - 1, 3), 0)))
                        for cc in range(4):
                            gV.tag(v.tensor_scalar(out=ktp_z[:, cc, :], in0=pTb[:, 0, :],
                                                   scalar1=mskc[:, cc:cc + 1],
                                                   scalar2=None, op0=ALU.mult))
                        ev[("ktpev_v", ui, g8, c4)] = gV.n

                    if FCHK < 2:
                        ev[("uev_v", ui, g8, c4)] = ctx.sem("gV").n
                        ev[("u_t", ui, g8, c4)] = ctx.sem("gT").n
                        continue

                    vbase_u = ctx.sem("gV").n

                    @ctx.block.tensor
                    def _(t, g8=g8, c4=c4, ui=ui, vbase_u=vbase_u):
                        t.wait_ge(gV.h, vbase_u)
                        for cc in range(4):
                            dst = pU if cc < 2 else pT
                            ev[("u_t", ui, g8, c4, cc)] = gT.n + 1
                            gT.tag(t.matmul(out=dst[:, 256 * (cc % 2):256 * (cc % 2) + 256],
                                            lhsT=ktp_z[:, cc, :],
                                            rhs=vpm[:, c4, g8, :],
                                            start=True, stop=True))
                        ev[("u_t", ui, g8, c4)] = gT.n

                    @ctx.block.vector
                    def _(v, g8=g8, c4=c4, ui=ui, pre=pre, vbase_u=vbase_u):
                        assert ctx.sem("gV").n == vbase_u
                        for cc in range(4):
                            if cc % 2 == 0:
                                v.wait_ge(gT.h, ev[("u_t", ui, g8, c4, cc + 1)])
                            src = pU if cc < 2 else pT
                            for j in range(4):
                                gV.tag(v.tensor_scalar(
                                    out=ubuf[32 * j:32 * (j + 1), g8, 4 * c4 + cc, :],
                                    in0=src[32 * j:32 * (j + 1),
                                            256 * (cc % 2) + 64 * j:256 * (cc % 2) + 64 * j + 64],
                                    scalar1=(pflag[32 * j:32 * (j + 1), 0:1] if pre else 1.0),
                                    scalar2=None, op0=ALU.mult))
                        ev[("uev_v", ui, g8, c4)] = gV.n
                    if FCHK < 3:
                        continue

                    if not pre:
                        @ctx.block.vector
                        def _(v, g8=g8, c4=c4, ui=ui):
                            v.wait_ge(gV.h, ev[("ktf_v", ui, g8, c4)])
                            v.wait_ge(gT.h, ev.get(("p_t", ui, g8, c4 - 1),
                                                   ev.get(("p_t", ui, g8 - 1, 3), 0)))
                            v.wait_ge(gT.h, ev.get(("d_t", ui, g8, c4 - 1),
                                                   ev.get(("d_t", ui, g8 - 1, 3), 0)))
                            for hp in range(4):
                                gV.tag(v.tensor_tensor(
                                    out=ktf_z[32 * hp:32 * (hp + 1), hp, :],
                                    in0=kkm[32 * hp:32 * (hp + 1), g8, 128 * c4:128 * (c4 + 1)],
                                    in1=exf[32 * hp:32 * (hp + 1), 1, :], op=ALU.mult))
                            gV.tag(v.tensor_tensor(out=rbk[:],
                                                   in0=kkm[:, g8, 128 * c4:128 * (c4 + 1)],
                                                   in1=rkm[:, g8, 128 * c4:128 * (c4 + 1)],
                                                   op=ALU.mult))
                            ev[("ktf2_v", ui, g8, c4)] = gV.n

                        @ctx.block.tensor
                        def _(t, g8=g8, c4=c4, ui=ui):
                            t.wait_ge(gV.h, ev[("ktf2_v", ui, g8, c4)])
                            t.wait_ge(gV.h, ev.get(("pev_v", ui, g8, c4 - 1),
                                                   ev.get(("pev_v", ui, g8 - 1, 3), 0)))
                            for hp in range(4):
                                mm = t.matmul(out=pT[:, 128 * hp:128 * (hp + 1)],
                                              lhsT=ktf_z[:, hp, :],
                                              rhs=rstash[:, g8, c4, :],
                                              start=True, stop=True)
                                if hp == 3:
                                    gT.tag(mm)
                            ev[("p_t", ui, g8, c4)] = gT.n
                            if FCHK >= 4:
                                t.wait_ge(gV.h, ev.get(("dcev_v", ui, g8, c4 - 1),
                                                       ev.get(("dcev_v", ui, g8 - 1, 3), 0)))
                                gT.tag(t.matmul(out=pI[:, 256:260], lhsT=rbk[:],
                                                rhs=bonsb[:, g8, :], start=True, stop=True))
                                ev[("d_t", ui, g8, c4)] = gT.n

                        @ctx.block.vector
                        def _(v, g8=g8, c4=c4, ui=ui):
                            v.wait_ge(gT.h, ev[("p_t", ui, g8, c4)])
                            for cc in range(4):
                                for hp in range(4):
                                    gV.tag(v.tensor_tensor(
                                        out=pmsb[32 * cc:32 * (cc + 1), g8, c4, hp, :],
                                        in0=pT[32 * cc:32 * (cc + 1),
                                               128 * hp + 32 * cc:128 * hp + 32 * (cc + 1)],
                                        in1=pmaskb[32 * cc:32 * (cc + 1), hp, :],
                                        op=ALU.mult))
                            ev[("pev_v", ui, g8, c4)] = gV.n
                            if FCHK >= 4:
                                v.wait_ge(gT.h, ev[("d_t", ui, g8, c4)])
                                ev[("dcev_v", ui, g8, c4)] = gV.n + 1
                                gV.tag(v.tensor_copy(out=dstash[:, g8, c4, :], in_=pI[:, 256:260]))

            def scan_block(g8):
                @ctx.block.vector
                def _(v, g8=g8, ui=ui):
                    v.wait_ge(gV.h, ev[("uev_v", ui, g8, 3)])
                    gV.tag(v.tensor_copy(out=straj[:, 0, :], in_=scar[:, g8, :]))
                    for c in range(16):
                        v.wait_ge(gV.h, gV.n)
                        gV.tag(v.scalar_tensor_tensor(
                            out=straj[:, c + 1, :], in0=straj[:, c, :],
                            scalar=ecol[:, g8, c:c + 1], in1=ubuf[:, g8, c, :],
                            op0=ALU.mult, op1=ALU.add))
                    v.wait_ge(gV.h, gV.n)
                    gV.tag(v.tensor_copy(out=scar[:, g8, :], in_=straj[:, 16, :]))
                ev[("scan_v", ui, g8)] = gV.n

            def va0(k):
                g8, c4, ph = k // 4, k % 4, k % 2

                @ctx.block.vector
                def _(v, g8=g8, c4=c4, k=k, ui=ui):
                    v.wait_ge(gT.h, ev.get(("tmm_t", ui, k - 1), 0))
                    v.wait_ge(gV.h, ev[("pev_v", ui, g8, c4)])
                    v.wait_ge(gV.h, ev[("scan_v", ui, g8)])
                    for cc in range(4):
                        for j in range(4):
                            gV.tag(v.tensor_scalar(
                                out=sbf_bd[32 * j:32 * (j + 1), cc, 64 * j:64 * (j + 1)],
                                in0=straj[32 * j:32 * (j + 1), 4 * c4 + cc, :],
                                scalar1=emc[32 * j:32 * (j + 1), g8,
                                            4 * c4 + cc:4 * c4 + cc + 1],
                                scalar2=None, op0=ALU.mult))
                    for cc in range(4):
                        gV.tag(v.tensor_copy(
                            out=pmbd[32 * cc:32 * (cc + 1), :, 32 * cc:32 * (cc + 1)],
                            in_=pmsb[32 * cc:32 * (cc + 1), g8, c4, :, :]))
                ev[("va0_v", ui, k)] = gV.n

            def tmm(k):
                g8, c4, ph = k // 4, k % 4, k % 2
                pnA, pnB = (pN[0], pN[1]) if ph == 0 else (pA[0], pA[1])

                @ctx.block.tensor
                def _(t, g8=g8, c4=c4, ph=ph, k=k, ui=ui, pnA=pnA, pnB=pnB):
                    t.wait_ge(gV.h, ev[("va0_v", ui, k)])
                    t.wait_ge(gV.h, ev.get(("va1_v", ui, k - 2), 0))
                    for hp in range(4):
                        t.matmul(out=pI[:, 256 * ph + 64 * hp:256 * ph + 64 * (hp + 1)],
                                 lhsT=pmbd[:, hp, :],
                                 rhs=vpm[:, c4, g8, 64 * hp:64 * (hp + 1)],
                                 start=True, stop=True)
                    for cc in range(4):
                        dst = pnA if cc < 2 else pnB
                        mm = t.matmul(out=dst[:, 256 * (cc % 2):256 * (cc % 2) + 256],
                                      lhsT=rstash[:, g8, c4, :],
                                      rhs=sbf_bd[:, cc, :],
                                      start=True, stop=True)
                    gT.tag(mm)
                ev[("tmm_t", ui, k)] = gT.n

            def va1(k):
                g8, c4, ph = k // 4, k % 4, k % 2
                pnA, pnB = (pN[0], pN[1]) if ph == 0 else (pA[0], pA[1])

                @ctx.block.vector
                def _(v, g8=g8, c4=c4, ph=ph, k=k, ui=ui, pnA=pnA, pnB=pnB):
                    v.wait_ge(gT.h, ev[("tmm_t", ui, k)])
                    v.wait_ge(gV.h, ev[("dcev_v", ui, g8, c4)])
                    for hp in range(4):
                        gV.tag(v.scalar_tensor_tensor(
                            out=osb[:, ph, 64 * hp:64 * (hp + 1)],
                            in0=vpm[:, c4, g8, 64 * hp:64 * (hp + 1)],
                            scalar=dstash[:, g8, c4, hp:hp + 1],
                            in1=pI[:, 256 * ph + 64 * hp:256 * ph + 64 * (hp + 1)],
                            op0=ALU.mult, op1=ALU.add))
                    v.wait_ge(gV.h, gV.n)
                    for cc in range(4):
                        dst = pnA if cc < 2 else pnB
                        gV.tag(v.tensor_tensor(
                            out=osb[32 * cc:32 * (cc + 1), ph, :],
                            in0=osb[32 * cc:32 * (cc + 1), ph, :],
                            in1=dst[32 * cc:32 * (cc + 1),
                                    256 * (cc % 2):256 * (cc % 2) + 256],
                            op=ALU.add))
                    v.wait_ge(gV.h, gV.n)
                    gV.tag(v.tensor_reduce(out=gst[:, 16 * ph:16 * ph + 4],
                                           in_=osb[:, ph, :].rearrange(
                                               "p (h x) -> p h x", h=4),
                                           axis=AX.X, op=ALU.add))
                    gV.tag(v.tensor_tensor(out=sqf[:], in0=osb[:, ph, :],
                                           in1=osb[:, ph, :], op=ALU.mult))
                    v.wait_ge(gV.h, gV.n)
                    gV.tag(v.tensor_reduce(out=gst[:, 16 * ph + 4:16 * ph + 8],
                                           in_=sqf.ap().rearrange(
                                               "p (h x) -> p h x", h=4),
                                           axis=AX.X, op=ALU.add))
                    gV.tag(v.tensor_scalar(out=gst[:, 16 * ph:16 * ph + 4],
                                           in0=gst[:, 16 * ph:16 * ph + 4],
                                           scalar1=1.0 / 64, scalar2=None, op0=ALU.mult))
                    v.wait_ge(gV.h, gV.n)
                    gV.tag(v.tensor_tensor(out=gst[:, 16 * ph + 8:16 * ph + 12],
                                           in0=gst[:, 16 * ph:16 * ph + 4],
                                           in1=gst[:, 16 * ph:16 * ph + 4], op=ALU.mult))
                    gV.tag(v.tensor_scalar(out=gst[:, 16 * ph + 4:16 * ph + 8],
                                           in0=gst[:, 16 * ph + 4:16 * ph + 8],
                                           scalar1=1.0 / 64, scalar2=None, op0=ALU.mult))
                    v.wait_ge(gV.h, gV.n)
                    gV.tag(v.tensor_tensor(out=gst[:, 16 * ph + 4:16 * ph + 8],
                                           in0=gst[:, 16 * ph + 4:16 * ph + 8],
                                           in1=gst[:, 16 * ph + 8:16 * ph + 12],
                                           op=ALU.subtract))
                ev[("va1_v", ui, k)] = gV.n

            def sb(k):
                g8, c4, ph = k // 4, k % 4, k % 2

                @ctx.block.scalar
                def _(s, g8=g8, c4=c4, k=k, ph=ph, ui=ui):
                    s.wait_ge(gV.h, ev[("va1_v", ui, k)])
                    s.wait_ge(gV.h, ev.get(("vc_v", ui, k - 1), 0))
                    gS.tag(s.activation(out=gst[:, 16 * ph + 12:16 * ph + 16],
                                        in_=gst[:, 16 * ph + 4:16 * ph + 8],
                                        func=AF.Ln, bias=ctx.cc[1e-5]))
                    s.wait_ge(gS.h, gS.n)
                    gS.tag(s.activation(out=gst[:, 16 * ph + 8:16 * ph + 12],
                                        in_=gst[:, 16 * ph + 12:16 * ph + 16],
                                        func=AF.Exp, scale=-0.5))
                    gS.tag(s.activation(out=sgl[:], in_=gpm[:, c4, g8, :],
                                        func=AF.Silu))
                ev[("sb_s", ui, k)] = gS.n

            def vc(k):
                g8, c4, ph = k // 4, k % 4, k % 2

                @ctx.block.vector
                def _(v, g8=g8, c4=c4, ph=ph, k=k, ui=ui):
                    v.wait_ge(gS.h, ev[("sb_s", ui, k)])
                    v.wait_ge(gT.h, ev.get(("td_t", ui, k - 1), 0))
                    for hp in range(4):
                        gV.tag(v.tensor_scalar(
                            out=onb[:, 64 * hp:64 * (hp + 1)],
                            in0=osb[:, ph, 64 * hp:64 * (hp + 1)],
                            scalar1=gst[:, 16 * ph + hp:16 * ph + hp + 1],
                            scalar2=gst[:, 16 * ph + 8 + hp:16 * ph + 9 + hp],
                            op0=ALU.subtract, op1=ALU.mult))
                    gV.tag(v.tensor_tensor(out=t1g[:], in0=sgl[:], in1=gnwb[:],
                                           op=ALU.mult))
                    v.wait_ge(gV.h, gV.n)
                    gV.tag(v.tensor_tensor(out=gtin[:], in0=onb[:], in1=t1g[:],
                                           op=ALU.mult))
                    v.wait_ge(gV.h, gV.n)
                    gV.tag(v.tensor_tensor(out=t1g[:], in0=sgl[:], in1=gnbb[:],
                                           op=ALU.mult))
                    v.wait_ge(gV.h, gV.n)
                    gV.tag(v.tensor_tensor(out=gtin[:], in0=gtin[:], in1=t1g[:],
                                           op=ALU.add))
                ev[("vc_v", ui, k)] = gV.n

            def td(k):
                @ctx.block.tensor
                def _(t, k=k, ui=ui):
                    t.wait_ge(gV.h, ev[("vc_v", ui, k)])
                    t.wait_ge(gS.h, ev.get(("se_s", ui, k - 1), 0))
                    t.wait_ge(gV.h, ev.get(("ktpev_v", ui, 7, 3), 0))
                    for hh in range(2):
                        gT.tag(t.transpose(out=pTb[:, 1 + hh, :],
                                           in_=gtin[:, 128 * hh:128 * (hh + 1)],
                                           identity=idn[:]))
                ev[("td_t", ui, k)] = gT.n

            def se(k):
                @ctx.block.scalar
                def _(s, k=k, ui=ui):
                    for hh in range(2):
                        s.wait_ge(gT.h, ev[("td_t", ui, k)] - 1 + hh)
                        idx = 64 * (ui - 4) + 2 * k + hh - 2
                        if idx >= 0:
                            gO.wait(s, idx)
                        ev[("se_s", ui, k, hh)] = gS.n + 1
                        gS.tag(s.copy(out=gtb[:, hh, :], in_=pTb[:, 1 + hh, :]))
                ev[("se_s", ui, k)] = gS.n

            def gf(k):
                g8, c4 = k // 4, k % 4

                @ctx.block.gpsimd
                def _(g, g8=g8, c4=c4, k=k, ui=ui, u=u):
                    for hh in range(2):
                        g.wait_ge(gS.h, ev[("se_s", ui, k, hh)])
                        row = g8 * 256 + hh * 128
                        col = u * 512 + c4 * 128
                        gO.issue(g, lambda: g.dma_start(
                            out=gatedT[row:row + 128, col:col + 128],
                            in_=gtb[:, hh, :]))

            if FST < 4:
                for g8 in range(NG):
                    ev[("scan_v", ui, g8)] = ctx.sem("gV").n
            else:
                do_asm = (not pre) and FST >= 5
                for g8 in range(NG):
                    scan_block(g8)
                    if do_asm:
                        for c4 in range(4):
                            k = 4 * g8 + c4
                            if k >= 1:
                                va1(k - 1)
                            if k >= 2:
                                vc(k - 2)
                                td(k - 2)
                                se(k - 2)
                                gf(k - 2)
                            if k >= 1:
                                sb(k - 1)
                            va0(k)
                            tmm(k)
                if do_asm:
                    va1(31)
                    vc(30)
                    td(30)
                    se(30)
                    gf(30)
                    sb(31)
                    vc(31)
                    td(31)
                    se(31)
                    gf(31)
                    ev[("asmS_last", ui)] = gS.n

            ev[("uend_t", ui)] = gT.n
            ev[("uend_v", ui)] = gV.n
            ev[("uend_s", ui)] = gS.n

        @ctx.block.gpsimd
        def _(g):
            for k in range(max(0, gO.issued - 4), gO.issued):
                gO.wait(g, k)


def ph_s(ctx, xn1p, x1f, pflag_d, xn1ownS, x1oS):
    nc = ctx.nc
    sL = DRing(ctx, "sL", 4)
    sV = ctx.sem("sV")
    sS = DRing(ctx, "sS", 4, cls="sw")
    sC = ctx.sem("sC", 16)
    with contextlib.ExitStack() as es:
        ta = [es.enter_context(nc.sbuf_tensor(f"phs_a{i}", [128, 2, 2049], BF16))
              for i in range(2)]
        tb = [es.enter_context(nc.sbuf_tensor(f"phs_b{i}", [128, 2, 2048], F32))
              for i in range(2)]
        da = [es.enter_context(nc.sbuf_tensor(f"phs_d{i}", [128, 2049], F32))
              for i in range(2)]
        oa = [es.enter_context(nc.sbuf_tensor(f"phs_oa{i}", [128, 2049], BF16))
              for i in range(2)]
        ob = [es.enter_context(nc.sbuf_tensor(f"phs_ob{i}", [128, 2048], F32))
              for i in range(2)]
        fl = es.enter_context(nc.sbuf_tensor("phs_fl", [128, 1], F32))
        zc = es.enter_context(nc.sbuf_tensor("phs_zc", [128, 1], BF16))
        sZ = ctx.sem("sZ")
        sZD = DRing(ctx, "sZD", 2, cls="sw")

        @ctx.block.vector
        def _(v):
            sZ.tag(v.memset(zc[:], 0.0))

        @ctx.block.gpsimd
        def _(g):
            g.wait_ge(sZ.h, sZ.n)
            for i in range(16):
                sZD.issue(g, lambda: g.dma_start(
                    out=xn1p[128 * i:128 * (i + 1), 0:1], in_=zc[:],
                    allow_slow_non_contiguous=True))
            for k in range(2):
                sZD.wait(g, sZD.issued - 2 + k)

        @ctx.block.sync
        def _(q):
            sC.tag(q.dma_start(out=fl[:], in_=pflag_d))
            sZD.wait(q, 15)
            for i in range(16):
                p = i % 2
                if i >= 2:
                    q.wait_ge(sV.h, sV.at(4 * (i - 1)))
                sL.issue(q, lambda: q.dma_start(
                    out=ta[p][:, 0, :], in_=xn1p[128 * i:128 * (i + 1), 0:2049]))
                sL.issue(q, lambda: q.dma_start(
                    out=ta[p][:, 1, :], in_=xn1p[128 * i:128 * (i + 1), 2048:4097]))
                sL.issue(q, lambda: q.dma_start(
                    out=tb[p][:, 0, :], in_=x1f[128 * i:128 * (i + 1), 0:2048]))
                sL.issue(q, lambda: q.dma_start(
                    out=tb[p][:, 1, :], in_=x1f[128 * i:128 * (i + 1), 2048:4096]))

        @ctx.block.vector
        def _(v):
            v.wait_ge(sC.h, sC.at(1))
            for i in range(16):
                p = i % 2
                for k in range(4):
                    sL.wait(v, 4 * i + k)
                if i >= 2:
                    sS.wait(v, 2 * (i - 2))
                    sS.wait(v, 2 * (i - 2) + 1)
                sV.tag(v.tensor_tensor(out=da[p][:], in0=ta[p][:, 1, :],
                                       in1=ta[p][:, 0, :], op=ALU.subtract))
                v.wait_ge(sV.h, sV.n)
                sV.tag(v.scalar_tensor_tensor(out=oa[p][:], in0=da[p][:],
                                              scalar=fl[:, 0:1], in1=ta[p][:, 0, :],
                                              op0=ALU.mult, op1=ALU.add))
                sV.tag(v.tensor_tensor(out=tb[p][:, 1, :], in0=tb[p][:, 1, :],
                                       in1=tb[p][:, 0, :], op=ALU.subtract))
                v.wait_ge(sV.h, sV.n)
                sV.tag(v.scalar_tensor_tensor(out=ob[p][:], in0=tb[p][:, 1, :],
                                              scalar=fl[:, 0:1], in1=tb[p][:, 0, :],
                                              op0=ALU.mult, op1=ALU.add))

        @ctx.block.gpsimd
        def _(g):
            for i in range(16):
                p = i % 2
                g.wait_ge(sV.h, sV.at(4 * i + 2))
                sS.issue(g, lambda: g.dma_start(
                    out=xn1ownS[128 * i:128 * (i + 1), :], in_=oa[p][:]))
                g.wait_ge(sV.h, sV.at(4 * i + 4))
                sS.issue(g, lambda: g.dma_start(
                    out=x1oS[128 * i:128 * (i + 1), :], in_=ob[p][:]))


def ph_g0(ctx, upT, downT, upS, dnS, bufs_es=None):
    nc = ctx.nc
    hW = DRing(ctx, "hW", 2)
    hS = DRing(ctx, "hS", 2, cls="sw")
    hC = ctx.sem("hC")
    jobs = ([("u", c, j) for c in range(16) for j in range(4)]
            + [("d", c, 0) for c in range(64)])
    with contextlib.ExitStack() as _es:
        es = bufs_es if bufs_es is not None else _es
        wci = [es.enter_context(nc.sbuf_tensor(f"phg0_i{i}", [128, D], F32)) for i in range(2)]
        wco = [es.enter_context(nc.sbuf_tensor(f"phg0_o{i}", [128, D], BF16)) for i in range(2)]

        def span(kind, c, j):
            srct = upT if kind == "u" else downT
            dst = upS if kind == "u" else dnS
            return (srct[128 * c:128 * (c + 1), 2048 * j:2048 * (j + 1)],
                    dst[128 * c:128 * (c + 1), 2048 * j:2048 * (j + 1)])

        @ctx.block.sync
        def _(q):
            def load(ji):
                kind, c, j = jobs[ji]
                if ji >= 2:
                    q.wait_ge(hC.h, hC.at(ji - 1))
                srcap, _ = span(kind, c, j)
                hW.issue(q, lambda: q.dma_start(out=wci[ji % 2][:], in_=srcap))
            load(0)
            load(1)
            for ji in range(len(jobs)):
                if ji + 2 < len(jobs):
                    load(ji + 2)

        @ctx.block.scalar
        def _(s):
            for ji, (kind, c, j) in enumerate(jobs):
                hW.wait(s, ji)
                if ji >= 2:
                    hS.wait(s, ji - 2)
                hC.tag(s.activation(out=wco[ji % 2][:], in_=wci[ji % 2][:], func=AF.Sign))

        @ctx.block.gpsimd
        def _(g):
            for ji, (kind, c, j) in enumerate(jobs):
                g.wait_ge(hC.h, hC.at(ji + 1))
                _, dstap = span(kind, c, j)
                hS.issue(g, lambda: g.dma_start(out=dstap, in_=wco[ji % 2][:]))


def ph_g(ctx, gatedT, wo, xn1o, upS, dnS, upsc, dnsc, osc, x1o, out):
    nc = ctx.nc
    NHC1 = HID1 // 128
    ev = {}
    kG = DRing(ctx, "kG", 4)
    kW = DRing(ctx, "kW", 2, cls="sw")
    kX = DRing(ctx, "kX", 4)
    kU = DRing(ctx, "kU", 2)
    kD = DRing(ctx, "kD", 2)
    kR = DRing(ctx, "kR", 2)
    kO = DRing(ctx, "kO", 2, cls="sw")
    kC = ctx.sem("kC", 16)
    kM = ctx.sem("kM")
    kE = ctx.sem("kE")
    kMu = ctx.sem("kMu")
    kRe = ctx.sem("kRe")
    kMd = ctx.sem("kMd")
    kF = ctx.sem("kF")
    kSi = ctx.sem("kSi")
    kV = ctx.sem("kV")
    kI = ctx.sem("kI")

    with contextlib.ExitStack() as es:
        gb = es.enter_context(nc.sbuf_tensor("phg_g", [128, 16, 512], BF16))
        wst = [es.enter_context(nc.sbuf_tensor(f"phg_w{i}", [128, 16, 128], BF16)) for i in range(2)]
        xb = es.enter_context(nc.sbuf_tensor("phg_x", [128, 16, 512], BF16))
        wu = [es.enter_context(nc.sbuf_tensor(f"phg_wu{i}", [128, 16, 128], BF16)) for i in range(2)]
        wd = [es.enter_context(nc.sbuf_tensor(f"phg_wd{i}", [128, NHC1, 128], BF16)) for i in range(2)]
        s1b = es.enter_context(nc.sbuf_tensor("phg_s1", [128, 16, 512], BF16))
        prod = es.enter_context(nc.sbuf_tensor("phg_pr", [128, NHC1, 512], BF16))
        scu = es.enter_context(nc.sbuf_tensor("phg_su", [128, NHC1], F32))
        scd = es.enter_context(nc.sbuf_tensor("phg_sd", [128, 16], F32))
        osc_s = es.enter_context(nc.sbuf_tensor("phg_os", [1, 1], F32))
        oner = es.enter_context(nc.sbuf_tensor("phg_1r", [1, 128], F32))
        oscol = es.enter_context(nc.sbuf_tensor("phg_oc", [128, 1], F32))
        f1t = es.enter_context(nc.sbuf_tensor("phg_f1", [128, 512], BF16))
        g1t = es.enter_context(nc.sbuf_tensor("phg_g1", [128, 512], BF16))
        x1c = [es.enter_context(nc.sbuf_tensor(f"phg_x1{i}", [128, 512], F32)) for i in range(2)]
        x2t = [es.enter_context(nc.sbuf_tensor(f"phg_x2{i}", [128, 512], F32)) for i in range(2)]
        pW = [es.enter_context(nc.psum_tensor(f"phg_pW{i}", [128, 512], F32)) for i in range(2)]
        pF = [es.enter_context(nc.psum_tensor(f"phg_pF{i}", [128, 512], F32)) for i in range(2)]
        pOs = es.enter_context(nc.psum_tensor("phg_pOs", [128, 1], F32))

        @ctx.block.sync
        def _(q):
            kC.tag(q.dma_start(out=scu[:], in_=upsc))
            kC.tag(q.dma_start(out=scd[:], in_=dnsc))
            kC.tag(q.dma_start(out=osc_s[:], in_=osc))

        @ctx.block.vector
        def _(v):
            kI.tag(v.memset(oner[:], 1.0))

        @ctx.block.tensor
        def _(t):
            t.wait_ge(kC.h, kC.at(3))
            t.wait_ge(kI.h, kI.at(1))
            kI.tag(t.matmul(out=pOs[:], lhsT=oner[:], rhs=osc_s[:], start=True, stop=True))

        @ctx.block.vector
        def _(v):
            v.wait_ge(kI.h, kI.at(2))
            kI.tag(v.tensor_copy(out=oscol[:], in_=pOs[:]))

        for qt in range(4):
            @ctx.block.sync
            def _(q, qt=qt):
                for fc in range(16):
                    if qt >= 1:
                        q.wait_ge(kM.h, kM.at(16 * qt))
                    kG.issue(q, lambda: q.dma_start(
                        out=gb[:, fc, :], in_=gatedT[128 * fc:128 * (fc + 1),
                                                     512 * qt:512 * (qt + 1)]))

            @ctx.block.gpsimd
            def _(g, qt=qt):
                for dc in range(16):
                    si = 16 * qt + dc
                    if si >= 2:
                        g.wait_ge(kM.h, kM.at(si - 1))
                    kW.issue(g, lambda: g.dma_start(
                        out=wst[si % 2][:],
                        in_=wo[:, 128 * dc:128 * (dc + 1)].rearrange("(c p) f -> p c f", p=128)))

            @ctx.block.tensor
            def _(t, qt=qt):
                for dc in range(16):
                    si = 16 * qt + dc
                    kW.wait(t, si)
                    if dc == 0:
                        for fc in range(16):
                            kG.wait(t, 16 * qt + fc)
                    if si >= 2:
                        t.wait_ge(kE.h, kE.at(si - 1))
                    for fc in range(15):
                        t.matmul(out=pW[si % 2][:], lhsT=wst[si % 2][:, fc, :],
                                 rhs=gb[:, fc, :], start=(fc == 0), stop=False)
                    kM.tag(t.matmul(out=pW[si % 2][:], lhsT=wst[si % 2][:, 15, :],
                                    rhs=gb[:, 15, :], start=False, stop=True))

            @ctx.block.scalar
            def _(s, qt=qt):
                for dc in range(16):
                    si = 16 * qt + dc
                    s.wait_ge(kM.h, kM.at(si + 1))
                    if qt >= 1 and dc == 0:
                        s.wait_ge(kV.h, kV.at(32 * qt))
                    kE.tag(s.copy(out=s1b[:, dc, :], in_=pW[si % 2][:]))
            ev[("s1_s", qt)] = ctx.sem("kE").n

            @ctx.block.sync
            def _(q, qt=qt):
                for dc in range(16):
                    if qt >= 1:
                        q.wait_ge(kMu.h, kMu.at(NHC1 * qt))
                    kX.issue(q, lambda: q.dma_start(
                        out=xb[:, dc, :], in_=xn1o[128 * dc:128 * (dc + 1),
                                                   1 + 512 * qt:1 + 512 * (qt + 1)]))
                for hid in range(NHC1):
                    j = NHC1 * qt + hid
                    if j >= 2:
                        q.wait_ge(kMu.h, kMu.at(j - 1))
                    kU.issue(q, lambda: q.dma_start(
                        out=wu[j % 2][:],
                        in_=upS[:, 128 * hid:128 * (hid + 1)].rearrange("(c p) f -> p c f", p=128)))

            @ctx.block.tensor
            def _(t, qt=qt):
                for fc in range(16):
                    kX.wait(t, 16 * qt + fc)
                for hid in range(NHC1):
                    j = NHC1 * qt + hid
                    kU.wait(t, j)
                    if j >= 2:
                        t.wait_ge(kRe.h, kRe.at(j - 1))
                    if hid == 0 and qt >= 1:
                        t.wait_ge(kMd.h, kMd.at(16 * qt))
                    for dc in range(15):
                        t.matmul(out=pF[j % 2][:], lhsT=wu[j % 2][:, dc, :],
                                 rhs=xb[:, dc, :], start=(dc == 0), stop=False)
                    kMu.tag(t.matmul(out=pF[j % 2][:], lhsT=wu[j % 2][:, 15, :],
                                     rhs=xb[:, 15, :], start=False, stop=True))

            @ctx.block.vector
            def _(v, qt=qt):
                for hid in range(NHC1):
                    j = NHC1 * qt + hid
                    v.wait_ge(kMu.h, kMu.at(j + 1))
                    kRe.tag(v.tensor_scalar(out=prod[:, hid, :], in0=pF[j % 2][:],
                                            scalar1=scu[:, hid:hid + 1], scalar2=0.0,
                                            op0=ALU.mult, op1=ALU.max))

            @ctx.block.sync
            def _(q, qt=qt):
                for dc in range(16):
                    si = 16 * qt + dc
                    if si >= 2:
                        q.wait_ge(kMd.h, kMd.at(si - 1))
                        q.wait_ge(kV.h, kV.at(2 * si - 2))
                    kD.issue(q, lambda: q.dma_start(
                        out=wd[si % 2][:],
                        in_=dnS[:, 128 * dc:128 * (dc + 1)].rearrange("(c p) f -> p c f", p=128)))
                    kR.issue(q, lambda: q.dma_start(
                        out=x1c[si % 2][:], in_=x1o[128 * dc:128 * (dc + 1),
                                                    512 * qt:512 * (qt + 1)]))

            @ctx.block.tensor
            def _(t, qt=qt):
                t.wait_ge(kRe.h, kRe.at(NHC1 * (qt + 1)))
                for dc in range(16):
                    si = 16 * qt + dc
                    kD.wait(t, si)
                    if si >= 2:
                        t.wait_ge(kF.h, kF.at(si - 1))
                    for hid in range(NHC1 - 1):
                        t.matmul(out=pF[si % 2][:], lhsT=wd[si % 2][:, hid, :],
                                 rhs=prod[:, hid, :], start=(hid == 0), stop=False)
                    kMd.tag(t.matmul(out=pF[si % 2][:], lhsT=wd[si % 2][:, NHC1 - 1, :],
                                     rhs=prod[:, NHC1 - 1, :], start=False, stop=True))

            @ctx.block.vector
            def _(v, qt=qt):
                v.wait_ge(kI.h, kI.at(3))

                def f1_step(si, dc):
                    v.wait_ge(kMd.h, kMd.at(si + 1))
                    if si >= 1:
                        v.wait_ge(kSi.h, kSi.at(si))
                    kF.tag(v.tensor_scalar(out=f1t[:], in0=pF[si % 2][:],
                                           scalar1=scd[:, dc:dc + 1], scalar2=oscol[:, 0:1],
                                           op0=ALU.mult, op1=ALU.mult))

                def resid_step(si, dc):
                    v.wait_ge(kSi.h, kSi.at(si + 1))
                    kR.wait(v, si)
                    if si >= 2:
                        kO.wait(v, si - 2)
                    kV.tag(v.tensor_tensor(out=x2t[si % 2][:], in0=s1b[:, dc, :],
                                           in1=g1t[:], op=ALU.mult))
                    v.wait_ge(kV.h, kV.n)
                    kV.tag(v.tensor_tensor(out=x2t[si % 2][:], in0=x2t[si % 2][:],
                                           in1=x1c[si % 2][:], op=ALU.add))

                for dc in range(16):
                    si = 16 * qt + dc
                    f1_step(si, dc)
                    if dc >= 1:
                        resid_step(si - 1, dc - 1)
                resid_step(16 * qt + 15, 15)

            @ctx.block.scalar
            def _(s, qt=qt):
                for dc in range(16):
                    si = 16 * qt + dc
                    s.wait_ge(kF.h, kF.at(si + 1))
                    if si >= 1:
                        s.wait_ge(kV.h, kV.at(2 * si))
                    kSi.tag(s.activation(out=g1t[:], in_=f1t[:], func=AF.Silu))

            @ctx.block.gpsimd
            def _(g, qt=qt):
                for dc in range(16):
                    si = 16 * qt + dc
                    g.wait_ge(kV.h, kV.at(2 * si + 2))
                    kO.issue(g, lambda: g.dma_start(
                        out=out[128 * dc:128 * (dc + 1), 512 * qt:512 * (qt + 1)],
                        in_=x2t[si % 2][:]))

        @ctx.block.gpsimd
        def _(g):
            for k in range(max(0, kO.issued - 2), kO.issued):
                kO.wait(g, k)



def build_l1():
    nc = bass.Bass("TRN2", target_bir_lowering=False,
                   detect_race_conditions=False)
    ei = lambda n, s, d: nc.dram_tensor(n, s, d, kind="ExternalInput").ap()
    x0 = ei("x0", [L, D], F32)
    x0T = ei("x0T", [D, L], F32)
    wqkvT = ei("wqkvT", [D, 6144], F32)
    bqkv = ei("bqkv", [128, 48], F32)
    cs_tab = ei("cs_tab", [64, L], BF16)
    maskq = ei("maskq", [4, 128, 512], BF16)
    woutT = ei("woutT", [D, D], F32)
    bout = ei("bout", [128, 16], F32)
    wgTq = ei("wgTq", [D, HID0], I32)
    wuTq = ei("wuTq", [D, HID0], I32)
    wdTq = ei("wdTq", [HID0, D], I32)
    sgbc = ei("sgbc", [128, HID0], F32)
    subc = ei("subc", [128, HID0], F32)
    sdbc = ei("sdbc", [128, D], F32)
    xn0_tm = nc.dram_tensor("xn0_tm", [L, D], BF16).ap()
    qf = nc.dram_tensor("qf", [16, 128, L], BF16).ap()
    kf = nc.dram_tensor("kf", [16, 128, L], BF16).ap()
    vf = nc.dram_tensor("vf", [16, 128, L], BF16).ap()
    of = nc.dram_tensor("of", [16, 128, L], BF16).ap()
    s0f = nc.dram_tensor("s0f", [D, L], BF16).ap()
    wgq = nc.dram_tensor("wgq", [D, HID0], BF16).ap()
    wuq = nc.dram_tensor("wuq", [D, HID0], BF16).ap()
    wdq = nc.dram_tensor("wdq", [HID0, D], BF16).ap()
    x1f = nc.dram_tensor("x1f", [D, L], F32, kind="ExternalOutput").ap()
    xn1f = nc.dram_tensor("xn1f", [D, L], BF16, kind="ExternalOutput").ap()

    es = contextlib.ExitStack()
    with es:
        ctx = Ctx(nc, es)
        ctx.block = es.enter_context(nc.Block())
        ctx.init_consts([1e-5, -12.0])
        es0 = contextlib.ExitStack()
        ph_e0(ctx, wgTq, wuTq, wdTq, sgbc, subc, sdbc, wgq, wuq, wdq, bufs_es=es0)
        ph_a(ctx, x0, xn0_tm)
        ctx.barrier()
        ph_b(ctx, xn0_tm, wqkvT, bqkv, cs_tab, qf, kf, vf)
        ctx.barrier()
        es0.close()
        ph_c(ctx, qf, kf, vf, maskq, of)
        ctx.barrier()
        ph_d(ctx, of, woutT, bout, s0f)
        ctx.barrier()
        ph_e(ctx, xn0_tm, wgq, wuq, wdq, x0T, s0f, x1f, xn1f)
        ctx.barrier()
    return nc


def build_l2():
    nc = bass.Bass("TRN2", target_bir_lowering=False,
                   detect_race_conditions=False)
    ei = lambda n, s, d: nc.dram_tensor(n, s, d, kind="ExternalInput").ap()
    xn1pre = ei("xn1pre", [D, 2049], BF16)
    xn1own = ei("xn1own", [D, 2049], BF16)
    x1o = ei("x1o", [D, 2048], F32)
    wf = {k: ei(k, s, F32) for k, s in (
        ("wx1", [D, 160]), ("ww1", [D, 64]), ("wr", [D, 1024]), ("wk", [D, 1024]),
        ("wv", [D, 2048]), ("wg", [D, 2048]))}
    cst = {}
    for k, s, dt in (("mucol", [128, 16], F32), ("bwsb", [128, 8], F32),
                     ("pmask", [128, 128], BF16), ("idn", [128, 128], BF16),
                     ("pflag", [128, 1], F32), ("bonus", [128, 32], F32),
                     ("gnwb", [128, 256], F32), ("gnbb", [128, 256], F32),
                     ("wx2s", [128, 2048], F32), ("wx2s2", [32, 2048], F32),
                     ("ww2s", [64, 1024], F32)):
        cst[k] = ei(k, s, dt)
    wo = ei("wo", [D, D], F32)
    upT = ei("upT", [D, HID1], F32)
    downT = ei("downT", [HID1, D], F32)
    upsc = ei("upsc", [128, 64], F32)
    dnsc = ei("dnsc", [128, 16], F32)
    osc = ei("osc", [1, 1], F32)
    gatedT = nc.dram_tensor("gatedT", [D, 2048], BF16).ap()
    upS = nc.dram_tensor("upS", [D, HID1], BF16).ap()
    dnS = nc.dram_tensor("dnS", [HID1, D], BF16).ap()
    out = nc.dram_tensor("out", [D, 2048], F32, kind="ExternalOutput").ap()

    es = contextlib.ExitStack()
    with es:
        ctx = Ctx(nc, es)
        ctx.block = es.enter_context(nc.Block())
        ctx.init_consts([1e-5, -12.0])
        ph_g0(ctx, upT, downT, upS, dnS)
        ctx.barrier()
        ph_f(ctx, xn1pre, xn1own, wf, cst, gatedT)
        ctx.barrier()
        ph_g(ctx, gatedT, wo, xn1own, upS, dnS, upsc, dnsc, osc, x1o, out)
        ctx.barrier()
    return nc


def build_fused(race=False):
    nc = bass.Bass("TRN2", target_bir_lowering=False,
                   detect_race_conditions=race)
    ei = lambda n, s, d: nc.dram_tensor(n, s, d, kind="ExternalInput").ap()
    x0 = ei("x0", [L, D], F32)
    x0T = ei("x0T", [D, L], F32)
    wqkvT = ei("wqkvT", [D, 6144], F32)
    bqkv = ei("bqkv", [128, 48], F32)
    cs_tab = ei("cs_tab", [64, L], BF16)
    maskq = ei("maskq", [4, 128, 512], BF16)
    woutT = ei("woutT", [D, D], F32)
    bout = ei("bout", [128, 16], F32)
    wgTq = ei("wgTq", [D, HID0], I32)
    wuTq = ei("wuTq", [D, HID0], I32)
    wdTq = ei("wdTq", [HID0, D], I32)
    sgbc = ei("sgbc", [128, HID0], F32)
    subc = ei("subc", [128, HID0], F32)
    sdbc = ei("sdbc", [128, D], F32)
    wf = {k: ei(k, s, F32) for k, s in (
        ("wx1", [D, 160]), ("ww1", [D, 64]), ("wr", [D, 1024]), ("wk", [D, 1024]),
        ("wv", [D, 2048]), ("wg", [D, 2048]))}
    cst = {}
    for k, s, dt in (("mucol", [128, 16], F32), ("bwsb", [128, 8], F32),
                     ("pmask", [128, 128], BF16), ("idn", [128, 128], BF16),
                     ("pflag", [128, 1], F32), ("bonus", [128, 32], F32),
                     ("gnwb", [128, 256], F32), ("gnbb", [128, 256], F32),
                     ("wx2s", [128, 2048], F32), ("wx2s2", [32, 2048], F32),
                     ("ww2s", [64, 1024], F32)):
        cst[k] = ei(k, s, dt)
    wo = ei("wo", [D, D], F32)
    upT = ei("upT", [D, HID1], F32)
    downT = ei("downT", [HID1, D], F32)
    upsc = ei("upsc", [128, 64], F32)
    dnsc = ei("dnsc", [128, 16], F32)
    osc = ei("osc", [1, 1], F32)
    xn0_tm = nc.dram_tensor("xn0_tm", [L, D], BF16).ap()
    qf = nc.dram_tensor("qf", [16, 128, L], BF16).ap()
    kf = nc.dram_tensor("kf", [16, 128, L], BF16).ap()
    vf = nc.dram_tensor("vf", [16, 128, L], BF16).ap()
    of = nc.dram_tensor("of", [16, 128, L], BF16).ap()
    s0f = nc.dram_tensor("s0f", [D, L], BF16).ap()
    wgq = nc.dram_tensor("wgq", [D, HID0], BF16).ap()
    wuq = nc.dram_tensor("wuq", [D, HID0], BF16).ap()
    wdq = nc.dram_tensor("wdq", [HID0, D], BF16).ap()
    x1f = nc.dram_tensor("x1f", [D, L], F32).ap()
    xn1p = nc.dram_tensor("xn1p", [D, L + 1], BF16).ap()
    xn1ownS = nc.dram_tensor("xn1ownS", [D, 2049], BF16).ap()
    x1oS = nc.dram_tensor("x1oS", [D, 2048], F32).ap()
    gatedT = nc.dram_tensor("gatedT", [D, 2048], BF16).ap()
    upS = nc.dram_tensor("upS", [D, HID1], BF16).ap()
    dnS = nc.dram_tensor("dnS", [HID1, D], BF16).ap()
    out = nc.dram_tensor("out", [D, 2048], F32, kind="ExternalOutput").ap()

    es = contextlib.ExitStack()
    with es:
        ctx = Ctx(nc, es)
        ctx.block = es.enter_context(nc.Block())
        ctx.init_consts([1e-5, -12.0])
        es0 = contextlib.ExitStack()
        ph_e0(ctx, wgTq, wuTq, wdTq, sgbc, subc, sdbc, wgq, wuq, wdq, bufs_es=es0)
        ph_a(ctx, x0, xn0_tm)
        ctx.barrier()
        ph_b(ctx, xn0_tm, wqkvT, bqkv, cs_tab, qf, kf, vf)
        ctx.barrier()
        ctx.new_phase()
        es0.close()
        ph_c(ctx, qf, kf, vf, maskq, of)
        ctx.barrier()
        ctx.new_phase()
        ph_d(ctx, of, woutT, bout, s0f)
        ctx.barrier()
        ctx.new_phase()
        ph_e(ctx, xn0_tm, wgq, wuq, wdq, x0T, s0f, x1f, xn1p[:, 1:L + 1])
        ctx.barrier()
        ctx.new_phase()
        es1 = contextlib.ExitStack()
        ph_g0(ctx, upT, downT, upS, dnS, bufs_es=es1)
        ph_s(ctx, xn1p, x1f, cst["pflag"], xn1ownS, x1oS)
        ctx.barrier()
        ctx.new_phase()
        es1.close()
        ph_f(ctx, xn1p[:, 0:2049], xn1ownS, wf, cst, gatedT)
        ctx.barrier()
        ctx.new_phase()
        ph_g(ctx, gatedT, wo, xn1ownS, upS, dnS, upsc, dnsc, osc, x1oS, out)
        ctx.barrier()
    return nc


def _bonus_bd(raw):
    r = raw.reshape(8, 4, 32)
    bd = np.zeros((4, 32, 8, 4), np.float32)
    for j in range(4):
        bd[j, :, :, j] = r[:, j, :].T
    return np.ascontiguousarray(bd.reshape(128, 32)).astype(np.float32)


def _as_f32(a):
    a = np.asarray(a)
    return a.view(np.float32) if (a.dtype != np.float32 and a.dtype.itemsize == 4
                                  and a.dtype != np.int32) else a.astype(np.float32)


def _bf_view(a):
    a = np.asarray(a)
    return a.view(BF) if a.dtype.itemsize == 2 and a.dtype != BF else a


def prep_maps(ins):
    cs = np.ascontiguousarray

    x = np.asarray(ins["L0_attn_x"], np.float32)
    theta = 1.0 / 10000.0 ** (np.arange(0, 32, 2, dtype=np.float64) / 32)
    t = np.arange(L, dtype=np.float64)
    ang = t[None, :] * np.concatenate([theta, theta])[:, None]
    Cc = np.cos(ang)
    Ss = np.sin(ang)
    Ss[0:16] *= -1.0
    cs_tab = np.concatenate([Cc, Ss], 0).astype(BF)
    mq = np.zeros((4, 128, 512), np.float32)
    for i in range(4):
        for r in range(128):
            mq[i, r, 128 * i + r:] = 1.0
    d1 = {
        "wqkvT": cs(np.asarray(ins["L0_attn_W_qkv"], np.float32).T),
        "bqkv": cs(np.asarray(ins["L0_attn_b_qkv"], np.float32).reshape(48, 128).T),
        "cs_tab": cs_tab, "maskq": mq.astype(BF),
        "woutT": cs(np.asarray(ins["L0_attn_W_out"], np.float32).T),
        "bout": cs(np.asarray(ins["L0_attn_b_out"], np.float32).reshape(16, 128).T),
        "wgTq": cs(np.asarray(ins["L0_ffn_wq_gate"], np.int32).T),
        "wuTq": cs(np.asarray(ins["L0_ffn_wq_up"], np.int32).T),
        "wdTq": cs(np.asarray(ins["L0_ffn_wq_down"], np.int32).T),
        "sgbc": cs(np.broadcast_to(np.asarray(ins["L0_ffn_s_gate"], np.float32),
                                   (128, HID0))),
        "subc": cs(np.broadcast_to(np.asarray(ins["L0_ffn_s_up"], np.float32),
                                   (128, HID0))),
        "sdbc": cs(np.broadcast_to(np.asarray(ins["L0_ffn_s_down"], np.float32),
                                   (128, D))),
    }
    d2 = {
        "wx1": np.asarray(ins["L1_attn_Wx1"], np.float32),
        "ww1": np.asarray(ins["L1_attn_Ww1"], np.float32),
        "wr": np.asarray(ins["L1_attn_Wr"], np.float32),
        "wk": np.asarray(ins["L1_attn_Wk"], np.float32),
        "wv": np.asarray(ins["L1_attn_Wv"], np.float32),
        "wg": np.asarray(ins["L1_attn_Wg"], np.float32),
        "mucol": cs(np.asarray(ins["L1_attn_mu_x"], np.float32).reshape(16, 128).T),
        "bwsb": cs(np.asarray(ins["L1_attn_bw"], np.float32).reshape(8, 128).T),
        "pmask": cs(np.tile(np.triu(np.ones((32, 32), np.float32), 1), (4, 4))).astype(BF),
        "idn": np.eye(128, dtype=np.float32).astype(BF),
        "bonus": _bonus_bd(np.asarray(ins["L1_attn_bonus"], np.float32)),
        "gnwb": cs(np.tile(np.asarray(ins["L1_attn_gn_w"], np.float32), (128, 4))),
        "gnbb": cs(np.tile(np.asarray(ins["L1_attn_gn_b"], np.float32), (128, 4))),
        "wx2s": cs(np.asarray(ins["L1_attn_Wx2"], np.float32).reshape(160, 2048)[0:128]),
        "wx2s2": cs(np.asarray(ins["L1_attn_Wx2"], np.float32).reshape(160, 2048)[128:160]),
        "ww2s": np.asarray(ins["L1_attn_Ww2"], np.float32),
        "wo": np.asarray(ins["L1_attn_Wo"], np.float32),
        "upT": cs(np.asarray(ins["L1_ffn_up_weight"], np.float32).T),
        "downT": cs(np.asarray(ins["L1_ffn_down_weight"], np.float32).T),
        "upsc": cs(np.asarray(ins["L1_ffn_up_scale"], np.float32).reshape(64, 128).T),
        "dnsc": cs(np.asarray(ins["L1_ffn_down_scale"], np.float32).reshape(16, 128).T),
        "osc": np.asarray(ins["L1_ffn_output_scale"], np.float32).reshape(1, 1),
    }
    in_maps = []
    for c in range(8):
        half = c % 2
        m = dict(d1)
        m.update(d2)
        m["x0"] = cs(x[c // 2])
        m["x0T"] = cs(x[c // 2].T)
        m["pflag"] = np.full((128, 1), float(half), np.float32)
        in_maps.append(m)
    return in_maps


def kernel(**ins):
    in_maps = prep_maps(ins)
    nc = build_fused()
    res = run_bass_kernel_spmd(nc, in_maps, core_ids=list(range(8)))

    out = np.empty((B, L, D), np.float32)
    for c in range(8):
        b, half = c // 2, c % 2
        o = _as_f32(res.results[c]["out"])
        out[b, 2048 * half:2048 * (half + 1), :] = o.T
    return out
